# version 22
# baseline (speedup 1.0000x reference)
import contextlib
import math
import numpy as np

import concourse.bass as bass
import concourse.mybir as mybir
from concourse.bass_utils import run_bass_kernel_spmd

F32 = mybir.dt.float32
BF16 = mybir.dt.bfloat16
I32 = mybir.dt.int32
AF = mybir.ActivationFunctionType
ALU = mybir.AluOpType

D = 1024
NB = 2
SEQ = 2048
T = NB * SEQ
CH = 512
NCH = T // CH
DC = D // 128
FF = 4096
FC = FF // 128
EPS = 1e-5
LOG127 = math.log(127.0)
DEV_PREP = False


class Sem:
    def __init__(self, handle):
        self.h = handle
        self.n = 0

    def inc(self, k=1):
        self.n += k
        return self.n


def _slab(s, sem_h, dst, src, nsp=4):
    stp = 128 // nsp
    last = None
    for i in range(nsp):
        last = s.dma_start(out=dst[i * stp:(i + 1) * stp, :],
                           in_=src[i * stp:(i + 1) * stp, :])
        if i < nsp - 1:
            last.then_inc(sem_h, 16)
    return last


class Prog:
    ENGINES = ("sync", "vector", "scalar", "tensor", "gpsimd")

    def __init__(self):
        self.steps = {e: [] for e in self.ENGINES}

    def on(self, eng, fn):
        self.steps[eng].append(fn)

    def run(self, block):
        prog = self

        @block.sync
        def _(e):
            for fn in prog.steps["sync"]:
                fn(e)

        @block.vector
        def _(e):
            for fn in prog.steps["vector"]:
                fn(e)

        @block.scalar
        def _(e):
            for fn in prog.steps["scalar"]:
                fn(e)

        @block.tensor
        def _(e):
            for fn in prog.steps["tensor"]:
                fn(e)

        @block.gpsimd
        def _(e):
            for fn in prog.steps["gpsimd"]:
                fn(e)


class R:
    pass


def build_nc():
    nc = bass.Bass("TRN2", target_bir_lowering=False, debug=False)
    r = R()
    r.nc = nc

    x_in = nc.dram_tensor("x", [NB, SEQ, D], F32, kind="ExternalInput").ap()
    out = nc.dram_tensor("out", [NB, SEQ, D], F32, kind="ExternalOutput").ap()
    W = {}
    for name, shape in [
        ("L0_attn_wq", [D, D]), ("L0_attn_bq", [D]), ("L0_attn_wu", [D, D]),
        ("L0_attn_bu", [D]), ("L0_ffn_w1", [FF, D]), ("L0_ffn_b1", [FF]),
        ("L0_ffn_w2", [D, FF]), ("L0_ffn_b2", [D]),
        ("L1_attn_ln1_g", [D]), ("L1_attn_ln1_b", [D]),
        ("L1_attn_Wk", [D, D]), ("L1_attn_Wv", [D, D]), ("L1_attn_Wr", [D, D]),
        ("L1_attn_Wo", [D, D]), ("L1_attn_time_decay", [D]),
        ("L1_attn_time_first", [D]), ("L1_attn_mix_k", [D]),
        ("L1_attn_mix_v", [D]), ("L1_attn_mix_r", [D]),
        ("L1_attn_ln2_g", [D]), ("L1_attn_ln2_b", [D]),
        ("L1_attn_Wfk", [D, FF]), ("L1_attn_Wfv", [FF, D]), ("L1_attn_Wfr", [D, D]),
        ("L1_ffn_bias", [D]), ("L1_ffn_act_scale", [1]),
    ]:
        W[name] = nc.dram_tensor(name, shape, F32, kind="ExternalInput").ap()
    W["L1_ffn_ternary_weight"] = nc.dram_tensor(
        "L1_ffn_ternary_weight", [D, D], I32, kind="ExternalInput").ap()
    r.W = W

    _ck = "ExternalOutput" if DEV_PREP else "Internal"
    r.w1T = nc.dram_tensor("w1T_c", [FC, 128, DC * 128], BF16, kind=_ck).ap()
    r.w2T = nc.dram_tensor("w2T_c", [DC, 128, FC * 128], BF16, kind=_ck).ap()
    r.wkB = nc.dram_tensor("wk_c", [DC, 128, DC * 128], BF16, kind=_ck).ap()
    r.wvB = nc.dram_tensor("wv_c", [DC, 128, DC * 128], BF16, kind=_ck).ap()
    r.wrB = nc.dram_tensor("wr_c", [DC, 128, DC * 128], BF16, kind=_ck).ap()
    r.woB = nc.dram_tensor("wo_c", [DC, 128, DC * 128], BF16, kind=_ck).ap()
    r.wfkB = nc.dram_tensor("wfk_c", [FC, 128, DC * 128], BF16, kind=_ck).ap()
    r.wfvB = nc.dram_tensor("wfv_c", [FF, D], BF16, kind=_ck).ap()
    r.wfrB = nc.dram_tensor("wfr_c", [DC, 128, DC * 128], BF16, kind=_ck).ap()
    r.wtT = nc.dram_tensor("wtT_c", [DC, 128, DC * 128], BF16, kind=_ck).ap()
    r.wqT = nc.dram_tensor("wqT_c", [DC, 128, DC * 128], BF16, kind=_ck).ap()
    r.wuT = nc.dram_tensor("wuT_c", [DC, 128, DC * 128], BF16, kind=_ck).ap()

    st = contextlib.ExitStack()
    nc._kernel_stack = st

    def sb(name, shape, dtype):
        return st.enter_context(nc.sbuf_tensor(name, shape, dtype))

    def mksem(name):
        return Sem(st.enter_context(nc.semaphore(name)))

    r.sb, r.sem, r.st = sb, mksem, st

    r.x_fm = sb("x_fm", [128, DC * T], BF16)
    r.idb = sb("idb", [128, 128], BF16)
    r.idf = sb("idf", [128, 128], F32)
    r.swapI = sb("swapI", [128, 128], BF16)
    r.ones_f_row1 = sb("ones_f_row1", [1, 128], F32)
    r.ones_f_col = sb("ones_f_col", [128, 1], F32)
    r.ones_bf_col = sb("ones_bf_col", [128, 1], BF16)
    r.epsc = sb("epsc", [128, 1], F32)
    r.l127 = sb("l127", [128, 1], F32)
    r.bqc = sb("bqc", [128, DC], F32)
    r.buc = sb("buc", [128, DC], F32)
    r.b1c = sb("b1c", [128, FC], F32)
    r.b2c = sb("b2c", [128, DC], F32)
    r.corr1c = sb("corr1c", [128, FC], F32)
    r.corr2c = sb("corr2c", [128, DC], F32)
    r.corrL1c = sb("corrL1c", [128, DC], F32)
    r.mw1 = sb("mw1", [1, 1], F32)
    r.mw2 = sb("mw2", [1, 1], F32)
    r.s1w_col = sb("s1w_col", [128, 1], F32)
    r.s2w_col = sb("s2w_col", [128, 1], F32)
    for nm in ("g1c", "b1lc", "mkc", "mvc", "mrc", "omkc", "omvc", "omrc",
               "tdc", "tfc", "ewc", "etfc", "g2lc", "b2lc", "bL1c"):
        setattr(r, nm, sb(nm, [128, DC], F32))
    r.s_col = sb("s_col", [128, 1], F32)
    r.sinv_col = sb("sinv_col", [128, 1], F32)
    r.deltac = sb("deltac", [128, 1], I32)
    r.offc = sb("offc", [128, 1], F32)
    r.offc64 = sb("offc64", [128, 1], F32)

    stackA = contextlib.ExitStack()
    r.sb_phase = lambda name, shape, dtype: stackA.enter_context(
        nc.sbuf_tensor(name, shape, dtype))
    r.st_phase = stackA
    with nc.Block() as blk:
        p = Prog()
        emit_consts(r, p)
        emit_load_x(r, p, x_in)
        emit_prep(r, p)
        p.run(blk)
    stackA.close()

    stackB = contextlib.ExitStack()
    psb = {"sb": lambda name, shape, dtype: stackB.enter_context(
        nc.sbuf_tensor(name, shape, dtype)), "st": stackB}
    with nc.Block() as blk:
        p = Prog()
        emit_l0(r, p, psb)
        p.run(blk)
    stackB.close()

    r.xn_buf = st.enter_context(nc.sbuf_tensor("xn_buf", [128, DC * T], BF16))
    stackC = contextlib.ExitStack()
    psc = {"sb": lambda name, shape, dtype: stackC.enter_context(
        nc.sbuf_tensor(name, shape, dtype)), "st": stackC}
    with nc.Block() as blk:
        p = Prog()
        emit_l1(r, p, psc, out)
        p.run(blk)
    stackC.close()

    nc.finalize()
    return nc



def emit_consts(r, p):
    nc = r.nc
    iop = r.sb_phase("iop_t", [128, 128], F32)
    iof = r.sb_phase("iof_t", [128, 128], F32)
    iot = r.sb_phase("iot_t", [128, 128], F32)
    s_id = r.sem("s_id")
    r.s_id = s_id

    def g_prep(g):
        g.iota(iop[:], [[0, 128]], channel_multiplier=1, allow_small_or_imprecise_dtypes=True)
        g.iota(iof[:], [[1, 128]], channel_multiplier=0, allow_small_or_imprecise_dtypes=True)
        g.memset(r.ones_f_row1[:], 1.0)
        g.memset(r.ones_f_col[:], 1.0)
        g.memset(r.ones_bf_col[:], 1.0)
        g.memset(r.epsc[:], EPS)
        g.memset(r.l127[:], LOG127)
        g.drain()
        g.sem_inc(s_id.h, 1)
    p.on("gpsimd", g_prep)
    s_id.inc()

    def v_ident(v, tgt=s_id.n):
        v.wait_ge(s_id.h, tgt)
        v.tensor_tensor(r.idb[:], iop[:], iof[:], ALU.is_equal)
        v.drain()
        v.tensor_copy(r.idf[:], r.idb[:])
        v.drain()
        v.tensor_scalar(iot[:], iop[:], 63.5, None, ALU.is_gt)
        v.drain()
        v.scalar_tensor_tensor(iop[:], iot[:], -128.0, iop[:], ALU.mult, ALU.add)
        v.drain()
        v.tensor_scalar(iop[:], iop[:], 64.0, None, ALU.add)
        v.drain()
        v.tensor_tensor(r.swapI[:], iop[:], iof[:], ALU.is_equal)
        v.drain()
        v.memset(r.offc[:], 0.75)
        v.drain()
        v.tensor_scalar(r.deltac[:], r.offc[:], 0.0, None, ALU.add)
        v.drain()
        v.tensor_scalar(r.offc[:], r.deltac[:], -0.5, 128.5, ALU.mult, ALU.add)
        v.tensor_scalar(r.offc64[:], r.deltac[:], -0.5, 64.5, ALU.mult, ALU.add)
        v.drain()
        v.sem_inc(s_id.h, 1)
    p.on("vector", v_ident)
    s_id.inc()
    r.consts_done = s_id.n



def emit_load_x(r, p, x_in):
    nc = r.nc
    xst = [r.sb_phase(f"xst{i}", [128, D], BF16) for i in range(2)]
    pt = r.st_phase.enter_context(nc.psum_tensor("pt_ld", [128, D], BF16))
    s_ld = [r.sem("s_ld0"), r.sem("s_ld1")]
    s_tr = r.sem("s_tr")
    s_ev = r.sem("s_ev")

    ntiles = T // 128
    for i in range(ntiles):
        buf = i % 2
        b = (i * 128) // SEQ
        trow = (i * 128) % SEQ

        def g_load(g, buf=buf, b=b, trow=trow, need=(8 * (i - 1)) if i >= 2 else 0):
            if need:
                g.wait_ge(s_tr.h, need)
            src = x_in[b, trow:trow + 128, :]
            for sp in range(4):
                g.dma_start(out=xst[buf][:, sp * 256:(sp + 1) * 256],
                            in_=src[:, sp * 256:(sp + 1) * 256]).then_inc(s_ld[buf].h, 16)
        p.on("gpsimd", g_load)
        ld_t = s_ld[buf].inc(64)

        def t_tr(t, buf=buf, ld_t=ld_t, i=i):
            t.wait_ge(s_ld[buf].h, ld_t)
            if i == 0:
                t.wait_ge(r.s_id.h, r.consts_done)
            if i > 0:
                t.wait_ge(s_ev.h, i)
            for j in range(DC):
                t.matmul(pt[:, j * 128:(j + 1) * 128], xst[buf][:, j * 128:(j + 1) * 128],
                         r.idb[:], is_transpose=True,
                         start=True, stop=True).then_inc(s_tr.h, 1)
        p.on("tensor", t_tr)
        s_tr.inc(8)
        tr_t = s_tr.n

        def v_ev(v, i=i, tr_t=tr_t):
            v.wait_ge(s_tr.h, tr_t)
            dst = bass.AP(r.x_fm, i * 128, [[DC * T, 128], [T, DC], [1, 128]])
            v.tensor_copy(dst, pt[:].rearrange("p (j c) -> p j c", j=DC)).then_inc(s_ev.h, 1)
        p.on("vector", v_ev)
        s_ev.inc()
    r.s_xload = s_ev
    r.xload_done = s_ev.n



def emit_prep(r, p):
    nc = r.nc
    st = r.st
    W = r.W
    wst = [r.sb_phase(f"wst{i}", [128, FF], F32) for i in range(2)] \
        + [r.sb_phase(f"wstS{i}", [128, D], F32) for i in range(2)]
    wsb = [r.sb_phase(f"wsb{i}", [128, FF], BF16) for i in range(2)] \
        + [r.sb_phase(f"wsbS{i}", [128, D], BF16) for i in range(2)]
    wib = r.sb_phase("wib", [128, FF], I32)
    wqs = r.sb_phase("wqs", [128, FF], F32)
    tpst = [[r.sb_phase(f"tpst_{b}_{j}", [128, 128], BF16) for j in range(8)]
            for b in range(4)]
    rowst = r.sb_phase("rowst", [FC, 128], F32)
    colsc = r.sb_phase("colsc", [128, 2], F32)
    rows = r.sb_phase("prep_rows", [1, 64], F32)
    pcol = r.st_phase.enter_context(nc.psum_tensor("pp_col", [128, 64], F32))
    ptp = [r.st_phase.enter_context(nc.psum_tensor(f"pp_tp{b}", [128, DC * 128], BF16))
           for b in range(4)]
    s_wl = [r.sem(f"s_wl{i}") for i in range(4)]
    s_wc = [r.sem(f"s_wc{i}") for i in range(4)]
    s_dv = r.sem("s_dv")
    s_tp = [r.sem(f"s_tp{i}") for i in range(4)]
    s_wo = [r.sem(f"s_wo{i}") for i in range(4)]
    s_go = [r.sem(f"s_go{i}") for i in range(4)]
    s_pm = r.sem("s_pm")
    s_td = r.sem("s_td")
    r.s_prep_done = s_td

    tile_no = [0]

    def tile_gate(eng, extra=0):
        k = tile_no[0]
        if k >= 2:
            eng.wait_ge(s_td.h, k - 1 - extra)

    NSP = 4

    def dma_split(eng, sem, dst, src, nsp=NSP):
        ncols = dst.shape[-1]
        step = ncols // nsp
        for i in range(nsp):
            eng.dma_start(out=dst[..., i * step:(i + 1) * step],
                          in_=src[..., i * step:(i + 1) * step]).then_inc(sem, 16)

    def emit_tp_grp(buf, srcs, wait_sem, wait_t, wo_prev=None):
        prev_tp = s_tp[buf].n
        srcs = tuple(srcs)

        def t_tp(t, buf=buf, srcs=srcs, wait_sem=wait_sem, wait_t=wait_t,
                 prev_tp=prev_tp):
            t.wait_ge(wait_sem.h, wait_t)
            if prev_tp:
                t.wait_ge(s_tp[buf].h, prev_tp)
            for j, src in enumerate(srcs):
                t.matmul(ptp[buf][:, j * 128:(j + 1) * 128], src, r.idb[:],
                         is_transpose=True, start=True, stop=True).then_inc(s_pm.h, 1)
        p.on("tensor", t_tp)
        pm_t = s_pm.inc(len(srcs))

        def v_tp(sc, buf=buf, n=len(srcs), pm_t=pm_t, wo_prev=wo_prev):
            sc.wait_ge(s_pm.h, pm_t)
            if wo_prev:
                sc.wait_ge(wo_prev[0].h, wo_prev[1])
            for j in range(n):
                sc.activation(out=tpst[buf][j][:],
                              in_=ptp[buf][:, j * 128:(j + 1) * 128], func=AF.Copy)
            sc.drain()
            sc.sem_inc(s_tp[buf].h, 16 * n)
        p.on("scalar", v_tp)
        return s_tp[buf].inc(16 * len(srcs))

    vec_jobs = [
        ("L0_attn_bq", r.bqc, DC), ("L0_attn_bu", r.buc, DC),
        ("L0_ffn_b1", r.b1c, FC), ("L0_ffn_b2", r.b2c, DC),
        ("L1_attn_ln1_g", r.g1c, DC), ("L1_attn_ln1_b", r.b1lc, DC),
        ("L1_attn_mix_k", r.mkc, DC), ("L1_attn_mix_v", r.mvc, DC),
        ("L1_attn_mix_r", r.mrc, DC), ("L1_attn_time_decay", r.tdc, DC),
        ("L1_attn_time_first", r.tfc, DC), ("L1_attn_ln2_g", r.g2lc, DC),
        ("L1_attn_ln2_b", r.b2lc, DC), ("L1_ffn_bias", r.bL1c, DC),
    ]
    for vi, (nm, dst, n) in enumerate(vec_jobs):
        def s_ldv(s, nm=nm, n=n, vi=vi, prev=s_dv.n):
            if vi > 0:
                s.wait_ge(s_dv.h, prev)
            s.dma_start(out=rowst[0:n, :],
                        in_=W[nm].rearrange("(a b) -> a b", b=128)).then_inc(s_wl[0].h, 16)
        p.on("sync", s_ldv)
        wl_t = s_wl[0].inc(16)

        def t_tp(t, wl_t=wl_t, n=n, vi=vi):
            t.wait_ge(s_wl[0].h, wl_t)
            if vi == 0:
                t.wait_ge(r.s_id.h, r.consts_done)
            t.matmul(pcol[:, 0:n], rowst[0:n, :], r.idf[0:n, 0:n],
                     is_transpose=True, start=True, stop=True).then_inc(s_pm.h, 1)
        p.on("tensor", t_tp)
        pm_t = s_pm.inc()

        def v_ev(v, pm_t=pm_t, dst=dst, n=n):
            v.wait_ge(s_pm.h, pm_t)
            v.tensor_copy(dst[:, 0:n], pcol[:, 0:n])
            v.drain()
            v.sem_inc(s_dv.h, 1)
        p.on("vector", v_ev)
        s_dv.inc()
    vdone = s_dv.n

    def sc_derived(sc, vdone=vdone):
        sc.wait_ge(s_dv.h, vdone)
        sc.activation(out=r.etfc[:], in_=r.tfc[:], func=AF.Exp)
        sc.activation(out=r.ewc[:], in_=r.tdc[:], func=AF.Exp)
        sc.drain()
        sc.activation(out=r.ewc[:], in_=r.ewc[:], func=AF.Exp, scale=-1.0)
        sc.drain()
        sc.sem_inc(s_dv.h, 1)
    p.on("scalar", sc_derived)
    s_dv.inc()

    def v_derived(v, vdone=vdone):
        v.wait_ge(s_dv.h, vdone)
        for m, om in ((r.mkc, r.omkc), (r.mvc, r.omvc), (r.mrc, r.omrc)):
            v.tensor_scalar(om[:], m[:], -1.0, 1.0, ALU.mult, ALU.add)
        v.drain()
        v.sem_inc(s_dv.h, 1)
    p.on("vector", v_derived)
    s_dv.inc()

    def s_lds(s, prev=s_dv.n):
        s.wait_ge(s_dv.h, prev)
        s.dma_start(out=rows[0:1, 0:1], in_=W["L1_ffn_act_scale"][None, 0:1]).then_inc(s_wl[1].h, 16)
    p.on("sync", s_lds)
    wl_t = s_wl[1].inc(16)

    def v_scol(v, wl_t=wl_t):
        v.wait_ge(s_wl[1].h, wl_t)
        v.tensor_scalar(rows[0:1, 1:2], rows[0:1, 0:1], 1e-5, None, ALU.max)
        v.drain()
        v.reciprocal(rows[0:1, 2:3], rows[0:1, 1:2])
        v.drain()
        v.sem_inc(s_dv.h, 1)
    p.on("vector", v_scol)
    sd_t = s_dv.inc()

    def t_scol(t, sd_t=sd_t):
        t.wait_ge(s_dv.h, sd_t)
        t.matmul(pcol[:, 8:9], r.ones_f_row1[:], rows[0:1, 1:2], start=True, stop=True)
        t.matmul(pcol[:, 9:10], r.ones_f_row1[:], rows[0:1, 2:3],
                 start=True, stop=True).then_inc(s_pm.h, 2)
    p.on("tensor", t_scol)
    pm_t = s_pm.inc(2)

    def v_scol2(v, pm_t=pm_t):
        v.wait_ge(s_pm.h, pm_t)
        v.tensor_copy(r.s_col[:], pcol[:, 8:9])
        v.tensor_copy(r.sinv_col[:], pcol[:, 9:10])
        v.drain()
        v.sem_inc(s_dv.h, 1)
    p.on("vector", v_scol2)
    s_dv.inc()

    for wi, (nm, dstT) in enumerate([("L0_attn_wq", r.wqT), ("L0_attn_wu", r.wuT)]):
        for oc in range(DC):
            buf = (wi * DC + oc) % 4
            k = tile_no[0]

            def g_cast(g, nm=nm, oc=oc, buf=buf, k=k):
                if k >= 4:
                    g.wait_ge(s_td.h, k - 3)
                dma_split(g, s_wc[buf].h, wsb[buf][:, 0:D],
                          W[nm][oc * 128:(oc + 1) * 128, :])
            p.on("gpsimd", g_cast)
            wc_t = s_wc[buf].inc(16 * NSP)

            tp_t = emit_tp_grp(buf, [wsb[buf][:, dc * 128:(dc + 1) * 128]
                                     for dc in range(DC)], s_wc[buf], wc_t)

            def s_wr(s, tp_t=tp_t, buf=buf, dstT=dstT, oc=oc):
                s.wait_ge(s_tp[buf].h, tp_t)
                for dc in range(DC):
                    s.dma_start(out=dstT[oc, :, dc * 128:(dc + 1) * 128],
                                in_=tpst[buf][dc][:]).then_inc(s_wo[buf].h, 16)
            p.on("sync", s_wr)
            wo_t = s_wo[buf].inc(16 * DC)

            def s_done(s, buf=buf, wo_t=wo_t):
                s.wait_ge(s_wo[buf].h, wo_t)
                s.sem_inc(s_td.h, 1)
            p.on("sync", s_done)
            s_td.inc()
            tile_no[0] += 1

    for wi, (nm, n_t, hbmT, corr, mwrow, swcol, rb) in enumerate([
            ("L0_ffn_w1", FC, r.w1T, r.corr1c, r.mw1, r.s1w_col, 16),
            ("L0_ffn_w2", DC, r.w2T, r.corr2c, r.mw2, r.s2w_col, 24)]):
        wsrc = W[nm]
        ncols = wsrc.shape[1]
        nel = float(n_t * 128 * ncols)
        nbuf = 4 if ncols == D else 2
        abs_toks = []
        p1_prev = s_dv.n
        for ti in range(n_t):
            buf = ti % nbuf

            def s_ld(s, ti=ti, buf=buf, wsrc=wsrc, ncols=ncols, p1_prev=p1_prev,
                     war=(abs_toks[ti - nbuf] if ti >= nbuf else None)):
                s.wait_ge(s_dv.h, p1_prev if war is None else war)
                dma_split(s, s_wl[buf].h, wst[buf][:, 0:ncols],
                          wsrc[ti * 128:(ti + 1) * 128, :])
            p.on("sync", s_ld)
            wl_t = s_wl[buf].inc(16 * NSP)

            def v_abs(v, wl_t=wl_t, ti=ti, buf=buf, ncols=ncols):
                v.wait_ge(s_wl[buf].h, wl_t)
                if ti == 0:
                    v.memset(colsc[:, 0:1], 0.0)
                    v.drain()
                absb = wib[:].bitcast(F32)
                v.tensor_scalar(absb[:, 0:ncols].bitcast(I32),
                                wst[buf][:, 0:ncols].bitcast(I32),
                                0x7FFFFFFF, None, ALU.bitwise_and)
                v.drain()
                v.reduce_sum(colsc[:, 1:2], absb[:, 0:ncols], axis=mybir.AxisListType.X)
                v.drain()
                v.tensor_tensor(colsc[:, 0:1], colsc[:, 0:1], colsc[:, 1:2], ALU.add)
                v.drain()
                v.sem_inc(s_dv.h, 1)
            p.on("vector", v_abs)
            abs_toks.append(s_dv.inc())
        absdone = s_dv.n

        def t_tot(t, absdone=absdone, rb=rb):
            t.wait_ge(s_dv.h, absdone)
            t.matmul(pcol[0:1, rb:rb + 1], r.ones_f_col[:], colsc[:, 0:1],
                     start=True, stop=True).then_inc(s_pm.h, 1)
        p.on("tensor", t_tot)
        pm_t = s_pm.inc()

        def v_scale(v, pm_t=pm_t, nel=nel, mwrow=mwrow, rb=rb):
            v.wait_ge(s_pm.h, pm_t)
            v.tensor_scalar(rows[0:1, rb + 1:rb + 2], pcol[0:1, rb:rb + 1],
                            1.0 / nel, 1e-5, ALU.mult, ALU.max)
            v.drain()
            v.tensor_copy(mwrow[0:1, 0:1], rows[0:1, rb + 1:rb + 2])
            v.reciprocal(rows[0:1, rb + 2:rb + 3], rows[0:1, rb + 1:rb + 2])
            v.drain()
            v.sem_inc(s_dv.h, 1)
        p.on("vector", v_scale)
        sc_t = s_dv.inc()

        def t_bc(t, sc_t=sc_t, rb=rb):
            t.wait_ge(s_dv.h, sc_t)
            t.matmul(pcol[:, rb + 3:rb + 4], r.ones_f_row1[:],
                     rows[0:1, rb + 2:rb + 3], start=True, stop=True).then_inc(s_pm.h, 1)
        p.on("tensor", t_bc)
        pm_t = s_pm.inc()

        def v_swcol(v, pm_t=pm_t, swcol=swcol, rb=rb):
            v.wait_ge(s_pm.h, pm_t)
            v.tensor_copy(swcol[:], pcol[:, rb + 3:rb + 4])
            v.drain()
            v.sem_inc(s_dv.h, 1)
        p.on("vector", v_swcol)
        s_dv.inc()

        pass1_done = s_dv.n
        for ti in range(n_t):
            buf = ti % nbuf
            k = tile_no[0]

            def s_ld(s, ti=ti, buf=buf, wsrc=wsrc, ncols=ncols, k=k, p1=pass1_done,
                     nbuf=nbuf):
                if ti < nbuf:
                    s.wait_ge(s_dv.h, p1)
                gn = k - (3 if nbuf == 4 else 1)
                if k >= 2 and gn >= 1:
                    s.wait_ge(s_td.h, gn)
                dma_split(s, s_wl[buf].h, wst[buf][:, 0:ncols],
                          wsrc[ti * 128:(ti + 1) * 128, :])
            p.on("sync", s_ld)
            wl_t = s_wl[buf].inc(16 * NSP)

            def v_q(v, wl_t=wl_t, buf=buf, ncols=ncols, swcol=swcol, ti=ti, corr=corr):
                v.wait_ge(s_wl[buf].h, wl_t)
                v.tensor_scalar(wqs[:, 0:ncols], wst[buf][:, 0:ncols], swcol[:, 0:1], None,
                                ALU.mult)
                v.drain()
                v.tensor_scalar(wib[:, 0:ncols], wqs[:, 0:ncols], r.offc64[:, 0:1], None,
                                ALU.add)
                v.drain()
                v.tensor_scalar(wib[:, 0:ncols], wib[:, 0:ncols], 63.0, 65.0, ALU.max, ALU.min)
                v.drain()
                v.tensor_scalar(wsb[buf][:, 0:ncols], wib[:, 0:ncols], -64.0, None, ALU.add)
                v.drain()
                v.reduce_sum(colsc[:, 1:2], wsb[buf][:, 0:ncols], axis=mybir.AxisListType.X)
                v.drain()
                v.tensor_scalar(corr[:, ti:ti + 1], colsc[:, 1:2], -128.0, None, ALU.mult)
                v.drain()
                v.sem_inc(s_dv.h, 1)
            p.on("vector", v_q)
            q_t = s_dv.inc()

            ngrp = ncols // 128 // 8
            for gi in range(ngrp):
                tp_t = emit_tp_grp(
                    buf, [wsb[buf][:, (gi * 8 + j) * 128:(gi * 8 + j + 1) * 128]
                          for j in range(8)], s_dv, q_t,
                    wo_prev=((s_go[buf], s_go[buf].n) if gi > 0 else None))

                def s_wr(g, tp_t=tp_t, buf=buf, gi=gi, hbmT=hbmT, ti=ti):
                    g.wait_ge(s_tp[buf].h, tp_t)
                    for j in range(8):
                        jj = gi * 8 + j
                        g.dma_start(out=hbmT[ti, :, jj * 128:(jj + 1) * 128],
                                    in_=tpst[buf][j][:]).then_inc(s_go[buf].h, 16)
                p.on("gpsimd", s_wr)
                s_go[buf].inc(16 * 8)
            wo_t = s_go[buf].n

            def s_done(g, buf=buf, wo_t=wo_t):
                g.wait_ge(s_go[buf].h, wo_t)
                g.sem_inc(s_td.h, 1)
            p.on("gpsimd", s_done)
            s_td.inc()
            tile_no[0] += 1

    cast_jobs = [
        ("L1_attn_Wfk", r.wfkB, 0.5),
    ]
    for nm, hbm, scale in cast_jobs:
        ntl = W[nm].shape[0] // 128
        ncols = W[nm].shape[1]
        for ti in range(ntl):
            buf = ti % 2
            k = tile_no[0]

            def g_cast(g, nm=nm, ti=ti, buf=buf, ncols=ncols, k=k):
                if k >= 2:
                    g.wait_ge(s_td.h, k - 1)
                dma_split(g, s_wc[buf].h, wsb[buf][:, 0:ncols],
                          W[nm][ti * 128:(ti + 1) * 128, :])
            p.on("gpsimd", g_cast)
            wc_t = s_wc[buf].inc(16 * NSP)
            dv_t = None
            if scale != 1.0:
                def v_s(v, wc_t=wc_t, buf=buf, ncols=ncols, scale=scale):
                    v.wait_ge(s_wc[buf].h, wc_t)
                    v.tensor_scalar(wsb[buf][:, 0:ncols], wsb[buf][:, 0:ncols], scale, None,
                                    ALU.mult)
                    v.drain()
                    v.sem_inc(s_dv.h, 1)
                p.on("vector", v_s)
                dv_t = s_dv.inc()

            def s_wr(s, wc_t=wc_t, buf=buf, ncols=ncols, hbm=hbm, ti=ti, dv_t=dv_t):
                if dv_t is None:
                    s.wait_ge(s_wc[buf].h, wc_t)
                else:
                    s.wait_ge(s_dv.h, dv_t)
                if hbm is r.wfvB:
                    dma_split(s, s_wo[buf].h, hbm[ti * 128:(ti + 1) * 128, :],
                              wsb[buf][:, 0:ncols])
                else:
                    nbl = ncols // 128
                    srcv = wsb[buf][:, 0:ncols].rearrange("q (blk f) -> q blk f", f=128)
                    stp = nbl // NSP
                    for i in range(NSP):
                        dstv = hbm[i * stp:(i + 1) * stp, :,
                                   ti * 128:(ti + 1) * 128].rearrange(
                                       "blk q f -> q blk f")
                        s.dma_start(out=dstv, in_=srcv[:, i * stp:(i + 1) * stp, :]
                                    ).then_inc(s_wo[buf].h, 16)
            p.on("sync", s_wr)
            wo_t = s_wo[buf].inc(16 * NSP)

            def s_done(s, buf=buf, wo_t=wo_t):
                s.wait_ge(s_wo[buf].h, wo_t)
                s.sem_inc(s_td.h, 1)
            p.on("sync", s_done)
            s_td.inc()
            tile_no[0] += 1

    for ti in range(DC):
        buf = ti % 2
        k = tile_no[0]

        def g_cast(g, ti=ti, buf=buf, k=k):
            if k >= 2:
                g.wait_ge(s_td.h, k - 1)
            dma_split(g, s_wc[buf].h, wsb[buf][:, 0:D],
                      r.W["L1_ffn_ternary_weight"][ti * 128:(ti + 1) * 128, :])
        p.on("gpsimd", g_cast)
        wc_t = s_wc[buf].inc(16 * NSP)

        def v_m1(v, wc_t=wc_t, buf=buf, ti=ti):
            v.wait_ge(s_wc[buf].h, wc_t)
            v.tensor_scalar(wsb[buf][:, 0:D], wsb[buf][:, 0:D], -1.0, None, ALU.add)
            v.drain()
            v.reduce_sum(colsc[:, 1:2], wsb[buf][:, 0:D], axis=mybir.AxisListType.X)
            v.drain()
            v.tensor_scalar(r.corrL1c[:, ti:ti + 1], colsc[:, 1:2], -128.0, None, ALU.mult)
            v.drain()
            v.sem_inc(s_dv.h, 1)
        p.on("vector", v_m1)
        dv_t = s_dv.inc()

        tp_t = emit_tp_grp(buf, [wsb[buf][:, j * 128:(j + 1) * 128]
                                 for j in range(DC)], s_dv, dv_t)

        def s_wr(s, tp_t=tp_t, buf=buf, ti=ti):
            s.wait_ge(s_tp[buf].h, tp_t)
            for j in range(DC):
                s.dma_start(out=r.wtT[ti, :, j * 128:(j + 1) * 128],
                            in_=tpst[buf][j][:]).then_inc(s_wo[buf].h, 16)
        p.on("sync", s_wr)
        wo_t = s_wo[buf].inc(16 * DC)

        def s_done(s, buf=buf, wo_t=wo_t):
            s.wait_ge(s_wo[buf].h, wo_t)
            s.sem_inc(s_td.h, 1)
        p.on("sync", s_done)
        s_td.inc()
        tile_no[0] += 1

    s_lc = r.sem("s_lc")

    def g_hbm_casts(g):
        for nm, hbm in [("L1_attn_Wk", r.wkB), ("L1_attn_Wv", r.wvB),
                        ("L1_attn_Wr", r.wrB), ("L1_attn_Wo", r.woB),
                        ("L1_attn_Wfr", r.wfrB)]:
            src = W[nm].rearrange("(dc q) (blk f) -> dc q blk f", q=128, f=128)
            dst = hbm.rearrange("blk q (dc f) -> dc q blk f", f=128)
            for i in range(DC):
                g.dma_start(out=dst[i], in_=src[i]).then_inc(s_lc.h, 16)
        for i in range(8):
            g.dma_start(out=r.wfvB[i * 512:(i + 1) * 512, :],
                        in_=W["L1_attn_Wfv"][i * 512:(i + 1) * 512, :]
                        ).then_inc(s_lc.h, 16)
    p.on("gpsimd", g_hbm_casts)
    r.l1_casts = (s_lc, s_lc.inc(16 * 48))

    r.prep_tiles_done = s_td.n
    r.s_dv_done = s_dv.n
    r.s_dv = s_dv



def _step(p, eng, waits, fn, inc=None, inck=1):
    def go(e, waits=tuple(waits), fn=fn, inc=inc, inck=inck):
        for s, n in waits:
            if n > 0:
                e.wait_ge(s.h, n)
        ret = fn(e)
        if inc is not None:
            mode, s = inc
            if mode == "then":
                ret.then_inc(s.h, inck)
            else:
                e.drain()
                e.sem_inc(s.h, inck)
    p.on(eng, go)


def emit_l0(r, p, psb):
    nc = r.nc
    sb = psb["sb"]
    st = psb["st"]
    xf = r.x_fm

    r.e16 = None
    r.ones_bf_row = sb("l0_ones_bf_row", [1, CH], BF16)
    r.e2 = sb("l0_e2", [128, 2], BF16)
    r.elo = sb("l0_elo", [1, 128], F32)
    r.ehi = sb("l0_ehi", [1, 128], F32)
    scr = sb("l0_scr", [128, DC * CH], F32)
    intq2 = [sb(f"l0_intq{i}", [128, CH], I32) for i in range(2)]
    sqs = [sb(f"l0_sqs{i}", [128, CH], F32) for i in range(2)]
    sq2 = [sb(f"l0_sq2{i}", [128, CH], F32) for i in range(2)]
    sqp = [sb(f"l0_sqp{i}", [128, CH], BF16) for i in range(2)]
    g2keep = sb("l0_g2k", [1, CH], F32)
    xnq = sb("l0_xnq", [128, DC * CH], BF16)
    qaf = sb("l0_qaf", [128, DC * CH], BF16)
    _qpk0 = sb("l0_qpk", [128, 2048], BF16)
    qpk = [_qpk0, _qpk0]
    vt = [sb(f"l0_vt{i}", [128, 1024], BF16) for i in range(2)]
    e_sb = [sb(f"l0_e{i}", [128, 2048], BF16) for i in range(2)]
    hbf = sb("l0_h", [128, FC * CH], BF16)
    w1s = [sb(f"l0_w1s{i}", [128, DC * 128], BF16) for i in range(2)]
    w2s = sb("l0_w2s", [128, FC * 128], BF16)
    wqs = [sb(f"l0_wqs{i}", [128, DC * 128], BF16) for i in range(2)]
    sA_sb = sb("l0_sA", [128, CH], F32)
    sB_sb = sb("l0_sB", [128, CH], F32)
    div_sb = sb("l0_div", [128, CH], F32)
    maxacc = sb("l0_maxacc", [128, CH], F32)
    mxcol = sb("l0_mxcol", [128, 4], F32)
    trst = sb("l0_trst", [16, 128], F32)
    m2acc = sb("l0_m2acc", [2, 8], F32)
    m41 = sb("l0_m41", [4, 1], F32)
    mrow2 = sb("l0_mrow2", [1, 4], F32)
    mcols = sb("l0_mcols", [128, 4], F32)
    zc4 = [sb(f"l0_zc4{i}", [128, 32], F32) for i in range(2)]
    z16 = sb("l0_z16", [128, 16], F32)
    zrow = [scr[0:1, 0:2048], scr[0:1, 2048:4096]]
    rows = sb("l0_rows", [1, 9 * CH], F32)
    nl127 = sb("l0_nl127", [128, 1], F32)

    def rw(i):
        return rows[0:1, i * CH:(i + 1) * CH]
    MEAN, M2, VAR, LNV, RSTD, R2I, G2, HMAX, C1 = 0, 1, 2, 3, 4, 5, 6, 7, 8
    U = 1
    MX = 2
    LNH = 5
    S1 = 3
    SAr = 1
    SSQ, LNV2, RSTD2, MXH2, LNM, LNHN, S2r, C2r, SBr = range(9)

    psA = st.enter_context(nc.psum_tensor("l0_psA", [128, 4 * CH], F32))
    psB = st.enter_context(nc.psum_tensor("l0_psB", [128, 4 * CH], F32))
    pb = [psA[:, 0:CH], psA[:, CH:2 * CH], psB[:, 0:CH], psB[:, CH:2 * CH],
          psA[:, 2 * CH:3 * CH], psA[:, 3 * CH:4 * CH],
          psB[:, 2 * CH:3 * CH], psB[:, 3 * CH:4 * CH]]
    pbAV = [pb[2], pb[3], pb[6], pb[7]]
    scoresF = psA[:, 0:4 * CH]
    bfree = {i: None for i in range(8)}

    def bw(i):
        return [bfree[i]] if bfree[i] else []

    sMM = r.sem("l0_mm")
    sDV = r.sem("l0_dv")
    sAC = r.sem("l0_ac")
    sWS = [r.sem("l0_ws0"), r.sem("l0_ws1")]
    sW2 = r.sem("l0_w2")
    sW2h = [r.sem("l0_w2a"), r.sem("l0_w2b")]
    sGA = r.sem("l0_ga")
    sGB = r.sem("l0_gb")
    sRD = r.sem("l0_rd")
    free = {}
    for nm in ("w1s0", "w1s1", "w2s", "wqs0", "wqs1", "qpk", "vt0", "vt1",
               "e0", "e1", "sqs0", "sqs1", "sq20", "sq21", "scr", "xnq", "qaf",
               "zc40", "zc41", "hbf", "maxacc", "trst", "sqp0", "sqp1",
               "w2h0", "w2h1"):
        free[nm] = None

    def fw(nm):
        f = free[nm]
        if f is None:
            return []
        return list(f) if isinstance(f, list) else [f]

    sBR = r.sem("l0_br")
    sBC = r.sem("l0_bc")

    def g_brows(g):
        g.memset(r.ones_bf_row[:], 1.0)
        g.memset(r.e2[:], 0.0)
        g.memset(r.elo[:], 0.0)
        g.memset(r.ehi[:], 0.0)
        g.drain()
        g.memset(r.e2[0:64, 0:1], 1.0)
        g.memset(r.e2[64:128, 1:2], 1.0)
        g.memset(r.elo[0:1, 0:64], 1.0)
        g.memset(r.ehi[0:1, 64:128], 1.0)
        g.drain()
        g.sem_inc(sBC.h, 1)
    p.on("gpsimd", g_brows)
    sBC.inc(1)
    r.bconsts_done = (sBC, sBC.n)

    def xap(c, dc=None):
        if dc is None:
            return bass.AP(xf, c * CH, [[DC * T, 128], [T, DC], [1, CH]])
        return bass.AP(xf, dc * T + c * CH, [[DC * T, 128], [1, CH]])

    def pbcast(bank, ndc=DC):
        return bank.rearrange("p (o t) -> p o t", o=1).broadcast_to((128, ndc, CH))

    ONE_CONST = [False]

    P_MEAN, P_VAR, P_LNV, P_RSTD = 0, 1, 2, 3

    def rw2(i):
        return scr[0:1, i * CH:(i + 1) * CH]

    def emit_stats(cc, g2_war):
        for dc in range(DC):
            buf = dc % 2
            _step(p, "vector", fw(f"sqp{buf}") + (fw("scr") if dc == 0 else []),
                  lambda v, cc=cc, dc=dc, buf=buf: v.tensor_tensor(
                      sqp[buf][:], xap(cc, dc), xap(cc, dc), ALU.mult),
                  ("drain", sDV))
            dv_t = sDV.inc()
            _step(p, "tensor", [(sDV, dv_t)] + (bw(1) if dc == 0 else []),
                  lambda t, buf=buf, dc=dc: t.matmul(
                      pb[1][0:1, :], r.ones_bf_col[:], sqp[buf][:],
                      start=(dc == 0), stop=(dc == DC - 1)),
                  ("then", sMM))
            free[f"sqp{buf}"] = (sMM, sMM.inc())
        s2_done = sMM.n
        for dc in range(DC):
            wl = (bw(0) + [(r.s_xload, r.xload_done)]) if dc == 0 else []
            _step(p, "tensor", wl,
                  lambda t, cc=cc, dc=dc: t.matmul(
                      pb[0][0:1, :], r.ones_bf_col[:], xap(cc, dc),
                      start=(dc == 0), stop=(dc == DC - 1)),
                  ("then", sMM))
            s1_done = sMM.inc()
        _step(p, "vector", [(sMM, s1_done)] + fw("scr"),
              lambda v: v.tensor_scalar(rw2(P_MEAN), pb[0][0:1, :], 1.0 / D,
                                        None, ALU.mult),
              ("drain", sDV))
        t_mean = sDV.inc()
        bfree[0] = (sDV, t_mean)
        _step(p, "scalar", [(sDV, t_mean)],
              lambda s: s.activation(out=rw2(P_LNV), in_=rw2(P_MEAN), func=AF.Square),
              ("drain", sAC))
        m2_t = sAC.inc()
        _step(p, "vector", [(sAC, m2_t), (sMM, s2_done)],
              lambda v: v.scalar_tensor_tensor(rw2(P_VAR), pb[1][0:1, :], 1.0 / D,
                                               rw2(P_LNV), ALU.mult, ALU.subtract),
              ("drain", sDV))
        var_t = sDV.inc()
        bfree[1] = (sDV, var_t)

        def sc_st1(s):
            s.activation(out=rw2(P_LNV), in_=rw2(P_VAR), func=AF.Ln,
                         bias=r.epsc[0:1, 0:1])
            s.drain()
            s.activation(out=rw2(P_RSTD), in_=rw2(P_LNV), func=AF.Exp, scale=-0.5)
            s.activation(out=g2keep[:], in_=rw2(P_LNV), func=AF.Exp, scale=-1.0)
        _step(p, "scalar", [(sDV, var_t)] + ([g2_war] if g2_war else []),
              sc_st1, ("drain", sAC))
        t_mr = sAC.inc()
        _step(p, "vector", [(sAC, t_mr)],
              lambda v: v.tensor_tensor(rw2(P_VAR), rw2(P_VAR), g2keep[:], ALU.mult),
              ("drain", sDV))
        u_t = sDV.inc()

        def sc_st2(s):
            s.activation(out=rw2(P_LNV), in_=rw2(P_VAR), func=AF.Ln,
                         bias=r.epsc[0:1, 0:1])
            s.drain()
            s.activation(out=g2keep[:], in_=rw2(P_LNV), func=AF.Exp, scale=-0.5)
        _step(p, "scalar", [(sDV, u_t)], sc_st2, ("drain", sAC))
        t_g2 = sAC.inc()
        return (t_mean, t_mr, t_g2)

    pst = None
    for c in range(NCH):
        if c == 0:
            _step(p, "vector", [(r.s_id, r.consts_done)],
                  lambda v: v.tensor_scalar(nl127[:], r.l127[:], -1.0, None, ALU.mult),
                  ("drain", sDV))
            sDV.inc()
            pst = emit_stats(0, None)
        t_mean, t_mr, t_g2 = pst
        _step(p, "tensor", bw(0) + bw(1) + [(sAC, t_mr), (sDV, t_mean)],
              lambda t: (t.matmul(pb[0][:, :], r.ones_f_row1[:], rw2(P_MEAN),
                                  start=True, stop=True).then_inc(sMM.h, 1),
                         t.matmul(pb[1][:, :], r.ones_f_row1[:], rw2(P_RSTD),
                                  start=True, stop=True))[1],
              ("then", sMM))
        sMM.inc(2)
        bc_done = sMM.n
        _step(p, "vector", [(sMM, bc_done), (sAC, t_g2)] + fw("scr"),
              lambda v, c=c: v.tensor_tensor(
                  scr[:].rearrange("p (j t) -> p j t", j=DC), xap(c), pbcast(pb[0]),
                  ALU.subtract),
              ("drain", sDV))
        sDV.inc()
        _step(p, "vector", [],
              lambda v: v.tensor_tensor(
                  scr[:].rearrange("p (j t) -> p j t", j=DC),
                  scr[:].rearrange("p (j t) -> p j t", j=DC), pbcast(pb[1]), ALU.mult),
              ("drain", sDV))
        xn32_done = sDV.inc()
        bfree[0] = (sDV, xn32_done)
        bfree[1] = (sDV, xn32_done)
        _step(p, "vector", fw("xnq"),
              lambda v: v.tensor_copy(xnq[:], scr[:]),
              ("drain", sDV))
        xn_done = sDV.inc()

        qev = []
        for oc in range(DC):
            buf = oc % 2
            _step(p, "sync", fw(f"wqs{buf}") + [(sWS[buf], sWS[buf].n)],
                  lambda s, oc=oc, buf=buf: _slab(
                      s, sWS[buf].h, wqs[buf][:], r.wqT[oc]),
                  ("then", sWS[buf]), inck=16)
            ws_t = sWS[buf].inc(64)
            mb = 4 + (oc % 2)
            for dc in range(DC):
                wl = [(sWS[buf], ws_t), (sDV, xn_done)] + (bw(mb) if dc == 0 else [])
                _step(p, "tensor", wl,
                      lambda t, mb=mb, buf=buf, dc=dc: t.matmul(
                          pb[mb][:, :], wqs[buf][:, dc * 128:(dc + 1) * 128],
                          xnq[:, dc * CH:(dc + 1) * CH], start=(dc == 0),
                          stop=(dc == DC - 1)),
                      ("then", sMM))
                mm_t = sMM.inc()
            free[f"wqs{buf}"] = (sMM, mm_t)
            _step(p, "vector", [(sMM, mm_t)] + (fw("qaf") if oc == 0 else []),
                  lambda v, mb=mb, oc=oc: v.tensor_scalar(
                      qaf[:, oc * CH:(oc + 1) * CH], pb[mb][:, :],
                      r.bqc[:, oc:oc + 1], None, ALU.add),
                  ("drain", sDV))
            bfree[mb] = (sDV, sDV.inc())
            qev.append(sDV.n)
        qproj_done = sDV.n

        hf32 = hbf[:].bitcast(F32)
        _step(p, "vector", fw("hbf"),
              lambda v, hf32=hf32: v.tensor_scalar(
                  hf32[:, 0:DC * CH].bitcast(I32), scr[:].bitcast(I32),
                  0x7FFFFFFF, None, ALU.bitwise_and),
              ("drain", sDV))
        sDV.inc()

        def v_tree(v, hf32=hf32):
            af = hf32[:, 0:DC * CH].rearrange("p (j t) -> p j t", j=DC)
            v.tensor_tensor(af[:, 0:4, :], af[:, 0:4, :], af[:, 4:8, :], ALU.max)
            v.drain()
            v.tensor_tensor(af[:, 0:2, :], af[:, 0:2, :], af[:, 2:4, :], ALU.max)
            v.drain()
            v.tensor_tensor(af[:, 0:1, :], af[:, 0:1, :], af[:, 1:2, :], ALU.max)
        _step(p, "vector", [], v_tree, ("drain", sDV))
        tree_done = sDV.inc()
        for j in range(4):
            _step(p, "tensor", [(sDV, tree_done)] + (bw(2) if j == 0 else []),
                  lambda t, j=j, hf32=hf32: t.matmul(
                      pb[2][:, j * 128:(j + 1) * 128],
                      hf32[:, j * 128:(j + 1) * 128], r.idf[:],
                      is_transpose=True, start=True, stop=True),
                  ("then", sMM))
            sMM.inc()
        tr_done = sMM.n
        free["hbf"] = (sMM, tr_done)

        def v_mxred(v):
            for j in range(4):
                v.reduce_max(mxcol[:, j:j + 1], pb[2][:, j * 128:(j + 1) * 128],
                             axis=mybir.AxisListType.X)
            v.drain()
        _step(p, "vector", [(sMM, tr_done)], v_mxred, ("drain", sDV))
        bfree[2] = (sDV, sDV.inc())
        _step(p, "tensor", bw(2),
              lambda t: t.matmul(pb[2][0:4, 0:128], mxcol[:, 0:4], r.idf[:],
                                 is_transpose=True, start=True, stop=True),
              ("then", sMM))
        sMM.inc()
        _step(p, "vector", [(sMM, sMM.n)] + fw("trst"),
              lambda v: v.tensor_copy(trst[0:4, :], pb[2][0:4, 0:128]),
              ("drain", sDV))
        bfree[2] = (sDV, sDV.inc())
        _step(p, "sync", [(sDV, sDV.n)],
              lambda s: s.dma_start(out=rw(MX), in_=trst[0:4, :]),
              ("then", sRD), inck=16)
        mx_done = sRD.inc(16)
        free["trst"] = (sRD, mx_done)

        def v_rows3(v):
            v.tensor_tensor(rw(HMAX), g2keep[:], rw(MX), ALU.mult)
            v.drain()
            v.tensor_scalar(rw(HMAX), rw(HMAX), 1e-5, None, ALU.max)
        _step(p, "vector", [(sRD, mx_done), (sAC, t_g2)], v_rows3, ("drain", sDV))
        sDV.inc()

        def sc_rows3(s):
            s.activation(out=rw(LNH), in_=rw(HMAX), func=AF.Ln)
            s.drain()
            s.activation(out=rw(S1), in_=rw(LNH), func=AF.Exp, scale=-1.0,
                         bias=r.l127[0:1, 0:1])
        _step(p, "scalar", [(sDV, sDV.n)], sc_rows3, ("drain", sAC))
        sAC.inc()

        def v_rows4(v):
            v.tensor_tensor(rw(C1), g2keep[:], rw(S1), ALU.mult)
            v.tensor_scalar(rw(SAr), rw(HMAX), 1.0 / 127.0, None, ALU.mult)
            v.drain()
            v.tensor_tensor(rw(SAr), rw(SAr), bass.AP(r.mw1, 0, [[1, 1], [0, CH]]),
                            ALU.mult)
        _step(p, "vector", [(sAC, sAC.n)], v_rows4, ("drain", sDV))
        rows4_done = sDV.inc()
        _step(p, "tensor", bw(0) + bw(1) + [(sDV, rows4_done)],
              lambda t: (t.matmul(pb[0][:, :], r.ones_f_row1[:], rw(C1),
                                  start=True, stop=True).then_inc(sMM.h, 1),
                         t.matmul(pb[1][:, :], r.ones_f_row1[:], rw(SAr),
                                  start=True, stop=True))[1],
              ("then", sMM))
        sMM.inc(2)
        cb_done = sMM.n
        _step(p, "vector", [(sMM, cb_done)],
              lambda v: v.tensor_copy(sA_sb[:], pb[1][:, :]),
              ("drain", sDV))
        bfree[1] = (sDV, sDV.inc())
        _step(p, "vector", [(sMM, cb_done), (sDV, qproj_done)],
              lambda v: v.tensor_tensor(
                  scr[:].rearrange("p (j t) -> p j t", j=DC),
                  scr[:].rearrange("p (j t) -> p j t", j=DC), pbcast(pb[0]), ALU.mult),
              ("drain", sDV))
        bfree[0] = (sDV, sDV.inc())
        def v_cast(v):
            for dc in range(DC):
                b2_ = dc % 2
                v.tensor_scalar(intq2[b2_][:], scr[:, dc * CH:(dc + 1) * CH],
                                r.offc[:, 0:1], None, ALU.add)
                v.drain()
                v.tensor_copy(xnq[:, dc * CH:(dc + 1) * CH], intq2[b2_][:])
                v.drain()
        _step(p, "vector", [(sMM, tr_done)], v_cast, ("drain", sDV))
        xq1_done = sDV.inc()
        free["scr"] = (sDV, xq1_done)

        _step(p, "scalar", [(sDV, qproj_done)] + fw("e0") + fw("e1")
              + fw("zc40") + fw("zc41"),
              lambda s: (s.activation(out=e_sb[0][:], in_=qaf[:, 0:2048],
                                      func=AF.Square),
                         s.drain(),
                         s.activation(out=e_sb[1][:], in_=qaf[:, 2048:4096],
                                      func=AF.Square))[-1],
              ("drain", sAC))
        sqq_done = sAC.inc()
        for oc in range(DC):
            _step(p, "tensor", [(sAC, sqq_done)] + bw(1)
                  + ([r.bconsts_done] if (c == 0 and oc == 0) else []),
                  lambda t, oc=oc: t.matmul(
                      pb[1][0:2, :], r.e2[:],
                      e_sb[oc // 4][:, (oc % 4) * 512:(oc % 4) * 512 + 512],
                      start=True, stop=True),
                  ("then", sMM))
            mm_t = sMM.inc()

            def v_m(v, oc=oc):
                for pi in range(4):
                    v.reduce_max(m2acc[0:2, 4 + pi:5 + pi],
                                 pb[1][0:2, 128 * pi:128 * pi + 128],
                                 axis=mybir.AxisListType.X)
                v.drain()
                if oc == 0:
                    v.tensor_copy(m2acc[0:2, 0:4], m2acc[0:2, 4:8])
                else:
                    v.tensor_tensor(m2acc[0:2, 0:4], m2acc[0:2, 0:4],
                                    m2acc[0:2, 4:8], ALU.max)
            _step(p, "vector", [(sMM, mm_t)], v_m, ("drain", sDV))
            bfree[1] = (sDV, sDV.inc())
        _step(p, "tensor", bw(1) + [(sDV, sDV.n)],
              lambda t: t.matmul(pb[1][0:4, 0:2], m2acc[0:2, 0:4], r.idf[0:2, 0:2],
                                 is_transpose=True, start=True, stop=True),
              ("then", sMM))
        sMM.inc()

        def v_m2(v):
            v.reduce_max(m41[0:4, 0:1], pb[1][0:4, 0:2], axis=mybir.AxisListType.X)
            v.drain()
            v.tensor_scalar(m41[0:4, 0:1], m41[0:4, 0:1], -1.0 / 32.0, None, ALU.mult)
        _step(p, "vector", [(sMM, sMM.n)], v_m2, ("drain", sDV))
        bfree[1] = (sDV, sDV.inc())
        _step(p, "tensor", bw(1) + [(sDV, sDV.n)],
              lambda t: t.matmul(pb[1][0:1, 0:4], m41[0:4, 0:1], r.idf[0:4, 0:4],
                                 is_transpose=True, start=True, stop=True),
              ("then", sMM))
        sMM.inc()
        _step(p, "vector", [(sMM, sMM.n)],
              lambda v: v.tensor_copy(mrow2[:], pb[1][0:1, 0:4]),
              ("drain", sDV))
        bfree[1] = (sDV, sDV.inc())
        _step(p, "tensor", bw(1) + [(sDV, sDV.n)],
              lambda t: t.matmul(pb[1][:, 0:4], r.ones_f_row1[:], mrow2[0:1, 0:4],
                                 start=True, stop=True),
              ("then", sMM))
        sMM.inc()
        _step(p, "vector", [(sMM, sMM.n)],
              lambda v: v.tensor_copy(mcols[:], pb[1][:, 0:4]),
              ("drain", sDV))
        bfree[1] = (sDV, sDV.inc())
        mprep_done = sDV.n
        free["e0"] = (sMM, sMM.n)
        free["e1"] = (sMM, sMM.n)

        gl_ts = []

        def emit_mm1(fc):
            buf = fc % 2
            _step(p, "sync", fw(f"w1s{buf}") + [(sWS[buf], sWS[buf].n)],
                  lambda s, fc=fc, buf=buf: _slab(
                      s, sWS[buf].h, w1s[buf][:], r.w1T[fc]),
                  ("then", sWS[buf]), inck=16)
            ws_t = sWS[buf].inc(64)
            mb = 4 + buf
            for dc in range(DC):
                wl = [(sWS[buf], ws_t), (sDV, xq1_done)] + (bw(mb) if dc == 0 else [])
                _step(p, "tensor", wl,
                      lambda t, mb=mb, buf=buf, dc=dc: t.matmul(
                          pb[mb][:, :], w1s[buf][:, dc * 128:(dc + 1) * 128],
                          xnq[:, dc * CH:(dc + 1) * CH],
                          start=(dc == 0), stop=(dc == DC - 1)),
                      ("then", sMM))
                mm_t = sMM.inc()
            free[f"w1s{buf}"] = (sMM, mm_t)
            free["xnq"] = (sMM, mm_t)
            _step(p, "vector", [(sMM, mm_t)] + fw(f"sqs{buf}"),
                  lambda v, mb=mb, buf=buf, fc=fc: v.scalar_tensor_tensor(
                      sqs[buf][:], pb[mb][:, :], r.corr1c[:, fc:fc + 1], sA_sb[:],
                      ALU.add, ALU.mult),
                  ("drain", sDV))
            bfree[mb] = (sDV, sDV.inc())
            _step(p, "scalar", [(sDV, sDV.n)] + (fw("hbf") if fc == 0 else []),
                  lambda s, buf=buf, fc=fc: s.activation(
                      out=hbf[:, fc * CH:(fc + 1) * CH], in_=sqs[buf][:], func=AF.Gelu,
                      bias=r.b1c[:, fc:fc + 1]),
                  ("drain", sAC))
            gl_ts.append(sAC.inc())
            free[f"sqs{buf}"] = (sAC, gl_ts[-1])

        asm_done = []
        for pp in range(2):
            ga_ev = {}
            for pi, (a, rr_) in enumerate([(0, 0), (1, 1), (0, 1), (1, 0)]):
                gb = pi % 2
                gps = psA[:, gb * 1024:(gb + 1) * 1024]
                bks = (0, 1) if gb == 0 else (4, 5)
                sel = r.idb if a == rr_ else r.swapI
                wl = [(sDV, qproj_done)]
                if pi < 2:
                    wl += bw(bks[0]) + bw(bks[1])
                else:
                    wl.append(ga_ev[pi - 2])

                def t_ga(t, a=a, rr_=rr_, gps=gps, sel=sel, pp=pp):
                    mov = qaf[rr_ * 64:rr_ * 64 + 64, :].rearrange(
                        "p (j t) -> p j t", j=DC)[:, :, 128 * (2 * pp + a):
                                                  128 * (2 * pp + a) + 128]
                    t.matmul(gps[:, 0:512], sel[rr_ * 64:rr_ * 64 + 64, :],
                             mov[:, 0:4, :], start=True, stop=True).then_inc(sMM.h, 1)
                    return t.matmul(gps[:, 512:1024], sel[rr_ * 64:rr_ * 64 + 64, :],
                                    mov[:, 4:8, :], start=True, stop=True)
                _step(p, "tensor", wl, t_ga, ("then", sMM))
                mm_t = sMM.inc(2)
                wl2 = [(sMM, mm_t)] + (fw("qpk") if pi == 0 else [])

                def v_ga(v, a=a, rr_=rr_, gps=gps, pp=pp):
                    src = gps[a * 64:a * 64 + 64, :].rearrange("p (u g) -> p u g", g=128)
                    dst = qpk[pp][a * 64:a * 64 + 64, :].rearrange(
                        "p (u g) -> p u g", g=128)[:, rr_::2, :]
                    v.tensor_copy(dst, src)
                _step(p, "vector", wl2, v_ga, ("drain", sDV))
                ga_ev[pi] = (sDV, sDV.inc())
                for b in bks:
                    bfree[b] = ga_ev[pi]
            ga_vt = {0: ga_ev[2], 1: ga_ev[3]}

            for a in range(2):
                for Rb in range(16):
                    wl = [ga_vt[a]] + (bw(0) if Rb == 0 else [])
                    _step(p, "tensor", wl,
                          lambda t, pp=pp, a=a, Rb=Rb: t.matmul(
                              pb[0][:].bitcast(BF16)[:, 64 * Rb:64 * Rb + 64],
                              qpk[pp][a * 64:a * 64 + 64, 128 * Rb:128 * Rb + 128],
                              r.idb[a * 64:a * 64 + 64, a * 64:a * 64 + 64],
                              is_transpose=True, start=True, stop=True),
                          ("then", sMM))
                    sMM.inc()
                _step(p, "vector", [(sMM, sMM.n)] + fw(f"vt{a}"),
                      lambda v, a=a: v.tensor_copy(vt[a][:], pb[0][:].bitcast(BF16)),
                      ("drain", sDV))
                bfree[0] = (sDV, sDV.inc())
            scbuf = [psA[:, 0:2 * CH], psA[:, 2 * CH:4 * CH]]
            exp_toks = []
            exp_done = {0: 0, 1: 0}
            av_prev = {0: 0, 1: 0}

            def emit_scores(a, Rb, rnd):
                i = len(exp_toks)
                sbu = i % 2
                sh = scbuf[sbu]
                for kc in (2 * rnd, 2 * rnd + 1):
                    wl = [(sDV, mprep_done)] if (i == 0 and kc == 2 * rnd) else []
                    if kc == 2 * rnd:
                        if i >= 2:
                            wl.append((sAC, exp_toks[i - 2]))
                        else:
                            wl += (bw(0) + bw(1)) if sbu == 0 else (bw(4) + bw(5))
                    _step(p, "tensor", wl,
                          lambda t, pp=pp, a=a, Rb=Rb, kc=kc, sh=sh: t.matmul(
                              sh[:, (kc % 2) * CH:(kc % 2) * CH + CH],
                              qpk[pp][a * 64:a * 64 + 64, 128 * Rb:128 * Rb + 128],
                              qpk[pp][a * 64:a * 64 + 64, kc * 512:(kc + 1) * 512],
                              start=True, stop=True),
                          ("then", sMM))
                    sMM.inc()
                wl2 = [(sMM, sMM.n)] + (
                    (fw(f"e{a}") + fw(f"zc4{a}")) if (Rb == 0 and rnd == 0) else [])
                _step(p, "scalar", wl2,
                      lambda s, a=a, Rb=Rb, rnd=rnd, pp=pp, sh=sh: s.activation(
                          out=e_sb[a][:, rnd * 1024:(rnd + 1) * 1024],
                          in_=sh, func=AF.Exp, scale=1.0 / 32.0,
                          bias=mcols[:, 2 * pp + a:2 * pp + a + 1],
                          accum_out=zc4[a][:, 2 * Rb + rnd:2 * Rb + rnd + 1]),
                      ("drain", sAC))
                exp_toks.append(sAC.inc())

            def emit_av(a, Rb):
                for qc in range(4):
                    avb = pbAV[qc]
                    avi = [2, 3, 6, 7][qc]
                    wl = [(sAC, exp_done[a])]
                    if Rb == 0:
                        wl += bw(avi)
                    _step(p, "tensor", wl,
                          lambda t, a=a, Rb=Rb, qc=qc, avb=avb: t.matmul(
                              avb[a * 64:a * 64 + 64, :],
                              vt[a][:, 64 * Rb:64 * Rb + 64],
                              e_sb[a][:, qc * 512:(qc + 1) * 512],
                              start=(Rb == 0), stop=(Rb == 15),
                              tile_position=(0, 64 * a),
                              skip_group_check=True),
                          ("then", sMM))
                    sMM.inc()
                av_prev[a] = sMM.n

            for Rb in range(16):
                for a in range(2):
                    if Rb > 0:
                        emit_av(a, Rb - 1)
                    emit_scores(a, Rb, 0)
                    emit_scores(a, Rb, 1)
                    exp_done[a] = sAC.n
            for a in range(2):
                emit_av(a, 15)
            bfree[0] = bfree[1] = (sAC, exp_toks[-2])
            bfree[4] = bfree[5] = (sAC, exp_toks[-1])
            av_done = sMM.n
            free["vt0"] = (sMM, av_done)
            free["vt1"] = (sMM, av_done)
            free["qpk"] = (sMM, av_done)

            for a in range(2):
                def v_z(v, a=a):
                    zv = zc4[a][:].rearrange("p (r k) -> p r k", k=2)
                    v.tensor_tensor(z16[:].rearrange("p (r o) -> p r o", o=1),
                                    zv[:, :, 0:1], zv[:, :, 1:2], ALU.add)
                    v.drain()
                    v.reciprocal(z16[:], z16[:])
                _step(p, "vector", [(sAC, sAC.n)], v_z, ("drain", sDV))
                sDV.inc()
                free[f"zc4{a}"] = (sDV, sDV.n)
                _step(p, "tensor", bw(0) + [(sDV, sDV.n)],
                      lambda t: t.matmul(pb[0][0:16, 0:128], z16[:], r.idf[:],
                                         is_transpose=True, start=True, stop=True),
                      ("then", sMM))
                sMM.inc()
                _step(p, "vector", [(sMM, sMM.n)] + fw("trst"),
                      lambda v: v.tensor_copy(trst[0:16, :], pb[0][0:16, 0:128]),
                      ("drain", sDV))
                bfree[0] = (sDV, sDV.inc())
                _step(p, "sync", [(sDV, sDV.n)],
                      lambda s, a=a: s.dma_start(out=zrow[a], in_=trst[0:16, :]),
                      ("then", sRD), inck=16)
                sRD.inc(16)
                free["trst"] = (sRD, sRD.n)
            z_done = sRD.n

            for qc in range(4):
                def t_div(t, qc=qc):
                    t.matmul(pb[0][:, :], r.elo[:], scr[0:1, qc * 512:(qc + 1) * 512],
                             start=True, stop=False).then_inc(sMM.h, 1)
                    return t.matmul(pb[0][:, :], r.ehi[:],
                                    scr[0:1, 2048 + qc * 512:2048 + (qc + 1) * 512],
                                    start=False, stop=True)
                _step(p, "tensor", bw(0) + [(sRD, z_done)], t_div, ("then", sMM))
                sMM.inc(2)
                _step(p, "vector", [(sMM, sMM.n)],
                      lambda v: v.tensor_copy(div_sb[:], pb[0][:, :]),
                      ("drain", sDV))
                bfree[0] = (sDV, sDV.inc())
                _step(p, "vector", [(sMM, av_done)],
                      lambda v, qc=qc: v.tensor_tensor(
                          e_sb[0][:, qc * 512:(qc + 1) * 512], pbAV[qc][:, :],
                          div_sb[:], ALU.mult),
                      ("drain", sDV))
                dv_t = sDV.inc()
                bfree[[2, 3, 6, 7][qc]] = (sDV, dv_t)
            evac_done = sDV.n
            free["e1"] = (sDV, evac_done)
            free["scr"] = (sMM, sMM.n)

            asm_ev = {}
            for pi, (a, rr_) in enumerate([(0, 0), (1, 1), (0, 1), (1, 0)]):
                gb = pi % 2
                gps = psA[:, gb * 1024:(gb + 1) * 1024]
                bks = (0, 1) if gb == 0 else (4, 5)
                sel = r.idb if a == rr_ else r.swapI
                wl = [(sDV, evac_done)]
                if pi < 2:
                    wl += bw(bks[0]) + bw(bks[1])
                else:
                    wl.append(asm_ev[pi - 2])

                def t_as(t, a=a, rr_=rr_, gps=gps, sel=sel):
                    mov = e_sb[0][a * 64:a * 64 + 64, :].rearrange(
                        "p (u g) -> p u g", g=128)[:, rr_::2, :]
                    t.matmul(gps[:, 0:512], sel[a * 64:a * 64 + 64, :],
                             mov[:, 0:4, :], start=True, stop=True).then_inc(sMM.h, 1)
                    return t.matmul(gps[:, 512:1024], sel[a * 64:a * 64 + 64, :],
                                    mov[:, 4:8, :], start=True, stop=True)
                _step(p, "tensor", wl, t_as, ("then", sMM))
                mm_t = sMM.inc(2)
                wl2 = [(sMM, mm_t)] + (fw("qaf") if pi == 0 else [])

                def v_as(v, a=a, rr_=rr_, gps=gps, pp=pp):
                    src = gps[rr_ * 64:rr_ * 64 + 64, :].rearrange(
                        "p (u g) -> p u g", g=128)
                    dst = qaf[rr_ * 64:rr_ * 64 + 64, :].rearrange(
                        "p (j t) -> p j t", j=DC)[:, :, 128 * (2 * pp + a):
                                                  128 * (2 * pp + a) + 128]
                    v.tensor_copy(dst, src)
                _step(p, "vector", wl2, v_as, ("drain", sDV))
                asm_ev[pi] = (sDV, sDV.inc())
                for b in bks:
                    bfree[b] = asm_ev[pi]
            asm_done.append(sDV.n)
            free["e0"] = (sMM, sMM.n)

        if c + 1 < NCH:
            pst = emit_stats(c + 1, (sDV, rows4_done))
        for fc in range(FC):
            emit_mm1(fc)

        for oc in range(DC):
            buf = oc % 2
            _step(p, "sync", fw(f"wqs{buf}") + [(sWS[buf], sWS[buf].n)],
                  lambda s, oc=oc, buf=buf: _slab(
                      s, sWS[buf].h, wqs[buf][:], r.wuT[oc]),
                  ("then", sWS[buf]), inck=16)
            ws_t = sWS[buf].inc(64)
            mb = 4 + (oc % 2)
            for dc in range(DC):
                wl = [(sWS[buf], ws_t), (sDV, asm_done[-1])] + (bw(mb) if dc == 0 else [])
                _step(p, "tensor", wl,
                      lambda t, mb=mb, buf=buf, dc=dc: t.matmul(
                          pb[mb][:, :], wqs[buf][:, dc * 128:(dc + 1) * 128],
                          qaf[:, dc * CH:(dc + 1) * CH], start=(dc == 0),
                          stop=(dc == DC - 1)),
                      ("then", sMM))
                mm_t = sMM.inc()
            free[f"wqs{buf}"] = (sMM, mm_t)
            _step(p, "vector", [(sMM, mm_t)],
                  lambda v, mb=mb, c=c, oc=oc: v.scalar_tensor_tensor(
                      xap(c, oc), pb[mb][:, :], r.buc[:, oc:oc + 1], xap(c, oc),
                      ALU.add, ALU.add),
                  ("drain", sDV))
            bfree[mb] = (sDV, sDV.inc())
        free["qaf"] = (sMM, sMM.n)

        for fc in range(FC):
            buf = fc % 2
            _step(p, "scalar", fw(f"sq2{buf}") + [(sAC, gl_ts[fc])],
                  lambda s, buf=buf, fc=fc: s.activation(
                      out=sq2[buf][:], in_=hbf[:, fc * CH:(fc + 1) * CH], func=AF.Square),
                  ("drain", sAC))
            sqh_t = sAC.inc()
            _step(p, "tensor", [(sAC, sqh_t)] + (bw(0) if fc == 0 else []),
                  lambda t, buf=buf, fc=fc: t.matmul(
                      pb[0][0:1, :], r.ones_f_col[:], sq2[buf][:],
                      start=(fc == 0), stop=(fc == FC - 1)),
                  ("then", sMM))
            mm_t = sMM.inc()

            def v_mx(v, buf=buf, fc=fc):
                if fc == 0:
                    v.tensor_copy(maxacc[:], sq2[buf][:])
                else:
                    v.tensor_tensor(maxacc[:], maxacc[:], sq2[buf][:], ALU.max)
            _step(p, "vector", [(sAC, sqh_t)] + (fw("maxacc") if fc == 0 else []),
                  v_mx, ("drain", sDV))
            dv_t = sDV.inc()
            free[f"sq2{buf}"] = [(sMM, mm_t), (sDV, dv_t)]
        ssq_done = sMM.n
        vmx_done = sDV.n

        _step(p, "vector", [(sMM, ssq_done)],
              lambda v: v.tensor_scalar(rw(SSQ), pb[0][0:1, :], 1.0 / FF, None, ALU.mult),
              ("drain", sDV))
        bfree[0] = (sDV, sDV.inc())

        def sc_hr1(s):
            s.activation(out=rw(LNV2), in_=rw(SSQ), func=AF.Ln, bias=r.epsc[0:1, 0:1])
            s.drain()
            s.activation(out=rw(RSTD2), in_=rw(LNV2), func=AF.Exp, scale=-0.5)
        _step(p, "scalar", [(sDV, sDV.n)], sc_hr1, ("drain", sAC))
        sAC.inc()
        for j in range(4):
            _step(p, "tensor", [(sDV, vmx_done)] + (bw(2) if j == 0 else []),
                  lambda t, j=j: t.matmul(
                      pb[2][:, j * 128:(j + 1) * 128], maxacc[:, j * 128:(j + 1) * 128],
                      r.idf[:], is_transpose=True, start=True, stop=True),
                  ("then", sMM))
            sMM.inc()
        free["maxacc"] = (sMM, sMM.n)
        _step(p, "vector", [(sMM, sMM.n)],
              lambda v: [v.reduce_max(mxcol[:, j:j + 1], pb[2][:, j * 128:(j + 1) * 128],
                                      axis=mybir.AxisListType.X) for j in range(4)][-1],
              ("drain", sDV))
        bfree[2] = (sDV, sDV.inc())
        _step(p, "tensor", bw(2),
              lambda t: t.matmul(pb[2][0:4, 0:128], mxcol[:, 0:4], r.idf[:],
                                 is_transpose=True, start=True, stop=True),
              ("then", sMM))
        sMM.inc()
        _step(p, "vector", [(sMM, sMM.n)] + fw("trst"),
              lambda v: v.tensor_copy(trst[0:4, :], pb[2][0:4, 0:128]),
              ("drain", sDV))
        bfree[2] = (sDV, sDV.inc())
        _step(p, "sync", [(sDV, sDV.n)],
              lambda s: s.dma_start(out=rw(MXH2), in_=trst[0:4, :]),
              ("then", sRD), inck=16)
        mxh_done = sRD.inc(16)
        free["trst"] = (sRD, mxh_done)

        def sc_hr2(s):
            s.activation(out=rw(LNM), in_=rw(MXH2), func=AF.Ln)
        _step(p, "scalar", [(sRD, mxh_done)], sc_hr2, ("drain", sAC))
        sAC.inc()

        def v_hr(v):
            v.tensor_scalar(rw(SSQ), rw(LNV2), 0.5, None, ALU.mult)
            v.drain()
            v.scalar_tensor_tensor(rw(LNHN), rw(LNM), 0.5, rw(SSQ), ALU.mult,
                                   ALU.subtract)
            v.drain()
            v.tensor_scalar(rw(LNHN), rw(LNHN), math.log(1e-5), None, ALU.max)
        _step(p, "vector", [(sAC, sAC.n)], v_hr, ("drain", sDV))
        sDV.inc()

        def sc_hr3(s):
            s.activation(out=rw(S2r), in_=rw(LNHN), func=AF.Exp, scale=-1.0,
                         bias=r.l127[0:1, 0:1])
            s.drain()
            s.activation(out=rw(SBr), in_=rw(LNHN), func=AF.Exp, bias=nl127[0:1, 0:1])
        _step(p, "scalar", [(sDV, sDV.n)], sc_hr3, ("drain", sAC))
        sAC.inc()

        def v_hr2(v):
            v.tensor_tensor(rw(C2r), rw(RSTD2), rw(S2r), ALU.mult)
            v.drain()
            v.tensor_tensor(rw(SBr), rw(SBr), bass.AP(r.mw2, 0, [[1, 1], [0, CH]]),
                            ALU.mult)
        _step(p, "vector", [(sAC, sAC.n)], v_hr2, ("drain", sDV))
        hr_done = sDV.inc()

        def t_hb(t):
            t.matmul(pb[6][:, :], r.ones_f_row1[:], rw(C2r),
                     start=True, stop=True).then_inc(sMM.h, 1)
            return t.matmul(pb[1][:, :], r.ones_f_row1[:], rw(SBr),
                            start=True, stop=True)
        _step(p, "tensor", bw(6) + bw(1) + [(sDV, hr_done)], t_hb, ("then", sMM))
        sMM.inc(2)
        hb_done = sMM.n
        _step(p, "vector", [(sMM, hb_done)],
              lambda v: v.tensor_copy(sB_sb[:], pb[1][:, :]),
              ("drain", sDV))
        bfree[1] = (sDV, sDV.inc())

        hq_toks = []
        for fc in range(FC):
            def v_q(v, fc=fc):
                b2_ = fc % 2
                hv = hbf[:, fc * CH:(fc + 1) * CH]
                v.tensor_tensor(sqs[b2_][:], hv, pb[6][:, :], ALU.mult)
                v.drain()
                v.tensor_scalar(intq2[b2_][:], sqs[b2_][:], r.offc[:, 0:1], None, ALU.add)
                v.drain()
                v.tensor_copy(hv, intq2[b2_][:])
                v.drain()
            _step(p, "vector", [(sMM, hb_done)], v_q, ("drain", sDV))
            hq_toks.append(sDV.inc())
        hq_done = sDV.n
        bfree[6] = (sDV, hq_done)

        for oc in range(DC):
            w2h_t = []
            for hf in range(2):
                _step(p, "sync", fw(f"w2h{hf}") + [(sW2h[hf], sW2h[hf].n)],
                      lambda s, oc=oc, hf=hf: _slab(
                          s, sW2h[hf].h, w2s[:, hf * 2048:(hf + 1) * 2048],
                          r.w2T[oc][:, hf * 2048:(hf + 1) * 2048]),
                      ("then", sW2h[hf]), inck=16)
                w2h_t.append(sW2h[hf].inc(64))
            mb = 4 + (oc % 2)
            for fc2 in range(FC):
                hf = fc2 // 16
                wl = [(sW2h[hf], w2h_t[hf])] \
                    + ([(sDV, hq_toks[fc2])] if oc == 0 else []) \
                    + (bw(mb) if fc2 == 0 else [])
                _step(p, "tensor", wl,
                      lambda t, mb=mb, fc2=fc2: t.matmul(
                          pb[mb][:, :], w2s[:, fc2 * 128:(fc2 + 1) * 128],
                          hbf[:, fc2 * CH:(fc2 + 1) * CH],
                          start=(fc2 == 0), stop=(fc2 == FC - 1)),
                      ("then", sMM))
                mm_t = sMM.inc()
                if fc2 == 15:
                    free["w2h0"] = (sMM, mm_t)
            free["w2h1"] = (sMM, mm_t)

            def v_ev2(v, mb=mb, c=c, oc=oc, buf=oc % 2):
                v.scalar_tensor_tensor(sqs[buf][:], pb[mb][:, :], r.corr2c[:, oc:oc + 1],
                                       sB_sb[:], ALU.add, ALU.mult)
                v.drain()
                v.scalar_tensor_tensor(xap(c, oc), sqs[buf][:], r.b2c[:, oc:oc + 1],
                                       xap(c, oc), ALU.add, ALU.add)
            _step(p, "vector", [(sMM, mm_t)] + fw(f"sqs{oc % 2}"), v_ev2, ("drain", sDV))
            dv_t = sDV.inc()
            bfree[mb] = (sDV, dv_t)
            free[f"sqs{oc % 2}"] = (sDV, dv_t)
        free["hbf"] = (sMM, sMM.n)

    r.l0_done = (sDV, sDV.n)



def _chunk_stats(r, p, pb, sMM, sDV, sAC, rw, src_ap_fn, sqpc, sqst, MEAN, M2, VAR,
                 LNV, RSTD, prev_ac, bw, bfree, src_ready=None):
    for dc in range(DC):
        buf = dc % 2
        wl = ([] if sqst[buf] == 0 else [(sMM, sqst[buf])])
        if src_ready is not None and dc == 0:
            wl = wl + [src_ready]
        _step(p, "scalar", wl,
              lambda s, dc=dc, buf=buf: s.activation(
                  out=sqpc[buf][:], in_=src_ap_fn(dc), func=AF.Square),
              ("drain", sAC))
        ac_t = sAC.inc()
        _step(p, "tensor", [(sAC, ac_t)] + (bw(1) if dc == 0 else []),
              lambda t, dc=dc, buf=buf: t.matmul(
                  pb[1][0:1, :], r.ones_bf_col[:], sqpc[buf][:],
                  start=(dc == 0), stop=(dc == DC - 1)),
              ("then", sMM))
        sqst[buf] = sMM.inc()
    s2_done = sMM.n
    for dc in range(DC):
        _step(p, "tensor", bw(0) if dc == 0 else [],
              lambda t, dc=dc: t.matmul(
                  pb[0][0:1, :], r.ones_bf_col[:], src_ap_fn(dc),
                  start=(dc == 0), stop=(dc == DC - 1)),
              ("then", sMM))
        s1_done = sMM.inc()
    _step(p, "vector", [(sMM, s1_done), (sAC, prev_ac)],
          lambda v: v.tensor_scalar(rw(MEAN), pb[0][0:1, :], 1.0 / D, None, ALU.mult),
          ("drain", sDV))
    bfree[0] = (sDV, sDV.inc())
    _step(p, "scalar", [(sDV, sDV.n)],
          lambda s: s.activation(out=rw(M2), in_=rw(MEAN), func=AF.Square),
          ("drain", sAC))
    sAC.inc()
    _step(p, "vector", [(sAC, sAC.n), (sMM, s2_done)],
          lambda v: v.scalar_tensor_tensor(rw(VAR), pb[1][0:1, :], 1.0 / D, rw(M2),
                                           ALU.mult, ALU.subtract),
          ("drain", sDV))
    bfree[1] = (sDV, sDV.inc())

    def sc_r(s):
        s.activation(out=rw(LNV), in_=rw(VAR), func=AF.Ln, bias=r.epsc[0:1, 0:1])
        s.drain()
        s.activation(out=rw(RSTD), in_=rw(LNV), func=AF.Exp, scale=-0.5)
    _step(p, "scalar", [(sDV, sDV.n)], sc_r, ("drain", sAC))
    return sAC.inc()


def emit_l1(r, p, psb, out):
    nc = r.nc
    sb = psb["sb"]
    st = psb["st"]
    xf = r.x_fm
    xn = r.xn_buf

    sqpc = [sb(f"l1_sq{i}", [128, CH], BF16) for i in range(2)]
    evsc = [sb(f"l1_ev{i}", [128, CH], F32) for i in range(2)]
    sqst = {0: 0, 1: 0}
    _i2 = sb("l1_int", [128, CH], I32)
    intq2 = [_i2, _i2]
    wts = [sb(f"l1_wts{i}", [128, DC * 128], BF16) for i in range(2)]
    rows = sb("l1_rows", [1, 4 * CH], F32)
    MEAN, M2, VAR, RSTD = range(4)
    LNV = VAR
    mixb = sb("l1_mixb", [128, DC * CH], BF16)
    xln = sb("l1_xln", [128, DC * CH], BF16)
    ekt = sb("l1_ek", [128, DC * CH], BF16)
    vt_ = sb("l1_v", [128, DC * CH], BF16)
    rst = sb("l1_rs", [128, DC * CH], BF16)
    At = sb("l1_A", [128, DC * CH], BF16)
    Bt = sb("l1_B", [128, DC * CH], BF16)
    carryA = sb("l1_cA", [128, DC], F32)
    carryB = sb("l1_cB", [128, DC], F32)
    xcar = sb("l1_xcar", [128, DC], BF16)
    ycar = sb("l1_ycar", [128, DC], BF16)
    com = sb("l1_com", [128, DC], F32)
    wfks = wts
    wfvs = [rst[:, 0:D], rst[:, D:2 * D]]
    kks = [rst[:, 2 * D:2 * D + 256], rst[:, 2 * D + 256:2 * D + 512]]
    xiq = vt_
    ynt = xln
    ymt = At
    rs2 = Bt
    outst = ekt[:].bitcast(F32)
    _mixf = mixb[:].bitcast(F32)
    tokst = [_mixf[:, 0:D], _mixf[:, D:2 * D]]

    pb = [st.enter_context(nc.psum_tensor(f"l1_pb{i}", [128, CH], F32)) for i in range(4)]
    pso = st.enter_context(nc.psum_tensor("l1_pso", [128, DC * 256], F32))
    bfree = {i: None for i in range(4)}

    def bw(i):
        return [bfree[i]] if bfree[i] else []

    sMM = r.sem("l1_mm")
    sDV = r.sem("l1_dv")
    sAC = r.sem("l1_ac")
    sWS = [r.sem("l1_ws0"), r.sem("l1_ws1")]
    sWV = [r.sem("l1_wv0"), r.sem("l1_wv1")]
    sOD = [r.sem("l1_od0"), r.sem("l1_od1")]

    free = {}
    for nm in ("wts0", "wts1", "xiq", "mixb", "tmp2", "xln", "ek", "v", "rs", "A", "B",
               "yn", "ym", "rs2", "kks0", "kks1", "wfks0", "wfks1", "wfvs0", "wfvs1",
               "outst", "tok0", "tok1", "ev0", "ev1"):
        free[nm] = None

    def fw(nm):
        f = free[nm]
        if f is None:
            return []
        return list(f) if isinstance(f, list) else [f]

    def rw(i):
        return rows[0:1, i * CH:(i + 1) * CH]

    def xap(c, dc=None):
        if dc is None:
            return bass.AP(xf, c * CH, [[DC * T, 128], [T, DC], [1, CH]])
        return bass.AP(xf, dc * T + c * CH, [[DC * T, 128], [1, CH]])

    def nap(c, dc=None):
        if dc is None:
            return bass.AP(xn, c * CH, [[DC * T, 128], [T, DC], [1, CH]])
        return bass.AP(xn, dc * T + c * CH, [[DC * T, 128], [1, CH]])

    def pbc(bank, ndc=DC):
        return bass.AP(bank, 0, [[CH, 128], [0, ndc], [1, CH]])

    def stream_w(buf, hbm, oc, sem_):
        def go(s, hbm=hbm, oc=oc, buf=buf):
            return _slab(s, sWS[buf].h, wts[buf][:], hbm[oc])
        return go

    def emit_p2a(c):
        prev_ac = sAC.n
        rstd_ac = _chunk_stats(r, p, pb, sMM, sDV, sAC, rw, lambda dc, c=c: xap(c, dc),
                               sqpc, sqst, MEAN, M2, VAR, LNV, RSTD, prev_ac, bw, bfree)
        _step(p, "tensor", bw(0) + bw(1) + [(sAC, rstd_ac)],
              lambda t: (t.matmul(pb[0][:, :], r.ones_f_row1[:], rw(MEAN),
                                  start=True, stop=True).then_inc(sMM.h, 1),
                         t.matmul(pb[1][:, :], r.ones_f_row1[:], rw(RSTD),
                                  start=True, stop=True))[1],
              ("then", sMM))
        sMM.inc(2)
        bc_done = sMM.n
        _step(p, "vector", [(sMM, bc_done)],
              lambda v, c=c: v.tensor_tensor(nap(c), xap(c), pbc(pb[0]), ALU.subtract),
              ("drain", sDV))
        sDV.inc()
        _step(p, "vector", [],
              lambda v, c=c: v.tensor_tensor(nap(c), nap(c), pbc(pb[1]), ALU.mult),
              ("drain", sDV))
        xn_done = sDV.inc()
        bfree[0] = (sDV, xn_done)
        bfree[1] = (sDV, xn_done)

        xitok = []
        for dc in range(DC):
            b2_ = dc % 2

            def v_xi1(v, c=c, dc=dc, b2_=b2_):
                v.tensor_scalar(evsc[b2_][:], nap(c, dc),
                                r.sinv_col[:, 0:1], -127.0, ALU.mult, ALU.max)
                v.drain()
                v.tensor_scalar(intq2[b2_][:], evsc[b2_][:], 127.0, r.offc[:, 0:1],
                                ALU.min, ALU.add)
                v.drain()
                v.tensor_copy(xiq[:, dc * CH:(dc + 1) * CH], intq2[b2_][:])
                v.drain()
            wl = ([(sDV, xn_done)] + fw("v") + fw("ev0") + fw("ev1")) if dc == 0 else []
            _step(p, "vector", wl, v_xi1, ("drain", sDV))
            xitok.append(sDV.inc())
        return {"xn_done": xn_done, "xitok": xitok}

    p2a = None
    for c in range(NCH):
        batch_start = (c % (SEQ // CH) == 0)
        if c == 0:
            p2a = emit_p2a(0)
        xn_done, xitok = p2a["xn_done"], p2a["xitok"]

        for oc in range(DC):
            buf = oc % 2
            _step(p, "sync", fw(f"wts{buf}") + [(sWS[buf], sWS[buf].n)]
                  + ([r.l1_casts] if c == 0 and oc == 0 else []),
                  stream_w(buf, r.wtT, oc, None), ("then", sWS[buf]), inck=16)
            ws_t = sWS[buf].inc(64)
            mb = 2 + buf
            for dc in range(DC):
                wl = [(sWS[buf], ws_t)] \
                    + ([(sDV, xitok[dc])] if oc == 0 else []) \
                    + (bw(mb) if dc == 0 else [])
                _step(p, "tensor", wl,
                      lambda t, mb=mb, buf=buf, dc=dc: t.matmul(
                          pb[mb][:, :], wts[buf][:, dc * 128:(dc + 1) * 128],
                          xiq[:, dc * CH:(dc + 1) * CH],
                          start=(dc == 0), stop=(dc == DC - 1)),
                      ("then", sMM))
                mm_t = sMM.inc()
            free[f"wts{buf}"] = (sMM, mm_t)

            def v_ev(v, mb=mb, c=c, oc=oc, buf=buf):
                v.tensor_scalar(evsc[buf][:], pb[mb][:, :], r.corrL1c[:, oc:oc + 1],
                                r.s_col[:, 0:1], ALU.add, ALU.mult)
                v.drain()
                v.scalar_tensor_tensor(xap(c, oc), evsc[buf][:], r.bL1c[:, oc:oc + 1],
                                       xap(c, oc), ALU.add, ALU.add)
            _step(p, "vector", [(sMM, mm_t)], v_ev, ("drain", sDV))
            bfree[mb] = (sDV, sDV.inc())
        free["v"] = (sMM, sMM.n)

        prev_ac = sAC.n
        rstd_ac = _chunk_stats(r, p, pb, sMM, sDV, sAC, rw, lambda dc, c=c: nap(c, dc),
                               sqpc, sqst, MEAN, M2, VAR, LNV, RSTD, prev_ac, bw, bfree,
                               src_ready=(sDV, xn_done))
        _step(p, "vector", [(sAC, rstd_ac)],
              lambda v: v.tensor_tensor(rw(M2), rw(MEAN), rw(RSTD), ALU.mult),
              ("drain", sDV))
        mr_done = sDV.inc()
        _step(p, "tensor", bw(0) + bw(1) + [(sDV, mr_done)],
              lambda t: (t.matmul(pb[0][:, :], r.ones_f_row1[:], rw(RSTD),
                                  start=True, stop=True).then_inc(sMM.h, 1),
                         t.matmul(pb[1][:, :], r.ones_f_row1[:], rw(M2),
                                  start=True, stop=True))[1],
              ("then", sMM))
        sMM.inc(2)
        bc_done = sMM.n
        _step(p, "vector", [(sMM, bc_done)] + fw("xln"),
              lambda v, c=c: v.tensor_tensor(
                  xln[:].rearrange("p (j t) -> p j t", j=DC), nap(c), pbc(pb[0]),
                  ALU.mult),
              ("drain", sDV))
        sDV.inc()
        _step(p, "vector", [],
              lambda v: v.tensor_tensor(
                  xln[:].rearrange("p (j t) -> p j t", j=DC),
                  xln[:].rearrange("p (j t) -> p j t", j=DC), pbc(pb[1]), ALU.subtract),
              ("drain", sDV))
        dv_t = sDV.inc()
        bfree[0] = (sDV, dv_t)
        bfree[1] = (sDV, dv_t)

        def v_gb(v):
            xv = xln[:].rearrange("p (j t) -> p j t", j=DC)
            v.tensor_tensor(xv, xv, bass.AP(r.g1c, 0, [[DC, 128], [1, DC], [0, CH]]),
                            ALU.mult)
            v.drain()
            v.tensor_tensor(xv, xv, bass.AP(r.b1lc, 0, [[DC, 128], [1, DC], [0, CH]]),
                            ALU.add)
        _step(p, "vector", [], v_gb, ("drain", sDV))
        xln_done = sDV.inc()

        proj_done = {}
        ektoks, vtoks, rstoks = [], [], []
        mix_war = None
        for mi, (mc, omc, whbm, nm) in enumerate([
                (r.mkc, r.omkc, r.wkB, "k"), (r.mvc, r.omvc, r.wvB, "v"),
                (r.mrc, r.omrc, r.wrB, "r")]):
            if not batch_start:
                _step(p, "vector", [],
                      lambda v, omc=omc: v.tensor_tensor(com[:], xcar[:], omc[:],
                                                         ALU.mult),
                      ("drain", sDV))
                sDV.inc()
            mixtok = []
            for dc in range(DC):
                def v_mix1(v, mc=mc, omc=omc, dc=dc, batch_start=batch_start):
                    v.tensor_scalar(mixb[:, dc * CH + 1:(dc + 1) * CH],
                                    xln[:, dc * CH:(dc + 1) * CH - 1],
                                    omc[:, dc:dc + 1], None, ALU.mult)
                    v.drain()
                    if batch_start:
                        v.memset(mixb[:, dc * CH:dc * CH + 1], 0.0)
                    else:
                        v.tensor_copy(mixb[:, dc * CH:dc * CH + 1], com[:, dc:dc + 1])
                    v.drain()
                    v.scalar_tensor_tensor(mixb[:, dc * CH:(dc + 1) * CH],
                                           xln[:, dc * CH:(dc + 1) * CH],
                                           mc[:, dc:dc + 1],
                                           mixb[:, dc * CH:(dc + 1) * CH],
                                           ALU.mult, ALU.add)
                    v.drain()
                wl = ([(sDV, xln_done)] + fw("mixb")) if (mi == 0 and dc == 0) else []
                if mix_war is not None:
                    wl = wl + [mix_war[dc]]
                _step(p, "vector", wl, v_mix1, ("drain", sDV))
                mixtok.append(sDV.inc())

            mtoks = [0] * DC
            for oc in range(DC):
                buf = oc % 2
                _step(p, "sync", fw(f"wts{buf}") + [(sWS[buf], sWS[buf].n)],
                      stream_w(buf, whbm, oc, None), ("then", sWS[buf]), inck=16)
                ws_t = sWS[buf].inc(64)
                mb = 2 + buf
                for dc in range(DC):
                    wl = [(sWS[buf], ws_t)] \
                        + ([(sDV, mixtok[dc])] if oc == 0 else []) \
                        + (bw(mb) if dc == 0 else [])
                    _step(p, "tensor", wl,
                          lambda t, mb=mb, buf=buf, dc=dc: t.matmul(
                              pb[mb][:, :], wts[buf][:, dc * 128:(dc + 1) * 128],
                              mixb[:, dc * CH:(dc + 1) * CH],
                              start=(dc == 0), stop=(dc == DC - 1)),
                          ("then", sMM))
                    mm_t = sMM.inc()
                    if oc == DC - 1:
                        mtoks[dc] = (sMM, mm_t)
                free[f"wts{buf}"] = (sMM, mm_t)
                if nm == "k":
                    _step(p, "scalar", [(sMM, mm_t)] + (fw("ek") if oc == 0 else []),
                          lambda s, mb=mb, oc=oc: s.activation(
                              out=ekt[:, oc * CH:(oc + 1) * CH], in_=pb[mb][:, :],
                              func=AF.Exp),
                          ("drain", sAC))
                    bfree[mb] = (sAC, sAC.inc())
                    ektoks.append((sAC, sAC.n))
                elif nm == "v":
                    _step(p, "vector", [(sMM, mm_t)] + (fw("v") if oc == 0 else []),
                          lambda v, mb=mb, oc=oc: v.tensor_copy(
                              vt_[:, oc * CH:(oc + 1) * CH], pb[mb][:, :]),
                          ("drain", sDV))
                    bfree[mb] = (sDV, sDV.inc())
                    vtoks.append((sDV, sDV.n))
                else:
                    _step(p, "scalar", [(sMM, mm_t)] + (fw("rs") if oc == 0 else []),
                          lambda s, mb=mb, oc=oc: s.activation(
                              out=rst[:, oc * CH:(oc + 1) * CH], in_=pb[mb][:, :],
                              func=AF.Sigmoid),
                          ("drain", sAC))
                    bfree[mb] = (sAC, sAC.inc())
                    rstoks.append((sAC, sAC.n))
            proj_done[nm] = (sAC.n, sDV.n)
            free["mixb"] = (sMM, sMM.n)
            mix_war = mtoks
        ek_done, v_done, rs_done = proj_done["k"][0], proj_done["v"][1], proj_done["r"][0]

        _step(p, "vector", [(sDV, xn_done)],
              lambda v, c=c: v.tensor_copy(
                  xcar[:].rearrange("p (j o) -> p j o", o=1),
                  nap(c)[:, :, CH - 1:CH]),
              ("drain", sDV))
        xcar_done = sDV.inc()

        scantok = []
        for dc in range(DC):
            def v_scan1(v, dc=dc, batch_start=batch_start):
                ekd = ekt[:, dc * CH:(dc + 1) * CH]
                vd = vt_[:, dc * CH:(dc + 1) * CH]
                if batch_start:
                    v.tensor_tensor(ekt[:, dc * CH:dc * CH + 1],
                                    ekt[:, dc * CH:dc * CH + 1],
                                    r.etfc[:, dc:dc + 1], ALU.mult)
                    v.drain()
                v.tensor_tensor(vd, ekd, vd, ALU.mult)
                v.drain()
                ini_a = 0.0 if batch_start else carryA[:, dc:dc + 1]
                ini_b = 0.0 if batch_start else carryB[:, dc:dc + 1]
                ewb = bass.AP(r.ewc, dc, [[DC, 128], [0, CH]])
                v.tensor_tensor_scan(At[:, dc * CH:(dc + 1) * CH], ewb, vd, ini_a,
                                     ALU.mult, ALU.add)
                v.tensor_tensor_scan(Bt[:, dc * CH:(dc + 1) * CH], ewb, ekd, ini_b,
                                     ALU.mult, ALU.add)
                v.drain()
            wl = [ektoks[dc], vtoks[dc]] + ((fw("A") + fw("B")) if dc == 0 else [])
            _step(p, "vector", wl, v_scan1, ("drain", sDV))
            scantok.append(sDV.inc())

        def v_car(v):
            v.tensor_copy(carryA[:].rearrange("p (j o) -> p j o", o=1),
                          At[:].rearrange("p (j t) -> p j t", j=DC)[:, :, CH - 1:CH])
            v.tensor_copy(carryB[:].rearrange("p (j o) -> p j o", o=1),
                          Bt[:].rearrange("p (j t) -> p j t", j=DC)[:, :, CH - 1:CH])
        _step(p, "vector", [], v_car, ("drain", sDV))
        scan_done = sDV.inc()

        wkvtok = []
        for dc in range(DC):
            b2_ = dc % 2

            def v_recip1(v, dc=dc, b2_=b2_):
                v.tensor_scalar(evsc[b2_][:], Bt[:, dc * CH:(dc + 1) * CH], 1e-8, None,
                                ALU.add)
            _step(p, "vector", [(sDV, scantok[dc])] + fw(f"ev{b2_}"), v_recip1,
                  ("drain", sDV))
            sDV.inc()

            def sc_recip(s, b2_=b2_):
                s.activation(out=evsc[b2_][:], in_=evsc[b2_][:], func=AF.Ln)
                s.drain()
                s.activation(out=evsc[b2_][:], in_=evsc[b2_][:], func=AF.Exp, scale=-1.0)
            _step(p, "scalar", [(sDV, sDV.n)], sc_recip, ("drain", sAC))
            ac_t = sAC.inc()

            def v_wkv(v, dc=dc, b2_=b2_):
                v.tensor_tensor(vt_[:, dc * CH:(dc + 1) * CH],
                                At[:, dc * CH:(dc + 1) * CH], evsc[b2_][:], ALU.mult)
                v.drain()
                v.tensor_tensor(rst[:, dc * CH:(dc + 1) * CH],
                                rst[:, dc * CH:(dc + 1) * CH],
                                vt_[:, dc * CH:(dc + 1) * CH], ALU.mult)
            _step(p, "vector", [(sAC, ac_t), rstoks[dc]], v_wkv, ("drain", sDV))
            free[f"ev{b2_}"] = (sDV, sDV.inc())
            wkvtok.append((sDV, sDV.n))
        rwkv_done = sDV.n
        free["A"] = (sDV, rwkv_done)
        free["B"] = (sDV, rwkv_done)
        free["ek"] = (sDV, rwkv_done)
        free["v"] = (sDV, rwkv_done)

        for oc in range(DC):
            buf = oc % 2
            _step(p, "sync", fw(f"wts{buf}") + [(sWS[buf], sWS[buf].n)],
                  stream_w(buf, r.woB, oc, None), ("then", sWS[buf]), inck=16)
            ws_t = sWS[buf].inc(64)
            mb = 2 + buf
            for dc in range(DC):
                wl = [(sWS[buf], ws_t)] \
                    + ([wkvtok[dc]] if oc == 0 else []) \
                    + (bw(mb) if dc == 0 else [])
                _step(p, "tensor", wl,
                      lambda t, mb=mb, buf=buf, dc=dc: t.matmul(
                          pb[mb][:, :], wts[buf][:, dc * 128:(dc + 1) * 128],
                          rst[:, dc * CH:(dc + 1) * CH],
                          start=(dc == 0), stop=(dc == DC - 1)),
                      ("then", sMM))
                mm_t = sMM.inc()
            free[f"wts{buf}"] = (sMM, mm_t)
            _step(p, "vector", [(sMM, mm_t), (sDV, xcar_done)],
                  lambda v, mb=mb, c=c, oc=oc: v.tensor_tensor(
                      nap(c, oc), nap(c, oc), pb[mb][:, :], ALU.add),
                  ("drain", sDV))
            bfree[mb] = (sDV, sDV.inc())
        free["rs"] = (sMM, sMM.n)
        xin_done = sDV.n

        if c + 1 < NCH:
            p2a = emit_p2a(c + 1)

        prev_ac = sAC.n
        rstd_ac = _chunk_stats(r, p, pb, sMM, sDV, sAC, rw, lambda dc, c=c: nap(c, dc),
                               sqpc, sqst, MEAN, M2, VAR, LNV, RSTD, prev_ac, bw, bfree,
                               src_ready=(sDV, xin_done))
        _step(p, "vector", [(sAC, rstd_ac)],
              lambda v: v.tensor_tensor(rw(M2), rw(MEAN), rw(RSTD), ALU.mult),
              ("drain", sDV))
        mr_done = sDV.inc()
        _step(p, "tensor", bw(0) + bw(1) + [(sDV, mr_done)],
              lambda t: (t.matmul(pb[0][:, :], r.ones_f_row1[:], rw(RSTD),
                                  start=True, stop=True).then_inc(sMM.h, 1),
                         t.matmul(pb[1][:, :], r.ones_f_row1[:], rw(M2),
                                  start=True, stop=True))[1],
              ("then", sMM))
        sMM.inc(2)
        bc_done = sMM.n
        _step(p, "vector", [(sMM, bc_done)] + fw("yn") + fw("mixb"),
              lambda v, c=c: v.tensor_tensor(
                  ynt[:].rearrange("p (j t) -> p j t", j=DC), nap(c), pbc(pb[0]),
                  ALU.mult),
              ("drain", sDV))
        sDV.inc()
        _step(p, "vector", [],
              lambda v: v.tensor_tensor(
                  ynt[:].rearrange("p (j t) -> p j t", j=DC),
                  ynt[:].rearrange("p (j t) -> p j t", j=DC), pbc(pb[1]), ALU.subtract),
              ("drain", sDV))
        dv_t = sDV.inc()
        bfree[0] = (sDV, dv_t)
        bfree[1] = (sDV, dv_t)

        def v_gb2(v):
            yv = ynt[:].rearrange("p (j t) -> p j t", j=DC)
            v.tensor_tensor(yv, yv, bass.AP(r.g2lc, 0, [[DC, 128], [1, DC], [0, CH]]),
                            ALU.mult)
            v.drain()
            v.tensor_tensor(yv, yv, bass.AP(r.b2lc, 0, [[DC, 128], [1, DC], [0, CH]]),
                            ALU.add)
        _step(p, "vector", [], v_gb2, ("drain", sDV))
        yn_done = sDV.inc()

        def v_ym(v, c=c, batch_start=batch_start):
            yv = ynt[:].rearrange("p (j t) -> p j t", j=DC)
            mv_ = ymt[:].rearrange("p (j t) -> p j t", j=DC)
            v.tensor_tensor(mv_[:, :, 1:CH], yv[:, :, 1:CH], yv[:, :, 0:CH - 1],
                            ALU.add)
            v.drain()
            if batch_start:
                v.tensor_copy(mv_[:, :, 0:1], yv[:, :, 0:1])
            else:
                v.tensor_tensor(mv_[:, :, 0:1], yv[:, :, 0:1],
                                ycar[:].rearrange("p (j o) -> p j o", o=1), ALU.add)
            v.drain()
            v.tensor_copy(ycar[:].rearrange("p (j o) -> p j o", o=1),
                          yv[:, :, CH - 1:CH])
        _step(p, "vector", [(sDV, yn_done)] + fw("ym") + fw("A"), v_ym, ("drain", sDV))
        ym_done = sDV.inc()

        for oc in range(DC):
            buf = oc % 2
            _step(p, "sync", fw(f"wts{buf}") + [(sWS[buf], sWS[buf].n)],
                  stream_w(buf, r.wfrB, oc, None), ("then", sWS[buf]), inck=16)
            ws_t = sWS[buf].inc(64)
            mb = 2 + buf
            for dc in range(DC):
                wl = [(sWS[buf], ws_t), (sDV, yn_done)] + (bw(mb) if dc == 0 else [])
                _step(p, "tensor", wl,
                      lambda t, mb=mb, buf=buf, dc=dc: t.matmul(
                          pb[mb][:, :], wts[buf][:, dc * 128:(dc + 1) * 128],
                          ynt[:, dc * CH:(dc + 1) * CH],
                          start=(dc == 0), stop=(dc == DC - 1)),
                      ("then", sMM))
                mm_t = sMM.inc()
            free[f"wts{buf}"] = (sMM, mm_t)
            _step(p, "scalar", [(sMM, mm_t), (sDV, rwkv_done)] + (fw("rs2") if oc == 0 else []),
                  lambda s, mb=mb, oc=oc: s.activation(
                      out=rs2[:, oc * CH:(oc + 1) * CH], in_=pb[mb][:, :],
                      func=AF.Sigmoid),
                  ("drain", sAC))
            bfree[mb] = (sAC, sAC.inc())
        rr_done = sAC.n
        free["yn"] = (sMM, sMM.n)

        free["ym"] = None
        for sub in range(CH // 256):
            toff = sub * 256
            kk_ready = {}
            for fs in range(FC + 1):
                if fs < FC:
                    buf = fs % 2
                    _step(p, "sync", fw(f"wts{buf}") + [(sWS[buf], sWS[buf].n)],
                          lambda s, fs=fs, buf=buf: _slab(
                              s, sWS[buf].h, wfks[buf][:], r.wfkB[fs]),
                          ("then", sWS[buf]), inck=16)
                    ws_t = sWS[buf].inc(64)
                    mb = 2 + buf
                    for dc in range(DC):
                        wl = [(sWS[buf], ws_t), (sDV, ym_done)] + (bw(mb) if dc == 0 else [])
                        _step(p, "tensor", wl,
                              lambda t, mb=mb, buf=buf, dc=dc, toff=toff: t.matmul(
                                  pb[mb][:, 0:256], wfks[buf][:, dc * 128:(dc + 1) * 128],
                                  ymt[:, dc * CH + toff:dc * CH + toff + 256],
                                  start=(dc == 0), stop=(dc == DC - 1)),
                              ("then", sMM))
                        mm_t = sMM.inc()
                    free[f"wts{buf}"] = (sMM, mm_t)

                    def v_relu(v, mb=mb, buf=buf):
                        v.tensor_relu(evsc[buf][:, 0:256], pb[mb][:, 0:256])
                    _step(p, "vector", [(sMM, mm_t)] + fw(f"ev{buf}"), v_relu,
                          ("drain", sDV))
                    bfree[mb] = (sDV, sDV.inc())
                    _step(p, "scalar", [(sDV, sDV.n)] + fw(f"kks{buf}"),
                          lambda s, buf=buf: s.activation(
                              out=kks[buf], in_=evsc[buf][:, 0:256],
                              func=AF.Square),
                          ("drain", sAC))
                    kk_ready[fs] = sAC.inc()
                    free[f"ev{buf}"] = (sAC, kk_ready[fs])
                if fs >= 1:
                    f2 = fs - 1
                    buf = f2 % 2
                    _step(p, "sync", fw(f"wfvs{buf}") + fw("rs") + [(sWV[buf], sWV[buf].n)],
                          lambda s, f2=f2, buf=buf: s.dma_start(
                              out=wfvs[buf],
                              in_=r.wfvB[f2 * 128:(f2 + 1) * 128, :]),
                          ("then", sWV[buf]), inck=16)
                    od_t = sWV[buf].inc(16)
                    for oc in range(DC):
                        wl = [(sAC, kk_ready[f2]), (sWV[buf], od_t)]
                        if f2 == 0 and oc == 0:
                            wl += fw("outst")
                        _step(p, "tensor", wl,
                              lambda t, f2=f2, buf=buf, oc=oc: t.matmul(
                                  pso[:, oc * 256:(oc + 1) * 256],
                                  wfvs[buf][:, oc * 128:(oc + 1) * 128], kks[buf],
                                  start=(f2 == 0), stop=(f2 == FC - 1),
                                  skip_group_check=True),
                              ("then", sMM))
                        mm_t2 = sMM.inc()
                    free[f"kks{buf}"] = (sMM, mm_t2)
                    free[f"wfvs{buf}"] = (sMM, mm_t2)
            cm_done = sMM.n

            def v_out(v, c=c, toff=toff):
                ov = outst[:].rearrange("p (j t) -> p j t", j=DC)
                xv = xap(c)[:, :, toff:toff + 256]
                nv = nap(c)[:, :, toff:toff + 256]
                rv = rs2[:].rearrange("p (j t) -> p j t", j=DC)[:, :, toff:toff + 256]
                pv = pso[:].rearrange("p (j t) -> p j t", j=DC)
                v.tensor_tensor(ov, pv, rv, ALU.mult)
                v.drain()
                v.tensor_tensor(ov, ov, xv, ALU.add)
                v.drain()
                v.tensor_tensor(ov, ov, nv, ALU.add)
            _step(p, "vector", [(sMM, cm_done), (sAC, rr_done), (sDV, xin_done)]
                  + fw("outst"), v_out, ("drain", sDV))
            out_done = sDV.inc()

            for tb in range(2):
                buf = tb % 2
                tok0 = c * CH + toff + tb * 128
                b_ = tok0 // SEQ
                trow = tok0 % SEQ
                for half in range(2):
                    _step(p, "tensor", [(sDV, out_done)] + bw(0),
                          lambda t, tb=tb, half=half: [t.matmul(
                              pb[0][:, j * 128:(j + 1) * 128],
                              outst[:, (half * 4 + j) * 256 + tb * 128:
                                    (half * 4 + j) * 256 + tb * 128 + 128],
                              r.idf[:], is_transpose=True, start=True,
                              stop=True).then_inc(sMM.h, 1) for j in range(4)][-1])
                    sMM.inc(4)
                    tr_t = sMM.n
                    _step(p, "vector", [(sMM, tr_t)] + (fw(f"tok{buf}") if half == 0 else []),
                          lambda v, buf=buf, half=half: v.tensor_copy(
                              tokst[buf][:, half * 512:(half + 1) * 512], pb[0][:, 0:512]),
                          ("drain", sDV))
                    bfree[0] = (sDV, sDV.inc())
                tok_done = sDV.n
                _step(p, "sync", [(sDV, tok_done), (sOD[buf], sOD[buf].n)],
                      lambda s, buf=buf, b_=b_, trow=trow: s.dma_start(
                          out=out[b_, trow:trow + 128, :], in_=tokst[buf][:]),
                      ("then", sOD[buf]), inck=16)
                free[f"tok{buf}"] = (sOD[buf], sOD[buf].inc(16))
            free["outst"] = (sMM, sMM.n)
            free["ek"] = (sMM, sMM.n)

        free["ym"] = (sMM, sMM.n)
        free["rs"] = (sMM, sMM.n)
        free["rs2"] = (sDV, sDV.n)
        free["mixb"] = [(sOD[0], sOD[0].n), (sOD[1], sOD[1].n)]

    def s_fin(s, f0=sOD[0].n, f1=sOD[1].n):
        s.wait_ge(sOD[0].h, f0)
        s.wait_ge(sOD[1].h, f1)
    p.on("sync", s_fin)



def emit_store_out(r, p, out):
    nc = r.nc
    ost = [r.sb_st(f"ost{i}", [128, D], F32) for i in range(2)]
    srcf = r.sb_st("srcf", [128, D], F32)
    ptf = r.st_st.enter_context(nc.psum_tensor("ptf_st", [128, 512], F32))
    s_c = r.sem("s_c")
    s_t2 = r.sem("s_t2")
    s_e2 = r.sem("s_e2")
    s_o2 = [r.sem("s_o2a"), r.sem("s_o2b")]

    ntiles = T // 128
    for i in range(ntiles):
        buf = i % 2
        b = (i * 128) // SEQ
        trow = (i * 128) % SEQ

        def v_gather(v, i=i, buf=buf, need=(16 * (i // 2)) if i >= 2 else 0):
            if i >= 2:
                v.wait_ge(s_o2[buf].h, need)
            src = bass.AP(r.x_fm, i * 128, [[DC * T, 128], [T, DC], [1, 128]])
            v.tensor_copy(srcf[:].rearrange("p (j c) -> p j c", j=DC), src)
            v.drain()
            v.sem_inc(s_c.h, 1)
        p.on("vector", v_gather)
        c_t = s_c.inc()

        for half in range(2):
            def t_tr(t, c_t=c_t, half=half, i=i):
                t.wait_ge(s_c.h, c_t)
                if i > 0 or half > 0:
                    t.wait_ge(s_e2.h, 2 * i + half)
                for j in range(4):
                    jj = half * 4 + j
                    t.matmul(ptf[:, j * 128:(j + 1) * 128],
                             srcf[:, jj * 128:(jj + 1) * 128], r.idf[:],
                             is_transpose=True, start=True, stop=True).then_inc(s_t2.h, 1)
            p.on("tensor", t_tr)
            t2_t = s_t2.inc(4)

            def v_ev(v, t2_t=t2_t, half=half, buf=buf):
                v.wait_ge(s_t2.h, t2_t)
                v.tensor_copy(ost[buf][:, half * 512:(half + 1) * 512], ptf[:, 0:512])
                v.drain()
                v.sem_inc(s_e2.h, 1)
            p.on("vector", v_ev)
            s_e2.inc()
        e2_t = s_e2.n

        def s_out(s, buf=buf, b=b, trow=trow, e2_t=e2_t):
            s.wait_ge(s_e2.h, e2_t)
            s.dma_start(out=out[b, trow:trow + 128, :], in_=ost[buf][:]).then_inc(s_o2[buf].h, 16)
        p.on("sync", s_out)
        s_o2[buf].inc(16)

    fin0, fin1 = s_o2[0].n, s_o2[1].n

    def s_fin(s, fin0=fin0, fin1=fin1):
        s.wait_ge(s_o2[0].h, fin0)
        s.wait_ge(s_o2[1].h, fin1)
    p.on("sync", s_fin)


def kernel(**inputs):
    xs = np.ascontiguousarray(inputs["L0_attn_x"])
    nc = build_nc()
    in_maps = []
    for c in range(8):
        m = {"x": xs[2 * c:2 * c + 2]}
        for k, v in inputs.items():
            if k == "L0_attn_x":
                continue
            a = np.ascontiguousarray(np.asarray(v))
            if k == "L1_ffn_act_scale":
                a = a.reshape(1)
            m[k] = a
        in_maps.append(m)
    res = run_bass_kernel_spmd(nc, in_maps, core_ids=list(range(8)))
    return np.concatenate([r_["out"] for r_ in res.results], axis=0)


# version 23
# speedup vs baseline: 1.0126x; 1.0126x over previous
import contextlib
import math
import numpy as np

import concourse.bass as bass
import concourse.mybir as mybir
from concourse.bass_utils import run_bass_kernel_spmd

F32 = mybir.dt.float32
BF16 = mybir.dt.bfloat16
I32 = mybir.dt.int32
AF = mybir.ActivationFunctionType
ALU = mybir.AluOpType

D = 1024
NB = 2
SEQ = 2048
T = NB * SEQ
CH = 512
NCH = T // CH
DC = D // 128
FF = 4096
FC = FF // 128
EPS = 1e-5
LOG127 = math.log(127.0)
DEV_PREP = False


class Sem:
    def __init__(self, handle):
        self.h = handle
        self.n = 0

    def inc(self, k=1):
        self.n += k
        return self.n


def _slab(s, sem_h, dst, src, nsp=4):
    stp = 128 // nsp
    last = None
    for i in range(nsp):
        last = s.dma_start(out=dst[i * stp:(i + 1) * stp, :],
                           in_=src[i * stp:(i + 1) * stp, :])
        if i < nsp - 1:
            last.then_inc(sem_h, 16)
    return last


class Prog:
    ENGINES = ("sync", "vector", "scalar", "tensor", "gpsimd")

    def __init__(self):
        self.steps = {e: [] for e in self.ENGINES}

    def on(self, eng, fn):
        self.steps[eng].append(fn)

    def run(self, block):
        prog = self

        @block.sync
        def _(e):
            for fn in prog.steps["sync"]:
                fn(e)

        @block.vector
        def _(e):
            for fn in prog.steps["vector"]:
                fn(e)

        @block.scalar
        def _(e):
            for fn in prog.steps["scalar"]:
                fn(e)

        @block.tensor
        def _(e):
            for fn in prog.steps["tensor"]:
                fn(e)

        @block.gpsimd
        def _(e):
            for fn in prog.steps["gpsimd"]:
                fn(e)


class R:
    pass


def build_nc():
    nc = bass.Bass("TRN2", target_bir_lowering=False, debug=False)
    r = R()
    r.nc = nc

    x_in = nc.dram_tensor("x", [NB, SEQ, D], F32, kind="ExternalInput").ap()
    out = nc.dram_tensor("out", [NB, SEQ, D], F32, kind="ExternalOutput").ap()
    W = {}
    for name, shape in [
        ("L0_attn_wq", [D, D]), ("L0_attn_bq", [D]), ("L0_attn_wu", [D, D]),
        ("L0_attn_bu", [D]), ("L0_ffn_w1", [FF, D]), ("L0_ffn_b1", [FF]),
        ("L0_ffn_w2", [D, FF]), ("L0_ffn_b2", [D]),
        ("L1_attn_ln1_g", [D]), ("L1_attn_ln1_b", [D]),
        ("L1_attn_Wk", [D, D]), ("L1_attn_Wv", [D, D]), ("L1_attn_Wr", [D, D]),
        ("L1_attn_Wo", [D, D]), ("L1_attn_time_decay", [D]),
        ("L1_attn_time_first", [D]), ("L1_attn_mix_k", [D]),
        ("L1_attn_mix_v", [D]), ("L1_attn_mix_r", [D]),
        ("L1_attn_ln2_g", [D]), ("L1_attn_ln2_b", [D]),
        ("L1_attn_Wfk", [D, FF]), ("L1_attn_Wfv", [FF, D]), ("L1_attn_Wfr", [D, D]),
        ("L1_ffn_bias", [D]), ("L1_ffn_act_scale", [1]),
    ]:
        W[name] = nc.dram_tensor(name, shape, F32, kind="ExternalInput").ap()
    W["L1_ffn_ternary_weight"] = nc.dram_tensor(
        "L1_ffn_ternary_weight", [D, D], I32, kind="ExternalInput").ap()
    r.W = W

    _ck = "ExternalOutput" if DEV_PREP else "Internal"
    r.w1T = nc.dram_tensor("w1T_c", [FC, 128, DC * 128], BF16, kind=_ck).ap()
    r.w2T = nc.dram_tensor("w2T_c", [DC, 128, FC * 128], BF16, kind=_ck).ap()
    r.wkB = nc.dram_tensor("wk_c", [DC, 128, DC * 128], BF16, kind=_ck).ap()
    r.wvB = nc.dram_tensor("wv_c", [DC, 128, DC * 128], BF16, kind=_ck).ap()
    r.wrB = nc.dram_tensor("wr_c", [DC, 128, DC * 128], BF16, kind=_ck).ap()
    r.woB = nc.dram_tensor("wo_c", [DC, 128, DC * 128], BF16, kind=_ck).ap()
    r.wfkB = nc.dram_tensor("wfk_c", [FC, 128, DC * 128], BF16, kind=_ck).ap()
    r.wfvB = nc.dram_tensor("wfv_c", [FF, D], BF16, kind=_ck).ap()
    r.wfrB = nc.dram_tensor("wfr_c", [DC, 128, DC * 128], BF16, kind=_ck).ap()
    r.wtT = nc.dram_tensor("wtT_c", [DC, 128, DC * 128], BF16, kind=_ck).ap()
    r.wqT = nc.dram_tensor("wqT_c", [DC, 128, DC * 128], BF16, kind=_ck).ap()
    r.wuT = nc.dram_tensor("wuT_c", [DC, 128, DC * 128], BF16, kind=_ck).ap()

    st = contextlib.ExitStack()
    nc._kernel_stack = st

    def sb(name, shape, dtype):
        return st.enter_context(nc.sbuf_tensor(name, shape, dtype))

    def mksem(name):
        return Sem(st.enter_context(nc.semaphore(name)))

    r.sb, r.sem, r.st = sb, mksem, st

    r.x_fm = sb("x_fm", [128, DC * T], BF16)
    r.idb = sb("idb", [128, 128], BF16)
    r.idf = sb("idf", [128, 128], F32)
    r.swapI = sb("swapI", [128, 128], BF16)
    r.ones_f_row1 = sb("ones_f_row1", [1, 128], F32)
    r.ones_f_col = sb("ones_f_col", [128, 1], F32)
    r.ones_bf_col = sb("ones_bf_col", [128, 1], BF16)
    r.epsc = sb("epsc", [128, 1], F32)
    r.l127 = sb("l127", [128, 1], F32)
    r.bqc = sb("bqc", [128, DC], F32)
    r.buc = sb("buc", [128, DC], F32)
    r.b1c = sb("b1c", [128, FC], F32)
    r.b2c = sb("b2c", [128, DC], F32)
    r.corr1c = sb("corr1c", [128, FC], F32)
    r.corr2c = sb("corr2c", [128, DC], F32)
    r.corrL1c = sb("corrL1c", [128, DC], F32)
    r.mw1 = sb("mw1", [1, 1], F32)
    r.mw2 = sb("mw2", [1, 1], F32)
    r.s1w_col = sb("s1w_col", [128, 1], F32)
    r.s2w_col = sb("s2w_col", [128, 1], F32)
    for nm in ("g1c", "b1lc", "mkc", "mvc", "mrc", "omkc", "omvc", "omrc",
               "tdc", "tfc", "ewc", "etfc", "g2lc", "b2lc", "bL1c"):
        setattr(r, nm, sb(nm, [128, DC], F32))
    r.s_col = sb("s_col", [128, 1], F32)
    r.sinv_col = sb("sinv_col", [128, 1], F32)
    r.deltac = sb("deltac", [128, 1], I32)
    r.offc = sb("offc", [128, 1], F32)
    r.offc64 = sb("offc64", [128, 1], F32)

    stackA = contextlib.ExitStack()
    r.sb_phase = lambda name, shape, dtype: stackA.enter_context(
        nc.sbuf_tensor(name, shape, dtype))
    r.st_phase = stackA
    with nc.Block() as blk:
        p = Prog()
        emit_consts(r, p)
        emit_load_x(r, p, x_in)
        emit_prep(r, p)
        p.run(blk)
    stackA.close()

    stackB = contextlib.ExitStack()
    psb = {"sb": lambda name, shape, dtype: stackB.enter_context(
        nc.sbuf_tensor(name, shape, dtype)), "st": stackB}
    with nc.Block() as blk:
        p = Prog()
        emit_l0(r, p, psb)
        p.run(blk)
    stackB.close()

    r.xn_buf = st.enter_context(nc.sbuf_tensor("xn_buf", [128, DC * T], BF16))
    stackC = contextlib.ExitStack()
    psc = {"sb": lambda name, shape, dtype: stackC.enter_context(
        nc.sbuf_tensor(name, shape, dtype)), "st": stackC}
    with nc.Block() as blk:
        p = Prog()
        emit_l1(r, p, psc, out)
        p.run(blk)
    stackC.close()

    nc.finalize()
    return nc



def emit_consts(r, p):
    nc = r.nc
    iop = r.sb_phase("iop_t", [128, 128], F32)
    iof = r.sb_phase("iof_t", [128, 128], F32)
    iot = r.sb_phase("iot_t", [128, 128], F32)
    s_id = r.sem("s_id")
    r.s_id = s_id

    def g_prep(g):
        g.iota(iop[:], [[0, 128]], channel_multiplier=1, allow_small_or_imprecise_dtypes=True)
        g.iota(iof[:], [[1, 128]], channel_multiplier=0, allow_small_or_imprecise_dtypes=True)
        g.memset(r.ones_f_row1[:], 1.0)
        g.memset(r.ones_f_col[:], 1.0)
        g.memset(r.ones_bf_col[:], 1.0)
        g.memset(r.epsc[:], EPS)
        g.memset(r.l127[:], LOG127)
        g.drain()
        g.sem_inc(s_id.h, 1)
    p.on("gpsimd", g_prep)
    s_id.inc()

    def v_ident(v, tgt=s_id.n):
        v.wait_ge(s_id.h, tgt)
        v.tensor_tensor(r.idb[:], iop[:], iof[:], ALU.is_equal)
        v.drain()
        v.tensor_copy(r.idf[:], r.idb[:])
        v.drain()
        v.tensor_scalar(iot[:], iop[:], 63.5, None, ALU.is_gt)
        v.drain()
        v.scalar_tensor_tensor(iop[:], iot[:], -128.0, iop[:], ALU.mult, ALU.add)
        v.drain()
        v.tensor_scalar(iop[:], iop[:], 64.0, None, ALU.add)
        v.drain()
        v.tensor_tensor(r.swapI[:], iop[:], iof[:], ALU.is_equal)
        v.drain()
        v.memset(r.offc[:], 0.75)
        v.drain()
        v.tensor_scalar(r.deltac[:], r.offc[:], 0.0, None, ALU.add)
        v.drain()
        v.tensor_scalar(r.offc[:], r.deltac[:], -0.5, 128.5, ALU.mult, ALU.add)
        v.tensor_scalar(r.offc64[:], r.deltac[:], -0.5, 64.5, ALU.mult, ALU.add)
        v.drain()
        v.sem_inc(s_id.h, 1)
    p.on("vector", v_ident)
    s_id.inc()
    r.consts_done = s_id.n



def emit_load_x(r, p, x_in):
    nc = r.nc
    xst = [r.sb_phase(f"xst{i}", [128, D], BF16) for i in range(2)]
    pt = r.st_phase.enter_context(nc.psum_tensor("pt_ld", [128, D], BF16))
    s_ld = [r.sem("s_ld0"), r.sem("s_ld1")]
    s_tr = r.sem("s_tr")
    s_ev = r.sem("s_ev")

    ntiles = T // 128
    for i in range(ntiles):
        buf = i % 2
        b = (i * 128) // SEQ
        trow = (i * 128) % SEQ

        def g_load(g, buf=buf, b=b, trow=trow, need=(8 * (i - 1)) if i >= 2 else 0):
            if need:
                g.wait_ge(s_tr.h, need)
            src = x_in[b, trow:trow + 128, :]
            for sp in range(4):
                g.dma_start(out=xst[buf][:, sp * 256:(sp + 1) * 256],
                            in_=src[:, sp * 256:(sp + 1) * 256]).then_inc(s_ld[buf].h, 16)
        p.on("gpsimd", g_load)
        ld_t = s_ld[buf].inc(64)

        def t_tr(t, buf=buf, ld_t=ld_t, i=i):
            t.wait_ge(s_ld[buf].h, ld_t)
            if i == 0:
                t.wait_ge(r.s_id.h, r.consts_done)
            if i > 0:
                t.wait_ge(s_ev.h, i)
            for j in range(DC):
                t.matmul(pt[:, j * 128:(j + 1) * 128], xst[buf][:, j * 128:(j + 1) * 128],
                         r.idb[:], is_transpose=True,
                         start=True, stop=True).then_inc(s_tr.h, 1)
        p.on("tensor", t_tr)
        s_tr.inc(8)
        tr_t = s_tr.n

        def v_ev(v, i=i, tr_t=tr_t):
            v.wait_ge(s_tr.h, tr_t)
            dst = bass.AP(r.x_fm, i * 128, [[DC * T, 128], [T, DC], [1, 128]])
            v.tensor_copy(dst, pt[:].rearrange("p (j c) -> p j c", j=DC)).then_inc(s_ev.h, 1)
        p.on("vector", v_ev)
        s_ev.inc()
    r.s_xload = s_ev
    r.xload_done = s_ev.n



def emit_prep(r, p):
    nc = r.nc
    st = r.st
    W = r.W
    wst = [r.sb_phase(f"wst{i}", [128, FF], F32) for i in range(2)] \
        + [r.sb_phase(f"wstS{i}", [128, D], F32) for i in range(2)]
    wsb = [r.sb_phase(f"wsb{i}", [128, FF], BF16) for i in range(2)] \
        + [r.sb_phase(f"wsbS{i}", [128, D], BF16) for i in range(2)]
    wib = r.sb_phase("wib", [128, FF], I32)
    wqs = r.sb_phase("wqs", [128, FF], F32)
    tpst = [[r.sb_phase(f"tpst_{b}_{j}", [128, 128], BF16) for j in range(8)]
            for b in range(4)]
    rowst = r.sb_phase("rowst", [FC, 128], F32)
    colsc = r.sb_phase("colsc", [128, 2], F32)
    rows = r.sb_phase("prep_rows", [1, 64], F32)
    pcol = r.st_phase.enter_context(nc.psum_tensor("pp_col", [128, 64], F32))
    ptp = [r.st_phase.enter_context(nc.psum_tensor(f"pp_tp{b}", [128, DC * 128], BF16))
           for b in range(4)]
    s_wl = [r.sem(f"s_wl{i}") for i in range(4)]
    s_wc = [r.sem(f"s_wc{i}") for i in range(4)]
    s_dv = r.sem("s_dv")
    s_tp = [r.sem(f"s_tp{i}") for i in range(4)]
    s_wo = [r.sem(f"s_wo{i}") for i in range(4)]
    s_go = [r.sem(f"s_go{i}") for i in range(4)]
    s_pm = r.sem("s_pm")
    s_td = r.sem("s_td")
    r.s_prep_done = s_td

    tile_no = [0]

    def tile_gate(eng, extra=0):
        k = tile_no[0]
        if k >= 2:
            eng.wait_ge(s_td.h, k - 1 - extra)

    NSP = 4

    def dma_split(eng, sem, dst, src, nsp=NSP):
        ncols = dst.shape[-1]
        step = ncols // nsp
        for i in range(nsp):
            eng.dma_start(out=dst[..., i * step:(i + 1) * step],
                          in_=src[..., i * step:(i + 1) * step]).then_inc(sem, 16)

    def emit_tp_grp(buf, srcs, wait_sem, wait_t, wo_prev=None):
        prev_tp = s_tp[buf].n
        srcs = tuple(srcs)

        def t_tp(t, buf=buf, srcs=srcs, wait_sem=wait_sem, wait_t=wait_t,
                 prev_tp=prev_tp):
            t.wait_ge(wait_sem.h, wait_t)
            if prev_tp:
                t.wait_ge(s_tp[buf].h, prev_tp)
            for j, src in enumerate(srcs):
                t.matmul(ptp[buf][:, j * 128:(j + 1) * 128], src, r.idb[:],
                         is_transpose=True, start=True, stop=True).then_inc(s_pm.h, 1)
        p.on("tensor", t_tp)
        pm_t = s_pm.inc(len(srcs))

        def v_tp(sc, buf=buf, n=len(srcs), pm_t=pm_t, wo_prev=wo_prev):
            sc.wait_ge(s_pm.h, pm_t)
            if wo_prev:
                sc.wait_ge(wo_prev[0].h, wo_prev[1])
            for j in range(n):
                sc.activation(out=tpst[buf][j][:],
                              in_=ptp[buf][:, j * 128:(j + 1) * 128], func=AF.Copy)
            sc.drain()
            sc.sem_inc(s_tp[buf].h, 16 * n)
        p.on("scalar", v_tp)
        return s_tp[buf].inc(16 * len(srcs))

    vec_jobs = [
        ("L0_attn_bq", r.bqc, DC), ("L0_attn_bu", r.buc, DC),
        ("L0_ffn_b1", r.b1c, FC), ("L0_ffn_b2", r.b2c, DC),
        ("L1_attn_ln1_g", r.g1c, DC), ("L1_attn_ln1_b", r.b1lc, DC),
        ("L1_attn_mix_k", r.mkc, DC), ("L1_attn_mix_v", r.mvc, DC),
        ("L1_attn_mix_r", r.mrc, DC), ("L1_attn_time_decay", r.tdc, DC),
        ("L1_attn_time_first", r.tfc, DC), ("L1_attn_ln2_g", r.g2lc, DC),
        ("L1_attn_ln2_b", r.b2lc, DC), ("L1_ffn_bias", r.bL1c, DC),
    ]
    for vi, (nm, dst, n) in enumerate(vec_jobs):
        def s_ldv(s, nm=nm, n=n, vi=vi, prev=s_dv.n):
            if vi > 0:
                s.wait_ge(s_dv.h, prev)
            s.dma_start(out=rowst[0:n, :],
                        in_=W[nm].rearrange("(a b) -> a b", b=128)).then_inc(s_wl[0].h, 16)
        p.on("sync", s_ldv)
        wl_t = s_wl[0].inc(16)

        def t_tp(t, wl_t=wl_t, n=n, vi=vi):
            t.wait_ge(s_wl[0].h, wl_t)
            if vi == 0:
                t.wait_ge(r.s_id.h, r.consts_done)
            t.matmul(pcol[:, 0:n], rowst[0:n, :], r.idf[0:n, 0:n],
                     is_transpose=True, start=True, stop=True).then_inc(s_pm.h, 1)
        p.on("tensor", t_tp)
        pm_t = s_pm.inc()

        def v_ev(v, pm_t=pm_t, dst=dst, n=n):
            v.wait_ge(s_pm.h, pm_t)
            v.tensor_copy(dst[:, 0:n], pcol[:, 0:n])
            v.drain()
            v.sem_inc(s_dv.h, 1)
        p.on("vector", v_ev)
        s_dv.inc()
    vdone = s_dv.n

    def sc_derived(sc, vdone=vdone):
        sc.wait_ge(s_dv.h, vdone)
        sc.activation(out=r.etfc[:], in_=r.tfc[:], func=AF.Exp)
        sc.activation(out=r.ewc[:], in_=r.tdc[:], func=AF.Exp)
        sc.drain()
        sc.activation(out=r.ewc[:], in_=r.ewc[:], func=AF.Exp, scale=-1.0)
        sc.drain()
        sc.sem_inc(s_dv.h, 1)
    p.on("scalar", sc_derived)
    s_dv.inc()

    def v_derived(v, vdone=vdone):
        v.wait_ge(s_dv.h, vdone)
        for m, om in ((r.mkc, r.omkc), (r.mvc, r.omvc), (r.mrc, r.omrc)):
            v.tensor_scalar(om[:], m[:], -1.0, 1.0, ALU.mult, ALU.add)
        v.drain()
        v.sem_inc(s_dv.h, 1)
    p.on("vector", v_derived)
    s_dv.inc()

    def s_lds(s, prev=s_dv.n):
        s.wait_ge(s_dv.h, prev)
        s.dma_start(out=rows[0:1, 0:1], in_=W["L1_ffn_act_scale"][None, 0:1]).then_inc(s_wl[1].h, 16)
    p.on("sync", s_lds)
    wl_t = s_wl[1].inc(16)

    def v_scol(v, wl_t=wl_t):
        v.wait_ge(s_wl[1].h, wl_t)
        v.tensor_scalar(rows[0:1, 1:2], rows[0:1, 0:1], 1e-5, None, ALU.max)
        v.drain()
        v.reciprocal(rows[0:1, 2:3], rows[0:1, 1:2])
        v.drain()
        v.sem_inc(s_dv.h, 1)
    p.on("vector", v_scol)
    sd_t = s_dv.inc()

    def t_scol(t, sd_t=sd_t):
        t.wait_ge(s_dv.h, sd_t)
        t.matmul(pcol[:, 8:9], r.ones_f_row1[:], rows[0:1, 1:2], start=True, stop=True)
        t.matmul(pcol[:, 9:10], r.ones_f_row1[:], rows[0:1, 2:3],
                 start=True, stop=True).then_inc(s_pm.h, 2)
    p.on("tensor", t_scol)
    pm_t = s_pm.inc(2)

    def v_scol2(v, pm_t=pm_t):
        v.wait_ge(s_pm.h, pm_t)
        v.tensor_copy(r.s_col[:], pcol[:, 8:9])
        v.tensor_copy(r.sinv_col[:], pcol[:, 9:10])
        v.drain()
        v.sem_inc(s_dv.h, 1)
    p.on("vector", v_scol2)
    s_dv.inc()

    for wi, (nm, dstT) in enumerate([("L0_attn_wq", r.wqT), ("L0_attn_wu", r.wuT)]):
        for oc in range(DC):
            buf = (wi * DC + oc) % 4
            k = tile_no[0]

            def g_cast(g, nm=nm, oc=oc, buf=buf, k=k):
                if k >= 4:
                    g.wait_ge(s_td.h, k - 3)
                dma_split(g, s_wc[buf].h, wsb[buf][:, 0:D],
                          W[nm][oc * 128:(oc + 1) * 128, :])
            p.on("gpsimd", g_cast)
            wc_t = s_wc[buf].inc(16 * NSP)

            tp_t = emit_tp_grp(buf, [wsb[buf][:, dc * 128:(dc + 1) * 128]
                                     for dc in range(DC)], s_wc[buf], wc_t)

            def s_wr(s, tp_t=tp_t, buf=buf, dstT=dstT, oc=oc):
                s.wait_ge(s_tp[buf].h, tp_t)
                for dc in range(DC):
                    s.dma_start(out=dstT[oc, :, dc * 128:(dc + 1) * 128],
                                in_=tpst[buf][dc][:]).then_inc(s_wo[buf].h, 16)
            p.on("sync", s_wr)
            wo_t = s_wo[buf].inc(16 * DC)

            def s_done(s, buf=buf, wo_t=wo_t):
                s.wait_ge(s_wo[buf].h, wo_t)
                s.sem_inc(s_td.h, 1)
            p.on("sync", s_done)
            s_td.inc()
            tile_no[0] += 1

    for wi, (nm, n_t, hbmT, corr, mwrow, swcol, rb) in enumerate([
            ("L0_ffn_w1", FC, r.w1T, r.corr1c, r.mw1, r.s1w_col, 16),
            ("L0_ffn_w2", DC, r.w2T, r.corr2c, r.mw2, r.s2w_col, 24)]):
        wsrc = W[nm]
        ncols = wsrc.shape[1]
        nel = float(n_t * 128 * ncols)
        nbuf = 4 if ncols == D else 2
        abs_toks = []
        p1_prev = s_dv.n
        for ti in range(n_t):
            buf = ti % nbuf

            def s_ld(s, ti=ti, buf=buf, wsrc=wsrc, ncols=ncols, p1_prev=p1_prev,
                     war=(abs_toks[ti - nbuf] if ti >= nbuf else None)):
                s.wait_ge(s_dv.h, p1_prev if war is None else war)
                dma_split(s, s_wl[buf].h, wst[buf][:, 0:ncols],
                          wsrc[ti * 128:(ti + 1) * 128, :])
            p.on("sync", s_ld)
            wl_t = s_wl[buf].inc(16 * NSP)

            def v_abs(v, wl_t=wl_t, ti=ti, buf=buf, ncols=ncols):
                v.wait_ge(s_wl[buf].h, wl_t)
                if ti == 0:
                    v.memset(colsc[:, 0:1], 0.0)
                    v.drain()
                absb = wib[:].bitcast(F32)
                v.tensor_scalar(absb[:, 0:ncols].bitcast(I32),
                                wst[buf][:, 0:ncols].bitcast(I32),
                                0x7FFFFFFF, None, ALU.bitwise_and)
                v.drain()
                v.reduce_sum(colsc[:, 1:2], absb[:, 0:ncols], axis=mybir.AxisListType.X)
                v.drain()
                v.tensor_tensor(colsc[:, 0:1], colsc[:, 0:1], colsc[:, 1:2], ALU.add)
                v.drain()
                v.sem_inc(s_dv.h, 1)
            p.on("vector", v_abs)
            abs_toks.append(s_dv.inc())
        absdone = s_dv.n

        def t_tot(t, absdone=absdone, rb=rb):
            t.wait_ge(s_dv.h, absdone)
            t.matmul(pcol[0:1, rb:rb + 1], r.ones_f_col[:], colsc[:, 0:1],
                     start=True, stop=True).then_inc(s_pm.h, 1)
        p.on("tensor", t_tot)
        pm_t = s_pm.inc()

        def v_scale(v, pm_t=pm_t, nel=nel, mwrow=mwrow, rb=rb):
            v.wait_ge(s_pm.h, pm_t)
            v.tensor_scalar(rows[0:1, rb + 1:rb + 2], pcol[0:1, rb:rb + 1],
                            1.0 / nel, 1e-5, ALU.mult, ALU.max)
            v.drain()
            v.tensor_copy(mwrow[0:1, 0:1], rows[0:1, rb + 1:rb + 2])
            v.reciprocal(rows[0:1, rb + 2:rb + 3], rows[0:1, rb + 1:rb + 2])
            v.drain()
            v.sem_inc(s_dv.h, 1)
        p.on("vector", v_scale)
        sc_t = s_dv.inc()

        def t_bc(t, sc_t=sc_t, rb=rb):
            t.wait_ge(s_dv.h, sc_t)
            t.matmul(pcol[:, rb + 3:rb + 4], r.ones_f_row1[:],
                     rows[0:1, rb + 2:rb + 3], start=True, stop=True).then_inc(s_pm.h, 1)
        p.on("tensor", t_bc)
        pm_t = s_pm.inc()

        def v_swcol(v, pm_t=pm_t, swcol=swcol, rb=rb):
            v.wait_ge(s_pm.h, pm_t)
            v.tensor_copy(swcol[:], pcol[:, rb + 3:rb + 4])
            v.drain()
            v.sem_inc(s_dv.h, 1)
        p.on("vector", v_swcol)
        s_dv.inc()

        pass1_done = s_dv.n
        for ti in range(n_t):
            buf = ti % nbuf
            k = tile_no[0]

            def s_ld(s, ti=ti, buf=buf, wsrc=wsrc, ncols=ncols, k=k, p1=pass1_done,
                     nbuf=nbuf):
                if ti < nbuf:
                    s.wait_ge(s_dv.h, p1)
                gn = k - (3 if nbuf == 4 else 1)
                if k >= 2 and gn >= 1:
                    s.wait_ge(s_td.h, gn)
                dma_split(s, s_wl[buf].h, wst[buf][:, 0:ncols],
                          wsrc[ti * 128:(ti + 1) * 128, :])
            p.on("sync", s_ld)
            wl_t = s_wl[buf].inc(16 * NSP)

            def v_q(v, wl_t=wl_t, buf=buf, ncols=ncols, swcol=swcol, ti=ti, corr=corr):
                v.wait_ge(s_wl[buf].h, wl_t)
                v.tensor_scalar(wqs[:, 0:ncols], wst[buf][:, 0:ncols], swcol[:, 0:1], None,
                                ALU.mult)
                v.drain()
                v.tensor_scalar(wib[:, 0:ncols], wqs[:, 0:ncols], r.offc64[:, 0:1], None,
                                ALU.add)
                v.drain()
                v.tensor_scalar(wib[:, 0:ncols], wib[:, 0:ncols], 63.0, 65.0, ALU.max, ALU.min)
                v.drain()
                v.tensor_scalar(wsb[buf][:, 0:ncols], wib[:, 0:ncols], -64.0, None, ALU.add)
                v.drain()
                v.reduce_sum(colsc[:, 1:2], wsb[buf][:, 0:ncols], axis=mybir.AxisListType.X)
                v.drain()
                v.tensor_scalar(corr[:, ti:ti + 1], colsc[:, 1:2], -128.0, None, ALU.mult)
                v.drain()
                v.sem_inc(s_dv.h, 1)
            p.on("vector", v_q)
            q_t = s_dv.inc()

            ngrp = ncols // 128 // 8
            for gi in range(ngrp):
                tp_t = emit_tp_grp(
                    buf, [wsb[buf][:, (gi * 8 + j) * 128:(gi * 8 + j + 1) * 128]
                          for j in range(8)], s_dv, q_t,
                    wo_prev=((s_go[buf], s_go[buf].n) if gi > 0 else None))

                def s_wr(g, tp_t=tp_t, buf=buf, gi=gi, hbmT=hbmT, ti=ti):
                    g.wait_ge(s_tp[buf].h, tp_t)
                    for j in range(8):
                        jj = gi * 8 + j
                        g.dma_start(out=hbmT[ti, :, jj * 128:(jj + 1) * 128],
                                    in_=tpst[buf][j][:]).then_inc(s_go[buf].h, 16)
                p.on("gpsimd", s_wr)
                s_go[buf].inc(16 * 8)
            wo_t = s_go[buf].n

            def s_done(g, buf=buf, wo_t=wo_t):
                g.wait_ge(s_go[buf].h, wo_t)
                g.sem_inc(s_td.h, 1)
            p.on("gpsimd", s_done)
            s_td.inc()
            tile_no[0] += 1

    cast_jobs = [
        ("L1_attn_Wfk", r.wfkB, 0.5),
    ]
    for nm, hbm, scale in cast_jobs:
        ntl = W[nm].shape[0] // 128
        ncols = W[nm].shape[1]
        for ti in range(ntl):
            buf = ti % 2
            k = tile_no[0]

            def g_cast(g, nm=nm, ti=ti, buf=buf, ncols=ncols, k=k):
                if k >= 2:
                    g.wait_ge(s_td.h, k - 1)
                dma_split(g, s_wc[buf].h, wsb[buf][:, 0:ncols],
                          W[nm][ti * 128:(ti + 1) * 128, :])
            p.on("gpsimd", g_cast)
            wc_t = s_wc[buf].inc(16 * NSP)
            dv_t = None
            if scale != 1.0:
                def v_s(v, wc_t=wc_t, buf=buf, ncols=ncols, scale=scale):
                    v.wait_ge(s_wc[buf].h, wc_t)
                    v.tensor_scalar(wsb[buf][:, 0:ncols], wsb[buf][:, 0:ncols], scale, None,
                                    ALU.mult)
                    v.drain()
                    v.sem_inc(s_dv.h, 1)
                p.on("vector", v_s)
                dv_t = s_dv.inc()

            def s_wr(s, wc_t=wc_t, buf=buf, ncols=ncols, hbm=hbm, ti=ti, dv_t=dv_t):
                if dv_t is None:
                    s.wait_ge(s_wc[buf].h, wc_t)
                else:
                    s.wait_ge(s_dv.h, dv_t)
                if hbm is r.wfvB:
                    dma_split(s, s_wo[buf].h, hbm[ti * 128:(ti + 1) * 128, :],
                              wsb[buf][:, 0:ncols])
                else:
                    nbl = ncols // 128
                    srcv = wsb[buf][:, 0:ncols].rearrange("q (blk f) -> q blk f", f=128)
                    stp = nbl // NSP
                    for i in range(NSP):
                        dstv = hbm[i * stp:(i + 1) * stp, :,
                                   ti * 128:(ti + 1) * 128].rearrange(
                                       "blk q f -> q blk f")
                        s.dma_start(out=dstv, in_=srcv[:, i * stp:(i + 1) * stp, :]
                                    ).then_inc(s_wo[buf].h, 16)
            p.on("sync", s_wr)
            wo_t = s_wo[buf].inc(16 * NSP)

            def s_done(s, buf=buf, wo_t=wo_t):
                s.wait_ge(s_wo[buf].h, wo_t)
                s.sem_inc(s_td.h, 1)
            p.on("sync", s_done)
            s_td.inc()
            tile_no[0] += 1

    for ti in range(DC):
        buf = ti % 2
        k = tile_no[0]

        def g_cast(g, ti=ti, buf=buf, k=k):
            if k >= 2:
                g.wait_ge(s_td.h, k - 1)
            dma_split(g, s_wc[buf].h, wsb[buf][:, 0:D],
                      r.W["L1_ffn_ternary_weight"][ti * 128:(ti + 1) * 128, :])
        p.on("gpsimd", g_cast)
        wc_t = s_wc[buf].inc(16 * NSP)

        def v_m1(v, wc_t=wc_t, buf=buf, ti=ti):
            v.wait_ge(s_wc[buf].h, wc_t)
            v.tensor_scalar(wsb[buf][:, 0:D], wsb[buf][:, 0:D], -1.0, None, ALU.add)
            v.drain()
            v.reduce_sum(colsc[:, 1:2], wsb[buf][:, 0:D], axis=mybir.AxisListType.X)
            v.drain()
            v.tensor_scalar(r.corrL1c[:, ti:ti + 1], colsc[:, 1:2], -128.0, None, ALU.mult)
            v.drain()
            v.sem_inc(s_dv.h, 1)
        p.on("vector", v_m1)
        dv_t = s_dv.inc()

        tp_t = emit_tp_grp(buf, [wsb[buf][:, j * 128:(j + 1) * 128]
                                 for j in range(DC)], s_dv, dv_t)

        def s_wr(s, tp_t=tp_t, buf=buf, ti=ti):
            s.wait_ge(s_tp[buf].h, tp_t)
            for j in range(DC):
                s.dma_start(out=r.wtT[ti, :, j * 128:(j + 1) * 128],
                            in_=tpst[buf][j][:]).then_inc(s_wo[buf].h, 16)
        p.on("sync", s_wr)
        wo_t = s_wo[buf].inc(16 * DC)

        def s_done(s, buf=buf, wo_t=wo_t):
            s.wait_ge(s_wo[buf].h, wo_t)
            s.sem_inc(s_td.h, 1)
        p.on("sync", s_done)
        s_td.inc()
        tile_no[0] += 1

    s_lc = r.sem("s_lc")

    def g_hbm_casts(g):
        for nm, hbm in [("L1_attn_Wk", r.wkB), ("L1_attn_Wv", r.wvB),
                        ("L1_attn_Wr", r.wrB), ("L1_attn_Wo", r.woB),
                        ("L1_attn_Wfr", r.wfrB)]:
            src = W[nm].rearrange("(dc q) (blk f) -> dc q blk f", q=128, f=128)
            dst = hbm.rearrange("blk q (dc f) -> dc q blk f", f=128)
            for i in range(DC):
                g.dma_start(out=dst[i], in_=src[i]).then_inc(s_lc.h, 16)
        for i in range(8):
            g.dma_start(out=r.wfvB[i * 512:(i + 1) * 512, :],
                        in_=W["L1_attn_Wfv"][i * 512:(i + 1) * 512, :]
                        ).then_inc(s_lc.h, 16)
    p.on("gpsimd", g_hbm_casts)
    r.l1_casts = (s_lc, s_lc.inc(16 * 48))

    r.prep_tiles_done = s_td.n
    r.s_dv_done = s_dv.n
    r.s_dv = s_dv



def _step(p, eng, waits, fn, inc=None, inck=1):
    def go(e, waits=tuple(waits), fn=fn, inc=inc, inck=inck):
        for s, n in waits:
            if n > 0:
                e.wait_ge(s.h, n)
        ret = fn(e)
        if inc is not None:
            mode, s = inc
            if mode == "then":
                ret.then_inc(s.h, inck)
            else:
                e.drain()
                e.sem_inc(s.h, inck)
    p.on(eng, go)


def emit_l0(r, p, psb):
    nc = r.nc
    sb = psb["sb"]
    st = psb["st"]
    xf = r.x_fm

    r.e16 = None
    r.ones_bf_row = sb("l0_ones_bf_row", [1, CH], BF16)
    r.e2 = sb("l0_e2", [128, 2], BF16)
    r.elo = sb("l0_elo", [1, 128], F32)
    r.ehi = sb("l0_ehi", [1, 128], F32)
    scr = sb("l0_scr", [128, DC * CH], F32)
    intq2 = [sb(f"l0_intq{i}", [128, CH], I32) for i in range(2)]
    sqs = [sb(f"l0_sqs{i}", [128, CH], F32) for i in range(2)]
    sq2 = [sb(f"l0_sq2{i}", [128, CH], F32) for i in range(2)]
    sqp = [sb(f"l0_sqp{i}", [128, CH], BF16) for i in range(2)]
    g2keep = sb("l0_g2k", [1, CH], F32)
    xnq = sb("l0_xnq", [128, DC * CH], BF16)
    qaf = sb("l0_qaf", [128, DC * CH], BF16)
    _qpk0 = sb("l0_qpk", [128, 2048], BF16)
    qpk = [_qpk0, _qpk0]
    vt = [sb(f"l0_vt{i}", [128, 1024], BF16) for i in range(2)]
    e_sb = [sb(f"l0_e{i}", [128, 2048], BF16) for i in range(2)]
    hbf = sb("l0_h", [128, FC * CH], BF16)
    w1s = [sb(f"l0_w1s{i}", [128, DC * 128], BF16) for i in range(2)]
    w2s = sb("l0_w2s", [128, FC * 128], BF16)
    wqs = [sb(f"l0_wqs{i}", [128, DC * 128], BF16) for i in range(2)]
    sA_sb = sb("l0_sA", [128, CH], F32)
    sB_sb = sb("l0_sB", [128, CH], F32)
    div_sb = sb("l0_div", [128, CH], F32)
    maxacc = sb("l0_maxacc", [128, CH], F32)
    mxcol = sb("l0_mxcol", [128, 4], F32)
    trst = sb("l0_trst", [16, 128], F32)
    m2acc = sb("l0_m2acc", [2, 8], F32)
    m41 = sb("l0_m41", [4, 1], F32)
    mrow2 = sb("l0_mrow2", [1, 4], F32)
    mcols = sb("l0_mcols", [128, 4], F32)
    zc4 = [sb(f"l0_zc4{i}", [128, 32], F32) for i in range(2)]
    z16 = sb("l0_z16", [128, 16], F32)
    zrow = [scr[0:1, 0:2048], scr[0:1, 2048:4096]]
    rows = sb("l0_rows", [1, 9 * CH], F32)
    nl127 = sb("l0_nl127", [128, 1], F32)

    def rw(i):
        return rows[0:1, i * CH:(i + 1) * CH]
    MEAN, M2, VAR, LNV, RSTD, R2I, G2, HMAX, C1 = 0, 1, 2, 3, 4, 5, 6, 7, 8
    U = 1
    MX = 2
    LNH = 5
    S1 = 3
    SAr = 1
    SSQ, LNV2, RSTD2, MXH2, LNM, LNHN, S2r, C2r, SBr = range(9)

    psA = st.enter_context(nc.psum_tensor("l0_psA", [128, 4 * CH], F32))
    psB = st.enter_context(nc.psum_tensor("l0_psB", [128, 4 * CH], F32))
    pb = [psA[:, 0:CH], psA[:, CH:2 * CH], psB[:, 0:CH], psB[:, CH:2 * CH],
          psA[:, 2 * CH:3 * CH], psA[:, 3 * CH:4 * CH],
          psB[:, 2 * CH:3 * CH], psB[:, 3 * CH:4 * CH]]
    pbAV = [pb[2], pb[3], pb[6], pb[7]]
    scoresF = psA[:, 0:4 * CH]
    bfree = {i: None for i in range(8)}

    def bw(i):
        return [bfree[i]] if bfree[i] else []

    sMM = r.sem("l0_mm")
    sDV = r.sem("l0_dv")
    sAC = r.sem("l0_ac")
    sWS = [r.sem("l0_ws0"), r.sem("l0_ws1")]
    sW2 = r.sem("l0_w2")
    sW2h = [r.sem("l0_w2a"), r.sem("l0_w2b")]
    sGA = r.sem("l0_ga")
    sGB = r.sem("l0_gb")
    sRD = r.sem("l0_rd")
    free = {}
    for nm in ("w1s0", "w1s1", "w2s", "wqs0", "wqs1", "qpk", "vt0", "vt1",
               "e0", "e1", "sqs0", "sqs1", "sq20", "sq21", "scr", "xnq", "qaf",
               "zc40", "zc41", "hbf", "maxacc", "trst", "sqp0", "sqp1",
               "w2h0", "w2h1"):
        free[nm] = None

    def fw(nm):
        f = free[nm]
        if f is None:
            return []
        return list(f) if isinstance(f, list) else [f]

    sBR = r.sem("l0_br")
    sBC = r.sem("l0_bc")

    def g_brows(g):
        g.memset(r.ones_bf_row[:], 1.0)
        g.memset(r.e2[:], 0.0)
        g.memset(r.elo[:], 0.0)
        g.memset(r.ehi[:], 0.0)
        g.drain()
        g.memset(r.e2[0:64, 0:1], 1.0)
        g.memset(r.e2[64:128, 1:2], 1.0)
        g.memset(r.elo[0:1, 0:64], 1.0)
        g.memset(r.ehi[0:1, 64:128], 1.0)
        g.drain()
        g.sem_inc(sBC.h, 1)
    p.on("gpsimd", g_brows)
    sBC.inc(1)
    r.bconsts_done = (sBC, sBC.n)

    def xap(c, dc=None):
        if dc is None:
            return bass.AP(xf, c * CH, [[DC * T, 128], [T, DC], [1, CH]])
        return bass.AP(xf, dc * T + c * CH, [[DC * T, 128], [1, CH]])

    def pbcast(bank, ndc=DC):
        return bank.rearrange("p (o t) -> p o t", o=1).broadcast_to((128, ndc, CH))

    ONE_CONST = [False]

    P_MEAN, P_VAR, P_LNV, P_RSTD = 0, 1, 2, 3

    def rw2(i):
        return scr[0:1, i * CH:(i + 1) * CH]

    def emit_stats(cc, g2_war):
        for dc in range(DC):
            buf = dc % 2
            _step(p, "vector", fw(f"sqp{buf}") + (fw("scr") if dc == 0 else []),
                  lambda v, cc=cc, dc=dc, buf=buf: v.tensor_tensor(
                      sqp[buf][:], xap(cc, dc), xap(cc, dc), ALU.mult),
                  ("drain", sDV))
            dv_t = sDV.inc()
            _step(p, "tensor", [(sDV, dv_t)] + (bw(1) if dc == 0 else []),
                  lambda t, buf=buf, dc=dc: t.matmul(
                      pb[1][0:1, :], r.ones_bf_col[:], sqp[buf][:],
                      start=(dc == 0), stop=(dc == DC - 1)),
                  ("then", sMM))
            free[f"sqp{buf}"] = (sMM, sMM.inc())
        s2_done = sMM.n
        for dc in range(DC):
            wl = (bw(0) + [(r.s_xload, r.xload_done)]) if dc == 0 else []
            _step(p, "tensor", wl,
                  lambda t, cc=cc, dc=dc: t.matmul(
                      pb[0][0:1, :], r.ones_bf_col[:], xap(cc, dc),
                      start=(dc == 0), stop=(dc == DC - 1)),
                  ("then", sMM))
            s1_done = sMM.inc()
        _step(p, "vector", [(sMM, s1_done)] + fw("scr"),
              lambda v: v.tensor_scalar(rw2(P_MEAN), pb[0][0:1, :], 1.0 / D,
                                        None, ALU.mult),
              ("drain", sDV))
        t_mean = sDV.inc()
        bfree[0] = (sDV, t_mean)
        _step(p, "scalar", [(sDV, t_mean)],
              lambda s: s.activation(out=rw2(P_LNV), in_=rw2(P_MEAN), func=AF.Square),
              ("drain", sAC))
        m2_t = sAC.inc()
        _step(p, "vector", [(sAC, m2_t), (sMM, s2_done)],
              lambda v: v.scalar_tensor_tensor(rw2(P_VAR), pb[1][0:1, :], 1.0 / D,
                                               rw2(P_LNV), ALU.mult, ALU.subtract),
              ("drain", sDV))
        var_t = sDV.inc()
        bfree[1] = (sDV, var_t)

        def sc_st1(s):
            s.activation(out=rw2(P_LNV), in_=rw2(P_VAR), func=AF.Ln,
                         bias=r.epsc[0:1, 0:1])
            s.drain()
            s.activation(out=rw2(P_RSTD), in_=rw2(P_LNV), func=AF.Exp, scale=-0.5)
            s.activation(out=g2keep[:], in_=rw2(P_LNV), func=AF.Exp, scale=-1.0)
        _step(p, "scalar", [(sDV, var_t)] + ([g2_war] if g2_war else []),
              sc_st1, ("drain", sAC))
        t_mr = sAC.inc()
        _step(p, "vector", [(sAC, t_mr)],
              lambda v: v.tensor_tensor(rw2(P_VAR), rw2(P_VAR), g2keep[:], ALU.mult),
              ("drain", sDV))
        u_t = sDV.inc()

        def sc_st2(s):
            s.activation(out=rw2(P_LNV), in_=rw2(P_VAR), func=AF.Ln,
                         bias=r.epsc[0:1, 0:1])
            s.drain()
            s.activation(out=g2keep[:], in_=rw2(P_LNV), func=AF.Exp, scale=-0.5)
        _step(p, "scalar", [(sDV, u_t)], sc_st2, ("drain", sAC))
        t_g2 = sAC.inc()
        return (t_mean, t_mr, t_g2)

    pst = None
    for c in range(NCH):
        if c == 0:
            _step(p, "vector", [(r.s_id, r.consts_done)],
                  lambda v: v.tensor_scalar(nl127[:], r.l127[:], -1.0, None, ALU.mult),
                  ("drain", sDV))
            sDV.inc()
            pst = emit_stats(0, None)
        t_mean, t_mr, t_g2 = pst
        _step(p, "tensor", bw(0) + bw(1) + [(sAC, t_mr), (sDV, t_mean)],
              lambda t: (t.matmul(pb[0][:, :], r.ones_f_row1[:], rw2(P_MEAN),
                                  start=True, stop=True).then_inc(sMM.h, 1),
                         t.matmul(pb[1][:, :], r.ones_f_row1[:], rw2(P_RSTD),
                                  start=True, stop=True))[1],
              ("then", sMM))
        sMM.inc(2)
        bc_done = sMM.n
        _step(p, "vector", [(sMM, bc_done), (sAC, t_g2)] + fw("scr"),
              lambda v, c=c: v.tensor_tensor(
                  scr[:].rearrange("p (j t) -> p j t", j=DC), xap(c), pbcast(pb[0]),
                  ALU.subtract),
              ("drain", sDV))
        sDV.inc()
        _step(p, "vector", [],
              lambda v: v.tensor_tensor(
                  scr[:].rearrange("p (j t) -> p j t", j=DC),
                  scr[:].rearrange("p (j t) -> p j t", j=DC), pbcast(pb[1]), ALU.mult),
              ("drain", sDV))
        xn32_done = sDV.inc()
        bfree[0] = (sDV, xn32_done)
        bfree[1] = (sDV, xn32_done)
        _step(p, "vector", fw("xnq"),
              lambda v: v.tensor_copy(xnq[:], scr[:]),
              ("drain", sDV))
        xn_done = sDV.inc()

        qev = []
        for oc in range(DC):
            buf = oc % 2
            _step(p, "sync", fw(f"wqs{buf}") + [(sWS[buf], sWS[buf].n)],
                  lambda s, oc=oc, buf=buf: _slab(
                      s, sWS[buf].h, wqs[buf][:], r.wqT[oc]),
                  ("then", sWS[buf]), inck=16)
            ws_t = sWS[buf].inc(64)
            mb = 4 + (oc % 2)
            for dc in range(DC):
                wl = ([(sWS[buf], ws_t), (sDV, xn_done)] + bw(mb)) if dc == 0 else []
                _step(p, "tensor", wl,
                      lambda t, mb=mb, buf=buf, dc=dc: t.matmul(
                          pb[mb][:, :], wqs[buf][:, dc * 128:(dc + 1) * 128],
                          xnq[:, dc * CH:(dc + 1) * CH], start=(dc == 0),
                          stop=(dc == DC - 1)),
                      ("then", sMM))
                mm_t = sMM.inc()
            free[f"wqs{buf}"] = (sMM, mm_t)
            _step(p, "vector", [(sMM, mm_t)] + (fw("qaf") if oc == 0 else []),
                  lambda v, mb=mb, oc=oc: v.tensor_scalar(
                      qaf[:, oc * CH:(oc + 1) * CH], pb[mb][:, :],
                      r.bqc[:, oc:oc + 1], None, ALU.add),
                  ("drain", sDV))
            bfree[mb] = (sDV, sDV.inc())
            qev.append(sDV.n)
        qproj_done = sDV.n

        hf32 = hbf[:].bitcast(F32)
        _step(p, "vector", fw("hbf"),
              lambda v, hf32=hf32: v.tensor_scalar(
                  hf32[:, 0:DC * CH].bitcast(I32), scr[:].bitcast(I32),
                  0x7FFFFFFF, None, ALU.bitwise_and),
              ("drain", sDV))
        sDV.inc()

        def v_tree(v, hf32=hf32):
            af = hf32[:, 0:DC * CH].rearrange("p (j t) -> p j t", j=DC)
            v.tensor_tensor(af[:, 0:4, :], af[:, 0:4, :], af[:, 4:8, :], ALU.max)
            v.drain()
            v.tensor_tensor(af[:, 0:2, :], af[:, 0:2, :], af[:, 2:4, :], ALU.max)
            v.drain()
            v.tensor_tensor(af[:, 0:1, :], af[:, 0:1, :], af[:, 1:2, :], ALU.max)
        _step(p, "vector", [], v_tree, ("drain", sDV))
        tree_done = sDV.inc()
        for j in range(4):
            _step(p, "tensor", [(sDV, tree_done)] + (bw(2) if j == 0 else []),
                  lambda t, j=j, hf32=hf32: t.matmul(
                      pb[2][:, j * 128:(j + 1) * 128],
                      hf32[:, j * 128:(j + 1) * 128], r.idf[:],
                      is_transpose=True, start=True, stop=True),
                  ("then", sMM))
            sMM.inc()
        tr_done = sMM.n
        free["hbf"] = (sMM, tr_done)

        def v_mxred(v):
            for j in range(4):
                v.reduce_max(mxcol[:, j:j + 1], pb[2][:, j * 128:(j + 1) * 128],
                             axis=mybir.AxisListType.X)
            v.drain()
        _step(p, "vector", [(sMM, tr_done)], v_mxred, ("drain", sDV))
        bfree[2] = (sDV, sDV.inc())
        _step(p, "tensor", bw(2),
              lambda t: t.matmul(pb[2][0:4, 0:128], mxcol[:, 0:4], r.idf[:],
                                 is_transpose=True, start=True, stop=True),
              ("then", sMM))
        sMM.inc()
        _step(p, "vector", [(sMM, sMM.n)] + fw("trst"),
              lambda v: v.tensor_copy(trst[0:4, :], pb[2][0:4, 0:128]),
              ("drain", sDV))
        bfree[2] = (sDV, sDV.inc())
        _step(p, "sync", [(sDV, sDV.n)],
              lambda s: s.dma_start(out=rw(MX), in_=trst[0:4, :]),
              ("then", sRD), inck=16)
        mx_done = sRD.inc(16)
        free["trst"] = (sRD, mx_done)

        def v_rows3(v):
            v.tensor_tensor(rw(HMAX), g2keep[:], rw(MX), ALU.mult)
            v.drain()
            v.tensor_scalar(rw(HMAX), rw(HMAX), 1e-5, None, ALU.max)
        _step(p, "vector", [(sRD, mx_done), (sAC, t_g2)], v_rows3, ("drain", sDV))
        sDV.inc()

        def sc_rows3(s):
            s.activation(out=rw(LNH), in_=rw(HMAX), func=AF.Ln)
            s.drain()
            s.activation(out=rw(S1), in_=rw(LNH), func=AF.Exp, scale=-1.0,
                         bias=r.l127[0:1, 0:1])
        _step(p, "scalar", [(sDV, sDV.n)], sc_rows3, ("drain", sAC))
        sAC.inc()

        def v_rows4(v):
            v.tensor_tensor(rw(C1), g2keep[:], rw(S1), ALU.mult)
            v.tensor_scalar(rw(SAr), rw(HMAX), 1.0 / 127.0, None, ALU.mult)
            v.drain()
            v.tensor_tensor(rw(SAr), rw(SAr), bass.AP(r.mw1, 0, [[1, 1], [0, CH]]),
                            ALU.mult)
        _step(p, "vector", [(sAC, sAC.n)], v_rows4, ("drain", sDV))
        rows4_done = sDV.inc()
        _step(p, "tensor", bw(0) + bw(1) + [(sDV, rows4_done)],
              lambda t: (t.matmul(pb[0][:, :], r.ones_f_row1[:], rw(C1),
                                  start=True, stop=True).then_inc(sMM.h, 1),
                         t.matmul(pb[1][:, :], r.ones_f_row1[:], rw(SAr),
                                  start=True, stop=True))[1],
              ("then", sMM))
        sMM.inc(2)
        cb_done = sMM.n
        _step(p, "vector", [(sMM, cb_done)],
              lambda v: v.tensor_copy(sA_sb[:], pb[1][:, :]),
              ("drain", sDV))
        bfree[1] = (sDV, sDV.inc())
        _step(p, "vector", [(sMM, cb_done), (sDV, qproj_done)],
              lambda v: v.tensor_tensor(
                  scr[:].rearrange("p (j t) -> p j t", j=DC),
                  scr[:].rearrange("p (j t) -> p j t", j=DC), pbcast(pb[0]), ALU.mult),
              ("drain", sDV))
        bfree[0] = (sDV, sDV.inc())
        def v_cast(v):
            for dc in range(DC):
                b2_ = dc % 2
                v.tensor_scalar(intq2[b2_][:], scr[:, dc * CH:(dc + 1) * CH],
                                r.offc[:, 0:1], None, ALU.add)
                v.drain()
                v.tensor_copy(xnq[:, dc * CH:(dc + 1) * CH], intq2[b2_][:])
                v.drain()
        _step(p, "vector", [(sMM, tr_done)], v_cast, ("drain", sDV))
        xq1_done = sDV.inc()
        free["scr"] = (sDV, xq1_done)

        _step(p, "scalar", [(sDV, qproj_done)] + fw("e0") + fw("e1")
              + fw("zc40") + fw("zc41"),
              lambda s: (s.activation(out=e_sb[0][:], in_=qaf[:, 0:2048],
                                      func=AF.Square),
                         s.drain(),
                         s.activation(out=e_sb[1][:], in_=qaf[:, 2048:4096],
                                      func=AF.Square))[-1],
              ("drain", sAC))
        sqq_done = sAC.inc()
        for oc in range(DC):
            _step(p, "tensor", [(sAC, sqq_done)] + bw(1)
                  + ([r.bconsts_done] if (c == 0 and oc == 0) else []),
                  lambda t, oc=oc: t.matmul(
                      pb[1][0:2, :], r.e2[:],
                      e_sb[oc // 4][:, (oc % 4) * 512:(oc % 4) * 512 + 512],
                      start=True, stop=True),
                  ("then", sMM))
            mm_t = sMM.inc()

            def v_m(v, oc=oc):
                for pi in range(4):
                    v.reduce_max(m2acc[0:2, 4 + pi:5 + pi],
                                 pb[1][0:2, 128 * pi:128 * pi + 128],
                                 axis=mybir.AxisListType.X)
                v.drain()
                if oc == 0:
                    v.tensor_copy(m2acc[0:2, 0:4], m2acc[0:2, 4:8])
                else:
                    v.tensor_tensor(m2acc[0:2, 0:4], m2acc[0:2, 0:4],
                                    m2acc[0:2, 4:8], ALU.max)
            _step(p, "vector", [(sMM, mm_t)], v_m, ("drain", sDV))
            bfree[1] = (sDV, sDV.inc())
        _step(p, "tensor", bw(1) + [(sDV, sDV.n)],
              lambda t: t.matmul(pb[1][0:4, 0:2], m2acc[0:2, 0:4], r.idf[0:2, 0:2],
                                 is_transpose=True, start=True, stop=True),
              ("then", sMM))
        sMM.inc()

        def v_m2(v):
            v.reduce_max(m41[0:4, 0:1], pb[1][0:4, 0:2], axis=mybir.AxisListType.X)
            v.drain()
            v.tensor_scalar(m41[0:4, 0:1], m41[0:4, 0:1], -1.0 / 32.0, None, ALU.mult)
        _step(p, "vector", [(sMM, sMM.n)], v_m2, ("drain", sDV))
        bfree[1] = (sDV, sDV.inc())
        _step(p, "tensor", bw(1) + [(sDV, sDV.n)],
              lambda t: t.matmul(pb[1][0:1, 0:4], m41[0:4, 0:1], r.idf[0:4, 0:4],
                                 is_transpose=True, start=True, stop=True),
              ("then", sMM))
        sMM.inc()
        _step(p, "vector", [(sMM, sMM.n)],
              lambda v: v.tensor_copy(mrow2[:], pb[1][0:1, 0:4]),
              ("drain", sDV))
        bfree[1] = (sDV, sDV.inc())
        _step(p, "tensor", bw(1) + [(sDV, sDV.n)],
              lambda t: t.matmul(pb[1][:, 0:4], r.ones_f_row1[:], mrow2[0:1, 0:4],
                                 start=True, stop=True),
              ("then", sMM))
        sMM.inc()
        _step(p, "vector", [(sMM, sMM.n)],
              lambda v: v.tensor_copy(mcols[:], pb[1][:, 0:4]),
              ("drain", sDV))
        bfree[1] = (sDV, sDV.inc())
        mprep_done = sDV.n
        free["e0"] = (sMM, sMM.n)
        free["e1"] = (sMM, sMM.n)

        gl_ts = []

        def emit_mm1(fc):
            buf = fc % 2
            _step(p, "sync", fw(f"w1s{buf}") + [(sWS[buf], sWS[buf].n)],
                  lambda s, fc=fc, buf=buf: _slab(
                      s, sWS[buf].h, w1s[buf][:], r.w1T[fc]),
                  ("then", sWS[buf]), inck=16)
            ws_t = sWS[buf].inc(64)
            mb = 4 + buf
            for dc in range(DC):
                wl = ([(sWS[buf], ws_t), (sDV, xq1_done)] + bw(mb)) if dc == 0 else []
                _step(p, "tensor", wl,
                      lambda t, mb=mb, buf=buf, dc=dc: t.matmul(
                          pb[mb][:, :], w1s[buf][:, dc * 128:(dc + 1) * 128],
                          xnq[:, dc * CH:(dc + 1) * CH],
                          start=(dc == 0), stop=(dc == DC - 1)),
                      ("then", sMM))
                mm_t = sMM.inc()
            free[f"w1s{buf}"] = (sMM, mm_t)
            free["xnq"] = (sMM, mm_t)
            _step(p, "vector", [(sMM, mm_t)] + fw(f"sqs{buf}"),
                  lambda v, mb=mb, buf=buf, fc=fc: v.scalar_tensor_tensor(
                      sqs[buf][:], pb[mb][:, :], r.corr1c[:, fc:fc + 1], sA_sb[:],
                      ALU.add, ALU.mult),
                  ("drain", sDV))
            bfree[mb] = (sDV, sDV.inc())
            _step(p, "scalar", [(sDV, sDV.n)] + (fw("hbf") if fc == 0 else []),
                  lambda s, buf=buf, fc=fc: s.activation(
                      out=hbf[:, fc * CH:(fc + 1) * CH], in_=sqs[buf][:], func=AF.Gelu,
                      bias=r.b1c[:, fc:fc + 1]),
                  ("drain", sAC))
            gl_ts.append(sAC.inc())
            free[f"sqs{buf}"] = (sAC, gl_ts[-1])

        asm_done = []
        for pp in range(2):
            ga_ev = {}
            for pi, (a, rr_) in enumerate([(0, 0), (1, 1), (0, 1), (1, 0)]):
                gb = pi % 2
                gps = psA[:, gb * 1024:(gb + 1) * 1024]
                bks = (0, 1) if gb == 0 else (4, 5)
                sel = r.idb if a == rr_ else r.swapI
                wl = [(sDV, qproj_done)]
                if pi < 2:
                    wl += bw(bks[0]) + bw(bks[1])
                else:
                    wl.append(ga_ev[pi - 2])

                def t_ga(t, a=a, rr_=rr_, gps=gps, sel=sel, pp=pp):
                    mov = qaf[rr_ * 64:rr_ * 64 + 64, :].rearrange(
                        "p (j t) -> p j t", j=DC)[:, :, 128 * (2 * pp + a):
                                                  128 * (2 * pp + a) + 128]
                    t.matmul(gps[:, 0:512], sel[rr_ * 64:rr_ * 64 + 64, :],
                             mov[:, 0:4, :], start=True, stop=True).then_inc(sMM.h, 1)
                    return t.matmul(gps[:, 512:1024], sel[rr_ * 64:rr_ * 64 + 64, :],
                                    mov[:, 4:8, :], start=True, stop=True)
                _step(p, "tensor", wl, t_ga, ("then", sMM))
                mm_t = sMM.inc(2)
                wl2 = [(sMM, mm_t)] + (fw("qpk") if pi == 0 else [])

                def v_ga(v, a=a, rr_=rr_, gps=gps, pp=pp):
                    src = gps[a * 64:a * 64 + 64, :].rearrange("p (u g) -> p u g", g=128)
                    dst = qpk[pp][a * 64:a * 64 + 64, :].rearrange(
                        "p (u g) -> p u g", g=128)[:, rr_::2, :]
                    v.tensor_copy(dst, src)
                _step(p, "vector", wl2, v_ga, ("drain", sDV))
                ga_ev[pi] = (sDV, sDV.inc())
                for b in bks:
                    bfree[b] = ga_ev[pi]
            ga_vt = {0: ga_ev[2], 1: ga_ev[3]}

            for a in range(2):
                for Rb in range(16):
                    wl = [ga_vt[a]] + (bw(0) if Rb == 0 else [])
                    _step(p, "tensor", wl,
                          lambda t, pp=pp, a=a, Rb=Rb: t.matmul(
                              pb[0][:].bitcast(BF16)[:, 64 * Rb:64 * Rb + 64],
                              qpk[pp][a * 64:a * 64 + 64, 128 * Rb:128 * Rb + 128],
                              r.idb[a * 64:a * 64 + 64, a * 64:a * 64 + 64],
                              is_transpose=True, start=True, stop=True),
                          ("then", sMM))
                    sMM.inc()
                _step(p, "vector", [(sMM, sMM.n)] + fw(f"vt{a}"),
                      lambda v, a=a: v.tensor_copy(vt[a][:], pb[0][:].bitcast(BF16)),
                      ("drain", sDV))
                bfree[0] = (sDV, sDV.inc())
            scbuf = [psA[:, 0:2 * CH], psA[:, 2 * CH:4 * CH]]
            exp_toks = []
            exp_done = {0: 0, 1: 0}
            av_prev = {0: 0, 1: 0}

            def emit_scores(a, Rb, rnd):
                i = len(exp_toks)
                sbu = i % 2
                sh = scbuf[sbu]
                for kc in (2 * rnd, 2 * rnd + 1):
                    wl = [(sDV, mprep_done)] if (i == 0 and kc == 2 * rnd) else []
                    if kc == 2 * rnd:
                        if i >= 2:
                            wl.append((sAC, exp_toks[i - 2]))
                        else:
                            wl += (bw(0) + bw(1)) if sbu == 0 else (bw(4) + bw(5))
                    _step(p, "tensor", wl,
                          lambda t, pp=pp, a=a, Rb=Rb, kc=kc, sh=sh: t.matmul(
                              sh[:, (kc % 2) * CH:(kc % 2) * CH + CH],
                              qpk[pp][a * 64:a * 64 + 64, 128 * Rb:128 * Rb + 128],
                              qpk[pp][a * 64:a * 64 + 64, kc * 512:(kc + 1) * 512],
                              start=True, stop=True),
                          ("then", sMM))
                    sMM.inc()
                wl2 = [(sMM, sMM.n)] + (
                    (fw(f"e{a}") + fw(f"zc4{a}")) if (Rb == 0 and rnd == 0) else [])
                _step(p, "scalar", wl2,
                      lambda s, a=a, Rb=Rb, rnd=rnd, pp=pp, sh=sh: s.activation(
                          out=e_sb[a][:, rnd * 1024:(rnd + 1) * 1024],
                          in_=sh, func=AF.Exp, scale=1.0 / 32.0,
                          bias=mcols[:, 2 * pp + a:2 * pp + a + 1],
                          accum_out=zc4[a][:, 2 * Rb + rnd:2 * Rb + rnd + 1]),
                      ("drain", sAC))
                exp_toks.append(sAC.inc())

            def emit_av(a, Rb):
                for qc in range(4):
                    avb = pbAV[qc]
                    avi = [2, 3, 6, 7][qc]
                    wl = [(sAC, exp_done[a])] if qc == 0 else []
                    if Rb == 0:
                        wl += bw(avi)
                    _step(p, "tensor", wl,
                          lambda t, a=a, Rb=Rb, qc=qc, avb=avb: t.matmul(
                              avb[a * 64:a * 64 + 64, :],
                              vt[a][:, 64 * Rb:64 * Rb + 64],
                              e_sb[a][:, qc * 512:(qc + 1) * 512],
                              start=(Rb == 0), stop=(Rb == 15),
                              tile_position=(0, 64 * a),
                              skip_group_check=True),
                          ("then", sMM))
                    sMM.inc()
                av_prev[a] = sMM.n

            for Rb in range(16):
                for a in range(2):
                    if Rb > 0:
                        emit_av(a, Rb - 1)
                    emit_scores(a, Rb, 0)
                    emit_scores(a, Rb, 1)
                    exp_done[a] = sAC.n
            for a in range(2):
                emit_av(a, 15)
            bfree[0] = bfree[1] = (sAC, exp_toks[-2])
            bfree[4] = bfree[5] = (sAC, exp_toks[-1])
            av_done = sMM.n
            free["vt0"] = (sMM, av_done)
            free["vt1"] = (sMM, av_done)
            free["qpk"] = (sMM, av_done)

            for a in range(2):
                def v_z(v, a=a):
                    zv = zc4[a][:].rearrange("p (r k) -> p r k", k=2)
                    v.tensor_tensor(z16[:].rearrange("p (r o) -> p r o", o=1),
                                    zv[:, :, 0:1], zv[:, :, 1:2], ALU.add)
                    v.drain()
                    v.reciprocal(z16[:], z16[:])
                _step(p, "vector", [(sAC, sAC.n)], v_z, ("drain", sDV))
                sDV.inc()
                free[f"zc4{a}"] = (sDV, sDV.n)
                _step(p, "tensor", bw(0) + [(sDV, sDV.n)],
                      lambda t: t.matmul(pb[0][0:16, 0:128], z16[:], r.idf[:],
                                         is_transpose=True, start=True, stop=True),
                      ("then", sMM))
                sMM.inc()
                _step(p, "vector", [(sMM, sMM.n)] + fw("trst"),
                      lambda v: v.tensor_copy(trst[0:16, :], pb[0][0:16, 0:128]),
                      ("drain", sDV))
                bfree[0] = (sDV, sDV.inc())
                _step(p, "sync", [(sDV, sDV.n)],
                      lambda s, a=a: s.dma_start(out=zrow[a], in_=trst[0:16, :]),
                      ("then", sRD), inck=16)
                sRD.inc(16)
                free["trst"] = (sRD, sRD.n)
            z_done = sRD.n

            for qc in range(4):
                def t_div(t, qc=qc):
                    t.matmul(pb[0][:, :], r.elo[:], scr[0:1, qc * 512:(qc + 1) * 512],
                             start=True, stop=False).then_inc(sMM.h, 1)
                    return t.matmul(pb[0][:, :], r.ehi[:],
                                    scr[0:1, 2048 + qc * 512:2048 + (qc + 1) * 512],
                                    start=False, stop=True)
                _step(p, "tensor", bw(0) + [(sRD, z_done)], t_div, ("then", sMM))
                sMM.inc(2)
                _step(p, "vector", [(sMM, sMM.n)],
                      lambda v: v.tensor_copy(div_sb[:], pb[0][:, :]),
                      ("drain", sDV))
                bfree[0] = (sDV, sDV.inc())
                _step(p, "vector", [(sMM, av_done)],
                      lambda v, qc=qc: v.tensor_tensor(
                          e_sb[0][:, qc * 512:(qc + 1) * 512], pbAV[qc][:, :],
                          div_sb[:], ALU.mult),
                      ("drain", sDV))
                dv_t = sDV.inc()
                bfree[[2, 3, 6, 7][qc]] = (sDV, dv_t)
            evac_done = sDV.n
            free["e1"] = (sDV, evac_done)
            free["scr"] = (sMM, sMM.n)

            asm_ev = {}
            for pi, (a, rr_) in enumerate([(0, 0), (1, 1), (0, 1), (1, 0)]):
                gb = pi % 2
                gps = psA[:, gb * 1024:(gb + 1) * 1024]
                bks = (0, 1) if gb == 0 else (4, 5)
                sel = r.idb if a == rr_ else r.swapI
                wl = [(sDV, evac_done)]
                if pi < 2:
                    wl += bw(bks[0]) + bw(bks[1])
                else:
                    wl.append(asm_ev[pi - 2])

                def t_as(t, a=a, rr_=rr_, gps=gps, sel=sel):
                    mov = e_sb[0][a * 64:a * 64 + 64, :].rearrange(
                        "p (u g) -> p u g", g=128)[:, rr_::2, :]
                    t.matmul(gps[:, 0:512], sel[a * 64:a * 64 + 64, :],
                             mov[:, 0:4, :], start=True, stop=True).then_inc(sMM.h, 1)
                    return t.matmul(gps[:, 512:1024], sel[a * 64:a * 64 + 64, :],
                                    mov[:, 4:8, :], start=True, stop=True)
                _step(p, "tensor", wl, t_as, ("then", sMM))
                mm_t = sMM.inc(2)
                wl2 = [(sMM, mm_t)] + (fw("qaf") if pi == 0 else [])

                def v_as(v, a=a, rr_=rr_, gps=gps, pp=pp):
                    src = gps[rr_ * 64:rr_ * 64 + 64, :].rearrange(
                        "p (u g) -> p u g", g=128)
                    dst = qaf[rr_ * 64:rr_ * 64 + 64, :].rearrange(
                        "p (j t) -> p j t", j=DC)[:, :, 128 * (2 * pp + a):
                                                  128 * (2 * pp + a) + 128]
                    v.tensor_copy(dst, src)
                _step(p, "vector", wl2, v_as, ("drain", sDV))
                asm_ev[pi] = (sDV, sDV.inc())
                for b in bks:
                    bfree[b] = asm_ev[pi]
            asm_done.append(sDV.n)
            free["e0"] = (sMM, sMM.n)

        if c + 1 < NCH:
            pst = emit_stats(c + 1, (sDV, rows4_done))
        for fc in range(FC):
            emit_mm1(fc)

        for oc in range(DC):
            buf = oc % 2
            _step(p, "sync", fw(f"wqs{buf}") + [(sWS[buf], sWS[buf].n)],
                  lambda s, oc=oc, buf=buf: _slab(
                      s, sWS[buf].h, wqs[buf][:], r.wuT[oc]),
                  ("then", sWS[buf]), inck=16)
            ws_t = sWS[buf].inc(64)
            mb = 4 + (oc % 2)
            for dc in range(DC):
                wl = ([(sWS[buf], ws_t), (sDV, asm_done[-1])] + bw(mb)) if dc == 0 else []
                _step(p, "tensor", wl,
                      lambda t, mb=mb, buf=buf, dc=dc: t.matmul(
                          pb[mb][:, :], wqs[buf][:, dc * 128:(dc + 1) * 128],
                          qaf[:, dc * CH:(dc + 1) * CH], start=(dc == 0),
                          stop=(dc == DC - 1)),
                      ("then", sMM))
                mm_t = sMM.inc()
            free[f"wqs{buf}"] = (sMM, mm_t)
            _step(p, "vector", [(sMM, mm_t)],
                  lambda v, mb=mb, c=c, oc=oc: v.scalar_tensor_tensor(
                      xap(c, oc), pb[mb][:, :], r.buc[:, oc:oc + 1], xap(c, oc),
                      ALU.add, ALU.add),
                  ("drain", sDV))
            bfree[mb] = (sDV, sDV.inc())
        free["qaf"] = (sMM, sMM.n)

        for fc in range(FC):
            buf = fc % 2
            _step(p, "scalar", fw(f"sq2{buf}") + [(sAC, gl_ts[fc])],
                  lambda s, buf=buf, fc=fc: s.activation(
                      out=sq2[buf][:], in_=hbf[:, fc * CH:(fc + 1) * CH], func=AF.Square),
                  ("drain", sAC))
            sqh_t = sAC.inc()
            _step(p, "tensor", [(sAC, sqh_t)] + (bw(0) if fc == 0 else []),
                  lambda t, buf=buf, fc=fc: t.matmul(
                      pb[0][0:1, :], r.ones_f_col[:], sq2[buf][:],
                      start=(fc == 0), stop=(fc == FC - 1)),
                  ("then", sMM))
            mm_t = sMM.inc()

            def v_mx(v, buf=buf, fc=fc):
                if fc == 0:
                    v.tensor_copy(maxacc[:], sq2[buf][:])
                else:
                    v.tensor_tensor(maxacc[:], maxacc[:], sq2[buf][:], ALU.max)
            _step(p, "vector", [(sAC, sqh_t)] + (fw("maxacc") if fc == 0 else []),
                  v_mx, ("drain", sDV))
            dv_t = sDV.inc()
            free[f"sq2{buf}"] = [(sMM, mm_t), (sDV, dv_t)]
        ssq_done = sMM.n
        vmx_done = sDV.n

        _step(p, "vector", [(sMM, ssq_done)],
              lambda v: v.tensor_scalar(rw(SSQ), pb[0][0:1, :], 1.0 / FF, None, ALU.mult),
              ("drain", sDV))
        bfree[0] = (sDV, sDV.inc())

        def sc_hr1(s):
            s.activation(out=rw(LNV2), in_=rw(SSQ), func=AF.Ln, bias=r.epsc[0:1, 0:1])
            s.drain()
            s.activation(out=rw(RSTD2), in_=rw(LNV2), func=AF.Exp, scale=-0.5)
        _step(p, "scalar", [(sDV, sDV.n)], sc_hr1, ("drain", sAC))
        sAC.inc()
        for j in range(4):
            _step(p, "tensor", [(sDV, vmx_done)] + (bw(2) if j == 0 else []),
                  lambda t, j=j: t.matmul(
                      pb[2][:, j * 128:(j + 1) * 128], maxacc[:, j * 128:(j + 1) * 128],
                      r.idf[:], is_transpose=True, start=True, stop=True),
                  ("then", sMM))
            sMM.inc()
        free["maxacc"] = (sMM, sMM.n)
        _step(p, "vector", [(sMM, sMM.n)],
              lambda v: [v.reduce_max(mxcol[:, j:j + 1], pb[2][:, j * 128:(j + 1) * 128],
                                      axis=mybir.AxisListType.X) for j in range(4)][-1],
              ("drain", sDV))
        bfree[2] = (sDV, sDV.inc())
        _step(p, "tensor", bw(2),
              lambda t: t.matmul(pb[2][0:4, 0:128], mxcol[:, 0:4], r.idf[:],
                                 is_transpose=True, start=True, stop=True),
              ("then", sMM))
        sMM.inc()
        _step(p, "vector", [(sMM, sMM.n)] + fw("trst"),
              lambda v: v.tensor_copy(trst[0:4, :], pb[2][0:4, 0:128]),
              ("drain", sDV))
        bfree[2] = (sDV, sDV.inc())
        _step(p, "sync", [(sDV, sDV.n)],
              lambda s: s.dma_start(out=rw(MXH2), in_=trst[0:4, :]),
              ("then", sRD), inck=16)
        mxh_done = sRD.inc(16)
        free["trst"] = (sRD, mxh_done)

        def sc_hr2(s):
            s.activation(out=rw(LNM), in_=rw(MXH2), func=AF.Ln)
        _step(p, "scalar", [(sRD, mxh_done)], sc_hr2, ("drain", sAC))
        sAC.inc()

        def v_hr(v):
            v.tensor_scalar(rw(SSQ), rw(LNV2), 0.5, None, ALU.mult)
            v.drain()
            v.scalar_tensor_tensor(rw(LNHN), rw(LNM), 0.5, rw(SSQ), ALU.mult,
                                   ALU.subtract)
            v.drain()
            v.tensor_scalar(rw(LNHN), rw(LNHN), math.log(1e-5), None, ALU.max)
        _step(p, "vector", [(sAC, sAC.n)], v_hr, ("drain", sDV))
        sDV.inc()

        def sc_hr3(s):
            s.activation(out=rw(S2r), in_=rw(LNHN), func=AF.Exp, scale=-1.0,
                         bias=r.l127[0:1, 0:1])
            s.drain()
            s.activation(out=rw(SBr), in_=rw(LNHN), func=AF.Exp, bias=nl127[0:1, 0:1])
        _step(p, "scalar", [(sDV, sDV.n)], sc_hr3, ("drain", sAC))
        sAC.inc()

        def v_hr2(v):
            v.tensor_tensor(rw(C2r), rw(RSTD2), rw(S2r), ALU.mult)
            v.drain()
            v.tensor_tensor(rw(SBr), rw(SBr), bass.AP(r.mw2, 0, [[1, 1], [0, CH]]),
                            ALU.mult)
        _step(p, "vector", [(sAC, sAC.n)], v_hr2, ("drain", sDV))
        hr_done = sDV.inc()

        def t_hb(t):
            t.matmul(pb[6][:, :], r.ones_f_row1[:], rw(C2r),
                     start=True, stop=True).then_inc(sMM.h, 1)
            return t.matmul(pb[1][:, :], r.ones_f_row1[:], rw(SBr),
                            start=True, stop=True)
        _step(p, "tensor", bw(6) + bw(1) + [(sDV, hr_done)], t_hb, ("then", sMM))
        sMM.inc(2)
        hb_done = sMM.n
        _step(p, "vector", [(sMM, hb_done)],
              lambda v: v.tensor_copy(sB_sb[:], pb[1][:, :]),
              ("drain", sDV))
        bfree[1] = (sDV, sDV.inc())

        hq_toks = []
        for fc in range(FC):
            def v_q(v, fc=fc):
                b2_ = fc % 2
                hv = hbf[:, fc * CH:(fc + 1) * CH]
                v.tensor_tensor(sqs[b2_][:], hv, pb[6][:, :], ALU.mult)
                v.drain()
                v.tensor_scalar(intq2[b2_][:], sqs[b2_][:], r.offc[:, 0:1], None, ALU.add)
                v.drain()
                v.tensor_copy(hv, intq2[b2_][:])
                v.drain()
            _step(p, "vector", [(sMM, hb_done)], v_q, ("drain", sDV))
            hq_toks.append(sDV.inc())
        hq_done = sDV.n
        bfree[6] = (sDV, hq_done)

        for oc in range(DC):
            w2h_t = []
            for hf in range(2):
                _step(p, "sync", fw(f"w2h{hf}") + [(sW2h[hf], sW2h[hf].n)],
                      lambda s, oc=oc, hf=hf: _slab(
                          s, sW2h[hf].h, w2s[:, hf * 2048:(hf + 1) * 2048],
                          r.w2T[oc][:, hf * 2048:(hf + 1) * 2048]),
                      ("then", sW2h[hf]), inck=16)
                w2h_t.append(sW2h[hf].inc(64))
            mb = 4 + (oc % 2)
            for fc2 in range(FC):
                hf = fc2 // 16
                wl = ([(sW2h[hf], w2h_t[hf])] if fc2 % 16 == 0 else []) \
                    + ([(sDV, hq_toks[fc2])] if oc == 0 else []) \
                    + (bw(mb) if fc2 == 0 else [])
                _step(p, "tensor", wl,
                      lambda t, mb=mb, fc2=fc2: t.matmul(
                          pb[mb][:, :], w2s[:, fc2 * 128:(fc2 + 1) * 128],
                          hbf[:, fc2 * CH:(fc2 + 1) * CH],
                          start=(fc2 == 0), stop=(fc2 == FC - 1)),
                      ("then", sMM))
                mm_t = sMM.inc()
                if fc2 == 15:
                    free["w2h0"] = (sMM, mm_t)
            free["w2h1"] = (sMM, mm_t)

            def v_ev2(v, mb=mb, c=c, oc=oc, buf=oc % 2):
                v.scalar_tensor_tensor(sqs[buf][:], pb[mb][:, :], r.corr2c[:, oc:oc + 1],
                                       sB_sb[:], ALU.add, ALU.mult)
                v.drain()
                v.scalar_tensor_tensor(xap(c, oc), sqs[buf][:], r.b2c[:, oc:oc + 1],
                                       xap(c, oc), ALU.add, ALU.add)
            _step(p, "vector", [(sMM, mm_t)] + fw(f"sqs{oc % 2}"), v_ev2, ("drain", sDV))
            dv_t = sDV.inc()
            bfree[mb] = (sDV, dv_t)
            free[f"sqs{oc % 2}"] = (sDV, dv_t)
        free["hbf"] = (sMM, sMM.n)

    r.l0_done = (sDV, sDV.n)



def _chunk_stats(r, p, pb, sMM, sDV, sAC, rw, src_ap_fn, sqpc, sqst, MEAN, M2, VAR,
                 LNV, RSTD, prev_ac, bw, bfree, src_ready=None):
    for dc in range(DC):
        buf = dc % 2
        wl = ([] if sqst[buf] == 0 else [(sMM, sqst[buf])])
        if src_ready is not None and dc == 0:
            wl = wl + [src_ready]
        _step(p, "scalar", wl,
              lambda s, dc=dc, buf=buf: s.activation(
                  out=sqpc[buf][:], in_=src_ap_fn(dc), func=AF.Square),
              ("drain", sAC))
        ac_t = sAC.inc()
        _step(p, "tensor", [(sAC, ac_t)] + (bw(1) if dc == 0 else []),
              lambda t, dc=dc, buf=buf: t.matmul(
                  pb[1][0:1, :], r.ones_bf_col[:], sqpc[buf][:],
                  start=(dc == 0), stop=(dc == DC - 1)),
              ("then", sMM))
        sqst[buf] = sMM.inc()
    s2_done = sMM.n
    for dc in range(DC):
        _step(p, "tensor", bw(0) if dc == 0 else [],
              lambda t, dc=dc: t.matmul(
                  pb[0][0:1, :], r.ones_bf_col[:], src_ap_fn(dc),
                  start=(dc == 0), stop=(dc == DC - 1)),
              ("then", sMM))
        s1_done = sMM.inc()
    _step(p, "vector", [(sMM, s1_done), (sAC, prev_ac)],
          lambda v: v.tensor_scalar(rw(MEAN), pb[0][0:1, :], 1.0 / D, None, ALU.mult),
          ("drain", sDV))
    bfree[0] = (sDV, sDV.inc())
    _step(p, "scalar", [(sDV, sDV.n)],
          lambda s: s.activation(out=rw(M2), in_=rw(MEAN), func=AF.Square),
          ("drain", sAC))
    sAC.inc()
    _step(p, "vector", [(sAC, sAC.n), (sMM, s2_done)],
          lambda v: v.scalar_tensor_tensor(rw(VAR), pb[1][0:1, :], 1.0 / D, rw(M2),
                                           ALU.mult, ALU.subtract),
          ("drain", sDV))
    bfree[1] = (sDV, sDV.inc())

    def sc_r(s):
        s.activation(out=rw(LNV), in_=rw(VAR), func=AF.Ln, bias=r.epsc[0:1, 0:1])
        s.drain()
        s.activation(out=rw(RSTD), in_=rw(LNV), func=AF.Exp, scale=-0.5)
    _step(p, "scalar", [(sDV, sDV.n)], sc_r, ("drain", sAC))
    return sAC.inc()


def emit_l1(r, p, psb, out):
    nc = r.nc
    sb = psb["sb"]
    st = psb["st"]
    xf = r.x_fm
    xn = r.xn_buf

    sqpc = [sb(f"l1_sq{i}", [128, CH], BF16) for i in range(2)]
    evsc = [sb(f"l1_ev{i}", [128, CH], F32) for i in range(2)]
    sqst = {0: 0, 1: 0}
    _i2 = sb("l1_int", [128, CH], I32)
    intq2 = [_i2, _i2]
    wts = [sb(f"l1_wts{i}", [128, DC * 128], BF16) for i in range(2)]
    rows = sb("l1_rows", [1, 4 * CH], F32)
    MEAN, M2, VAR, RSTD = range(4)
    LNV = VAR
    mixb = sb("l1_mixb", [128, DC * CH], BF16)
    xln = sb("l1_xln", [128, DC * CH], BF16)
    ekt = sb("l1_ek", [128, DC * CH], BF16)
    vt_ = sb("l1_v", [128, DC * CH], BF16)
    rst = sb("l1_rs", [128, DC * CH], BF16)
    At = sb("l1_A", [128, DC * CH], BF16)
    Bt = sb("l1_B", [128, DC * CH], BF16)
    carryA = sb("l1_cA", [128, DC], F32)
    carryB = sb("l1_cB", [128, DC], F32)
    xcar = sb("l1_xcar", [128, DC], BF16)
    ycar = sb("l1_ycar", [128, DC], BF16)
    com = sb("l1_com", [128, DC], F32)
    wfks = wts
    wfvs = [rst[:, 0:D], rst[:, D:2 * D]]
    kks = [rst[:, 2 * D:2 * D + 256], rst[:, 2 * D + 256:2 * D + 512]]
    xiq = vt_
    ynt = xln
    ymt = At
    rs2 = Bt
    outst = ekt[:].bitcast(F32)
    _mixf = mixb[:].bitcast(F32)
    tokst = [_mixf[:, 0:D], _mixf[:, D:2 * D]]

    pb = [st.enter_context(nc.psum_tensor(f"l1_pb{i}", [128, CH], F32)) for i in range(4)]
    pso = st.enter_context(nc.psum_tensor("l1_pso", [128, DC * 256], F32))
    bfree = {i: None for i in range(4)}

    def bw(i):
        return [bfree[i]] if bfree[i] else []

    sMM = r.sem("l1_mm")
    sDV = r.sem("l1_dv")
    sAC = r.sem("l1_ac")
    sWS = [r.sem("l1_ws0"), r.sem("l1_ws1")]
    sWV = [r.sem("l1_wv0"), r.sem("l1_wv1")]
    sOD = [r.sem("l1_od0"), r.sem("l1_od1")]

    free = {}
    for nm in ("wts0", "wts1", "xiq", "mixb", "tmp2", "xln", "ek", "v", "rs", "A", "B",
               "yn", "ym", "rs2", "kks0", "kks1", "wfks0", "wfks1", "wfvs0", "wfvs1",
               "outst", "tok0", "tok1", "ev0", "ev1"):
        free[nm] = None

    def fw(nm):
        f = free[nm]
        if f is None:
            return []
        return list(f) if isinstance(f, list) else [f]

    def rw(i):
        return rows[0:1, i * CH:(i + 1) * CH]

    def xap(c, dc=None):
        if dc is None:
            return bass.AP(xf, c * CH, [[DC * T, 128], [T, DC], [1, CH]])
        return bass.AP(xf, dc * T + c * CH, [[DC * T, 128], [1, CH]])

    def nap(c, dc=None):
        if dc is None:
            return bass.AP(xn, c * CH, [[DC * T, 128], [T, DC], [1, CH]])
        return bass.AP(xn, dc * T + c * CH, [[DC * T, 128], [1, CH]])

    def pbc(bank, ndc=DC):
        return bass.AP(bank, 0, [[CH, 128], [0, ndc], [1, CH]])

    def stream_w(buf, hbm, oc, sem_):
        def go(s, hbm=hbm, oc=oc, buf=buf):
            return _slab(s, sWS[buf].h, wts[buf][:], hbm[oc])
        return go

    def emit_p2a(c):
        prev_ac = sAC.n
        rstd_ac = _chunk_stats(r, p, pb, sMM, sDV, sAC, rw, lambda dc, c=c: xap(c, dc),
                               sqpc, sqst, MEAN, M2, VAR, LNV, RSTD, prev_ac, bw, bfree)
        _step(p, "tensor", bw(0) + bw(1) + [(sAC, rstd_ac)],
              lambda t: (t.matmul(pb[0][:, :], r.ones_f_row1[:], rw(MEAN),
                                  start=True, stop=True).then_inc(sMM.h, 1),
                         t.matmul(pb[1][:, :], r.ones_f_row1[:], rw(RSTD),
                                  start=True, stop=True))[1],
              ("then", sMM))
        sMM.inc(2)
        bc_done = sMM.n
        _step(p, "vector", [(sMM, bc_done)],
              lambda v, c=c: v.tensor_tensor(nap(c), xap(c), pbc(pb[0]), ALU.subtract),
              ("drain", sDV))
        sDV.inc()
        _step(p, "vector", [],
              lambda v, c=c: v.tensor_tensor(nap(c), nap(c), pbc(pb[1]), ALU.mult),
              ("drain", sDV))
        xn_done = sDV.inc()
        bfree[0] = (sDV, xn_done)
        bfree[1] = (sDV, xn_done)

        xitok = []
        for dc in range(DC):
            b2_ = dc % 2

            def v_xi1(v, c=c, dc=dc, b2_=b2_):
                v.tensor_scalar(evsc[b2_][:], nap(c, dc),
                                r.sinv_col[:, 0:1], -127.0, ALU.mult, ALU.max)
                v.drain()
                v.tensor_scalar(intq2[b2_][:], evsc[b2_][:], 127.0, r.offc[:, 0:1],
                                ALU.min, ALU.add)
                v.drain()
                v.tensor_copy(xiq[:, dc * CH:(dc + 1) * CH], intq2[b2_][:])
                v.drain()
            wl = ([(sDV, xn_done)] + fw("v") + fw("ev0") + fw("ev1")) if dc == 0 else []
            _step(p, "vector", wl, v_xi1, ("drain", sDV))
            xitok.append(sDV.inc())
        return {"xn_done": xn_done, "xitok": xitok}

    p2a = None
    for c in range(NCH):
        batch_start = (c % (SEQ // CH) == 0)
        if c == 0:
            p2a = emit_p2a(0)
        xn_done, xitok = p2a["xn_done"], p2a["xitok"]

        for oc in range(DC):
            buf = oc % 2
            _step(p, "sync", fw(f"wts{buf}") + [(sWS[buf], sWS[buf].n)]
                  + ([r.l1_casts] if c == 0 and oc == 0 else []),
                  stream_w(buf, r.wtT, oc, None), ("then", sWS[buf]), inck=16)
            ws_t = sWS[buf].inc(64)
            mb = 2 + buf
            for dc in range(DC):
                wl = ([(sWS[buf], ws_t)] if dc == 0 else []) \
                    + ([(sDV, xitok[dc])] if oc == 0 else []) \
                    + (bw(mb) if dc == 0 else [])
                _step(p, "tensor", wl,
                      lambda t, mb=mb, buf=buf, dc=dc: t.matmul(
                          pb[mb][:, :], wts[buf][:, dc * 128:(dc + 1) * 128],
                          xiq[:, dc * CH:(dc + 1) * CH],
                          start=(dc == 0), stop=(dc == DC - 1)),
                      ("then", sMM))
                mm_t = sMM.inc()
            free[f"wts{buf}"] = (sMM, mm_t)

            def v_ev(v, mb=mb, c=c, oc=oc, buf=buf):
                v.tensor_scalar(evsc[buf][:], pb[mb][:, :], r.corrL1c[:, oc:oc + 1],
                                r.s_col[:, 0:1], ALU.add, ALU.mult)
                v.drain()
                v.scalar_tensor_tensor(xap(c, oc), evsc[buf][:], r.bL1c[:, oc:oc + 1],
                                       xap(c, oc), ALU.add, ALU.add)
            _step(p, "vector", [(sMM, mm_t)], v_ev, ("drain", sDV))
            bfree[mb] = (sDV, sDV.inc())
        free["v"] = (sMM, sMM.n)

        prev_ac = sAC.n
        rstd_ac = _chunk_stats(r, p, pb, sMM, sDV, sAC, rw, lambda dc, c=c: nap(c, dc),
                               sqpc, sqst, MEAN, M2, VAR, LNV, RSTD, prev_ac, bw, bfree,
                               src_ready=(sDV, xn_done))
        _step(p, "vector", [(sAC, rstd_ac)],
              lambda v: v.tensor_tensor(rw(M2), rw(MEAN), rw(RSTD), ALU.mult),
              ("drain", sDV))
        mr_done = sDV.inc()
        _step(p, "tensor", bw(0) + bw(1) + [(sDV, mr_done)],
              lambda t: (t.matmul(pb[0][:, :], r.ones_f_row1[:], rw(RSTD),
                                  start=True, stop=True).then_inc(sMM.h, 1),
                         t.matmul(pb[1][:, :], r.ones_f_row1[:], rw(M2),
                                  start=True, stop=True))[1],
              ("then", sMM))
        sMM.inc(2)
        bc_done = sMM.n
        _step(p, "vector", [(sMM, bc_done)] + fw("xln"),
              lambda v, c=c: v.tensor_tensor(
                  xln[:].rearrange("p (j t) -> p j t", j=DC), nap(c), pbc(pb[0]),
                  ALU.mult),
              ("drain", sDV))
        sDV.inc()
        _step(p, "vector", [],
              lambda v: v.tensor_tensor(
                  xln[:].rearrange("p (j t) -> p j t", j=DC),
                  xln[:].rearrange("p (j t) -> p j t", j=DC), pbc(pb[1]), ALU.subtract),
              ("drain", sDV))
        dv_t = sDV.inc()
        bfree[0] = (sDV, dv_t)
        bfree[1] = (sDV, dv_t)

        def v_gb(v):
            xv = xln[:].rearrange("p (j t) -> p j t", j=DC)
            v.tensor_tensor(xv, xv, bass.AP(r.g1c, 0, [[DC, 128], [1, DC], [0, CH]]),
                            ALU.mult)
            v.drain()
            v.tensor_tensor(xv, xv, bass.AP(r.b1lc, 0, [[DC, 128], [1, DC], [0, CH]]),
                            ALU.add)
        _step(p, "vector", [], v_gb, ("drain", sDV))
        xln_done = sDV.inc()

        proj_done = {}
        ektoks, vtoks, rstoks = [], [], []
        mix_war = None
        for mi, (mc, omc, whbm, nm) in enumerate([
                (r.mkc, r.omkc, r.wkB, "k"), (r.mvc, r.omvc, r.wvB, "v"),
                (r.mrc, r.omrc, r.wrB, "r")]):
            if not batch_start:
                _step(p, "vector", [],
                      lambda v, omc=omc: v.tensor_tensor(com[:], xcar[:], omc[:],
                                                         ALU.mult),
                      ("drain", sDV))
                sDV.inc()
            mixtok = []
            for dc in range(DC):
                def v_mix1(v, mc=mc, omc=omc, dc=dc, batch_start=batch_start):
                    v.tensor_scalar(mixb[:, dc * CH + 1:(dc + 1) * CH],
                                    xln[:, dc * CH:(dc + 1) * CH - 1],
                                    omc[:, dc:dc + 1], None, ALU.mult)
                    v.drain()
                    if batch_start:
                        v.memset(mixb[:, dc * CH:dc * CH + 1], 0.0)
                    else:
                        v.tensor_copy(mixb[:, dc * CH:dc * CH + 1], com[:, dc:dc + 1])
                    v.drain()
                    v.scalar_tensor_tensor(mixb[:, dc * CH:(dc + 1) * CH],
                                           xln[:, dc * CH:(dc + 1) * CH],
                                           mc[:, dc:dc + 1],
                                           mixb[:, dc * CH:(dc + 1) * CH],
                                           ALU.mult, ALU.add)
                    v.drain()
                wl = ([(sDV, xln_done)] + fw("mixb")) if (mi == 0 and dc == 0) else []
                if mix_war is not None:
                    wl = wl + [mix_war[dc]]
                _step(p, "vector", wl, v_mix1, ("drain", sDV))
                mixtok.append(sDV.inc())

            mtoks = [0] * DC
            for oc in range(DC):
                buf = oc % 2
                _step(p, "sync", fw(f"wts{buf}") + [(sWS[buf], sWS[buf].n)],
                      stream_w(buf, whbm, oc, None), ("then", sWS[buf]), inck=16)
                ws_t = sWS[buf].inc(64)
                mb = 2 + buf
                for dc in range(DC):
                    wl = ([(sWS[buf], ws_t)] if dc == 0 else []) \
                        + ([(sDV, mixtok[dc])] if oc == 0 else []) \
                        + (bw(mb) if dc == 0 else [])
                    _step(p, "tensor", wl,
                          lambda t, mb=mb, buf=buf, dc=dc: t.matmul(
                              pb[mb][:, :], wts[buf][:, dc * 128:(dc + 1) * 128],
                              mixb[:, dc * CH:(dc + 1) * CH],
                              start=(dc == 0), stop=(dc == DC - 1)),
                          ("then", sMM))
                    mm_t = sMM.inc()
                    if oc == DC - 1:
                        mtoks[dc] = (sMM, mm_t)
                free[f"wts{buf}"] = (sMM, mm_t)
                if nm == "k":
                    _step(p, "scalar", [(sMM, mm_t)] + (fw("ek") if oc == 0 else []),
                          lambda s, mb=mb, oc=oc: s.activation(
                              out=ekt[:, oc * CH:(oc + 1) * CH], in_=pb[mb][:, :],
                              func=AF.Exp),
                          ("drain", sAC))
                    bfree[mb] = (sAC, sAC.inc())
                    ektoks.append((sAC, sAC.n))
                elif nm == "v":
                    _step(p, "vector", [(sMM, mm_t)] + (fw("v") if oc == 0 else []),
                          lambda v, mb=mb, oc=oc: v.tensor_copy(
                              vt_[:, oc * CH:(oc + 1) * CH], pb[mb][:, :]),
                          ("drain", sDV))
                    bfree[mb] = (sDV, sDV.inc())
                    vtoks.append((sDV, sDV.n))
                else:
                    _step(p, "scalar", [(sMM, mm_t)] + (fw("rs") if oc == 0 else []),
                          lambda s, mb=mb, oc=oc: s.activation(
                              out=rst[:, oc * CH:(oc + 1) * CH], in_=pb[mb][:, :],
                              func=AF.Sigmoid),
                          ("drain", sAC))
                    bfree[mb] = (sAC, sAC.inc())
                    rstoks.append((sAC, sAC.n))
            proj_done[nm] = (sAC.n, sDV.n)
            free["mixb"] = (sMM, sMM.n)
            mix_war = mtoks
        ek_done, v_done, rs_done = proj_done["k"][0], proj_done["v"][1], proj_done["r"][0]

        _step(p, "vector", [(sDV, xn_done)],
              lambda v, c=c: v.tensor_copy(
                  xcar[:].rearrange("p (j o) -> p j o", o=1),
                  nap(c)[:, :, CH - 1:CH]),
              ("drain", sDV))
        xcar_done = sDV.inc()

        scantok = []
        for dc in range(DC):
            def v_scan1(v, dc=dc, batch_start=batch_start):
                ekd = ekt[:, dc * CH:(dc + 1) * CH]
                vd = vt_[:, dc * CH:(dc + 1) * CH]
                if batch_start:
                    v.tensor_tensor(ekt[:, dc * CH:dc * CH + 1],
                                    ekt[:, dc * CH:dc * CH + 1],
                                    r.etfc[:, dc:dc + 1], ALU.mult)
                    v.drain()
                v.tensor_tensor(vd, ekd, vd, ALU.mult)
                v.drain()
                ini_a = 0.0 if batch_start else carryA[:, dc:dc + 1]
                ini_b = 0.0 if batch_start else carryB[:, dc:dc + 1]
                ewb = bass.AP(r.ewc, dc, [[DC, 128], [0, CH]])
                v.tensor_tensor_scan(At[:, dc * CH:(dc + 1) * CH], ewb, vd, ini_a,
                                     ALU.mult, ALU.add)
                v.tensor_tensor_scan(Bt[:, dc * CH:(dc + 1) * CH], ewb, ekd, ini_b,
                                     ALU.mult, ALU.add)
                v.drain()
            wl = [ektoks[dc], vtoks[dc]] + ((fw("A") + fw("B")) if dc == 0 else [])
            _step(p, "vector", wl, v_scan1, ("drain", sDV))
            scantok.append(sDV.inc())

        def v_car(v):
            v.tensor_copy(carryA[:].rearrange("p (j o) -> p j o", o=1),
                          At[:].rearrange("p (j t) -> p j t", j=DC)[:, :, CH - 1:CH])
            v.tensor_copy(carryB[:].rearrange("p (j o) -> p j o", o=1),
                          Bt[:].rearrange("p (j t) -> p j t", j=DC)[:, :, CH - 1:CH])
        _step(p, "vector", [], v_car, ("drain", sDV))
        scan_done = sDV.inc()

        wkvtok = []
        for dc in range(DC):
            b2_ = dc % 2

            def v_recip1(v, dc=dc, b2_=b2_):
                v.tensor_scalar(evsc[b2_][:], Bt[:, dc * CH:(dc + 1) * CH], 1e-8, None,
                                ALU.add)
            _step(p, "vector", [(sDV, scantok[dc])] + fw(f"ev{b2_}"), v_recip1,
                  ("drain", sDV))
            sDV.inc()

            def sc_recip(s, b2_=b2_):
                s.activation(out=evsc[b2_][:], in_=evsc[b2_][:], func=AF.Ln)
                s.drain()
                s.activation(out=evsc[b2_][:], in_=evsc[b2_][:], func=AF.Exp, scale=-1.0)
            _step(p, "scalar", [(sDV, sDV.n)], sc_recip, ("drain", sAC))
            ac_t = sAC.inc()

            def v_wkv(v, dc=dc, b2_=b2_):
                v.tensor_tensor(vt_[:, dc * CH:(dc + 1) * CH],
                                At[:, dc * CH:(dc + 1) * CH], evsc[b2_][:], ALU.mult)
                v.drain()
                v.tensor_tensor(rst[:, dc * CH:(dc + 1) * CH],
                                rst[:, dc * CH:(dc + 1) * CH],
                                vt_[:, dc * CH:(dc + 1) * CH], ALU.mult)
            _step(p, "vector", [(sAC, ac_t), rstoks[dc]], v_wkv, ("drain", sDV))
            free[f"ev{b2_}"] = (sDV, sDV.inc())
            wkvtok.append((sDV, sDV.n))
        rwkv_done = sDV.n
        free["A"] = (sDV, rwkv_done)
        free["B"] = (sDV, rwkv_done)
        free["ek"] = (sDV, rwkv_done)
        free["v"] = (sDV, rwkv_done)

        for oc in range(DC):
            buf = oc % 2
            _step(p, "sync", fw(f"wts{buf}") + [(sWS[buf], sWS[buf].n)],
                  stream_w(buf, r.woB, oc, None), ("then", sWS[buf]), inck=16)
            ws_t = sWS[buf].inc(64)
            mb = 2 + buf
            for dc in range(DC):
                wl = ([(sWS[buf], ws_t)] if dc == 0 else []) \
                    + ([wkvtok[dc]] if oc == 0 else []) \
                    + (bw(mb) if dc == 0 else [])
                _step(p, "tensor", wl,
                      lambda t, mb=mb, buf=buf, dc=dc: t.matmul(
                          pb[mb][:, :], wts[buf][:, dc * 128:(dc + 1) * 128],
                          rst[:, dc * CH:(dc + 1) * CH],
                          start=(dc == 0), stop=(dc == DC - 1)),
                      ("then", sMM))
                mm_t = sMM.inc()
            free[f"wts{buf}"] = (sMM, mm_t)
            _step(p, "vector", [(sMM, mm_t), (sDV, xcar_done)],
                  lambda v, mb=mb, c=c, oc=oc: v.tensor_tensor(
                      nap(c, oc), nap(c, oc), pb[mb][:, :], ALU.add),
                  ("drain", sDV))
            bfree[mb] = (sDV, sDV.inc())
        free["rs"] = (sMM, sMM.n)
        xin_done = sDV.n

        if c + 1 < NCH:
            p2a = emit_p2a(c + 1)

        prev_ac = sAC.n
        rstd_ac = _chunk_stats(r, p, pb, sMM, sDV, sAC, rw, lambda dc, c=c: nap(c, dc),
                               sqpc, sqst, MEAN, M2, VAR, LNV, RSTD, prev_ac, bw, bfree,
                               src_ready=(sDV, xin_done))
        _step(p, "vector", [(sAC, rstd_ac)],
              lambda v: v.tensor_tensor(rw(M2), rw(MEAN), rw(RSTD), ALU.mult),
              ("drain", sDV))
        mr_done = sDV.inc()
        _step(p, "tensor", bw(0) + bw(1) + [(sDV, mr_done)],
              lambda t: (t.matmul(pb[0][:, :], r.ones_f_row1[:], rw(RSTD),
                                  start=True, stop=True).then_inc(sMM.h, 1),
                         t.matmul(pb[1][:, :], r.ones_f_row1[:], rw(M2),
                                  start=True, stop=True))[1],
              ("then", sMM))
        sMM.inc(2)
        bc_done = sMM.n
        _step(p, "vector", [(sMM, bc_done)] + fw("yn") + fw("mixb"),
              lambda v, c=c: v.tensor_tensor(
                  ynt[:].rearrange("p (j t) -> p j t", j=DC), nap(c), pbc(pb[0]),
                  ALU.mult),
              ("drain", sDV))
        sDV.inc()
        _step(p, "vector", [],
              lambda v: v.tensor_tensor(
                  ynt[:].rearrange("p (j t) -> p j t", j=DC),
                  ynt[:].rearrange("p (j t) -> p j t", j=DC), pbc(pb[1]), ALU.subtract),
              ("drain", sDV))
        dv_t = sDV.inc()
        bfree[0] = (sDV, dv_t)
        bfree[1] = (sDV, dv_t)

        def v_gb2(v):
            yv = ynt[:].rearrange("p (j t) -> p j t", j=DC)
            v.tensor_tensor(yv, yv, bass.AP(r.g2lc, 0, [[DC, 128], [1, DC], [0, CH]]),
                            ALU.mult)
            v.drain()
            v.tensor_tensor(yv, yv, bass.AP(r.b2lc, 0, [[DC, 128], [1, DC], [0, CH]]),
                            ALU.add)
        _step(p, "vector", [], v_gb2, ("drain", sDV))
        yn_done = sDV.inc()

        def v_ym(v, c=c, batch_start=batch_start):
            yv = ynt[:].rearrange("p (j t) -> p j t", j=DC)
            mv_ = ymt[:].rearrange("p (j t) -> p j t", j=DC)
            v.tensor_tensor(mv_[:, :, 1:CH], yv[:, :, 1:CH], yv[:, :, 0:CH - 1],
                            ALU.add)
            v.drain()
            if batch_start:
                v.tensor_copy(mv_[:, :, 0:1], yv[:, :, 0:1])
            else:
                v.tensor_tensor(mv_[:, :, 0:1], yv[:, :, 0:1],
                                ycar[:].rearrange("p (j o) -> p j o", o=1), ALU.add)
            v.drain()
            v.tensor_copy(ycar[:].rearrange("p (j o) -> p j o", o=1),
                          yv[:, :, CH - 1:CH])
        _step(p, "vector", [(sDV, yn_done)] + fw("ym") + fw("A"), v_ym, ("drain", sDV))
        ym_done = sDV.inc()

        for oc in range(DC):
            buf = oc % 2
            _step(p, "sync", fw(f"wts{buf}") + [(sWS[buf], sWS[buf].n)],
                  stream_w(buf, r.wfrB, oc, None), ("then", sWS[buf]), inck=16)
            ws_t = sWS[buf].inc(64)
            mb = 2 + buf
            for dc in range(DC):
                wl = [(sWS[buf], ws_t), (sDV, yn_done)] + (bw(mb) if dc == 0 else [])
                _step(p, "tensor", wl,
                      lambda t, mb=mb, buf=buf, dc=dc: t.matmul(
                          pb[mb][:, :], wts[buf][:, dc * 128:(dc + 1) * 128],
                          ynt[:, dc * CH:(dc + 1) * CH],
                          start=(dc == 0), stop=(dc == DC - 1)),
                      ("then", sMM))
                mm_t = sMM.inc()
            free[f"wts{buf}"] = (sMM, mm_t)
            _step(p, "scalar", [(sMM, mm_t), (sDV, rwkv_done)] + (fw("rs2") if oc == 0 else []),
                  lambda s, mb=mb, oc=oc: s.activation(
                      out=rs2[:, oc * CH:(oc + 1) * CH], in_=pb[mb][:, :],
                      func=AF.Sigmoid),
                  ("drain", sAC))
            bfree[mb] = (sAC, sAC.inc())
        rr_done = sAC.n
        free["yn"] = (sMM, sMM.n)

        free["ym"] = None
        for sub in range(CH // 256):
            toff = sub * 256
            kk_ready = {}
            for fs in range(FC + 1):
                if fs < FC:
                    buf = fs % 2
                    _step(p, "sync", fw(f"wts{buf}") + [(sWS[buf], sWS[buf].n)],
                          lambda s, fs=fs, buf=buf: _slab(
                              s, sWS[buf].h, wfks[buf][:], r.wfkB[fs]),
                          ("then", sWS[buf]), inck=16)
                    ws_t = sWS[buf].inc(64)
                    mb = 2 + buf
                    for dc in range(DC):
                        wl = [(sWS[buf], ws_t), (sDV, ym_done)] + (bw(mb) if dc == 0 else [])
                        _step(p, "tensor", wl,
                              lambda t, mb=mb, buf=buf, dc=dc, toff=toff: t.matmul(
                                  pb[mb][:, 0:256], wfks[buf][:, dc * 128:(dc + 1) * 128],
                                  ymt[:, dc * CH + toff:dc * CH + toff + 256],
                                  start=(dc == 0), stop=(dc == DC - 1)),
                              ("then", sMM))
                        mm_t = sMM.inc()
                    free[f"wts{buf}"] = (sMM, mm_t)

                    def v_relu(v, mb=mb, buf=buf):
                        v.tensor_relu(evsc[buf][:, 0:256], pb[mb][:, 0:256])
                    _step(p, "vector", [(sMM, mm_t)] + fw(f"ev{buf}"), v_relu,
                          ("drain", sDV))
                    bfree[mb] = (sDV, sDV.inc())
                    _step(p, "scalar", [(sDV, sDV.n)] + fw(f"kks{buf}"),
                          lambda s, buf=buf: s.activation(
                              out=kks[buf], in_=evsc[buf][:, 0:256],
                              func=AF.Square),
                          ("drain", sAC))
                    kk_ready[fs] = sAC.inc()
                    free[f"ev{buf}"] = (sAC, kk_ready[fs])
                if fs >= 1:
                    f2 = fs - 1
                    buf = f2 % 2
                    _step(p, "sync", fw(f"wfvs{buf}") + fw("rs") + [(sWV[buf], sWV[buf].n)],
                          lambda s, f2=f2, buf=buf: s.dma_start(
                              out=wfvs[buf],
                              in_=r.wfvB[f2 * 128:(f2 + 1) * 128, :]),
                          ("then", sWV[buf]), inck=16)
                    od_t = sWV[buf].inc(16)
                    for oc in range(DC):
                        wl = [(sAC, kk_ready[f2]), (sWV[buf], od_t)]
                        if f2 == 0 and oc == 0:
                            wl += fw("outst")
                        _step(p, "tensor", wl,
                              lambda t, f2=f2, buf=buf, oc=oc: t.matmul(
                                  pso[:, oc * 256:(oc + 1) * 256],
                                  wfvs[buf][:, oc * 128:(oc + 1) * 128], kks[buf],
                                  start=(f2 == 0), stop=(f2 == FC - 1),
                                  skip_group_check=True),
                              ("then", sMM))
                        mm_t2 = sMM.inc()
                    free[f"kks{buf}"] = (sMM, mm_t2)
                    free[f"wfvs{buf}"] = (sMM, mm_t2)
            cm_done = sMM.n

            def v_out(v, c=c, toff=toff):
                ov = outst[:].rearrange("p (j t) -> p j t", j=DC)
                xv = xap(c)[:, :, toff:toff + 256]
                nv = nap(c)[:, :, toff:toff + 256]
                rv = rs2[:].rearrange("p (j t) -> p j t", j=DC)[:, :, toff:toff + 256]
                pv = pso[:].rearrange("p (j t) -> p j t", j=DC)
                v.tensor_tensor(ov, pv, rv, ALU.mult)
                v.drain()
                v.tensor_tensor(ov, ov, xv, ALU.add)
                v.drain()
                v.tensor_tensor(ov, ov, nv, ALU.add)
            _step(p, "vector", [(sMM, cm_done), (sAC, rr_done), (sDV, xin_done)]
                  + fw("outst"), v_out, ("drain", sDV))
            out_done = sDV.inc()

            for tb in range(2):
                buf = tb % 2
                tok0 = c * CH + toff + tb * 128
                b_ = tok0 // SEQ
                trow = tok0 % SEQ
                for half in range(2):
                    _step(p, "tensor", [(sDV, out_done)] + bw(0),
                          lambda t, tb=tb, half=half: [t.matmul(
                              pb[0][:, j * 128:(j + 1) * 128],
                              outst[:, (half * 4 + j) * 256 + tb * 128:
                                    (half * 4 + j) * 256 + tb * 128 + 128],
                              r.idf[:], is_transpose=True, start=True,
                              stop=True).then_inc(sMM.h, 1) for j in range(4)][-1])
                    sMM.inc(4)
                    tr_t = sMM.n
                    _step(p, "vector", [(sMM, tr_t)] + (fw(f"tok{buf}") if half == 0 else []),
                          lambda v, buf=buf, half=half: v.tensor_copy(
                              tokst[buf][:, half * 512:(half + 1) * 512], pb[0][:, 0:512]),
                          ("drain", sDV))
                    bfree[0] = (sDV, sDV.inc())
                tok_done = sDV.n
                _step(p, "sync", [(sDV, tok_done), (sOD[buf], sOD[buf].n)],
                      lambda s, buf=buf, b_=b_, trow=trow: s.dma_start(
                          out=out[b_, trow:trow + 128, :], in_=tokst[buf][:]),
                      ("then", sOD[buf]), inck=16)
                free[f"tok{buf}"] = (sOD[buf], sOD[buf].inc(16))
            free["outst"] = (sMM, sMM.n)
            free["ek"] = (sMM, sMM.n)

        free["ym"] = (sMM, sMM.n)
        free["rs"] = (sMM, sMM.n)
        free["rs2"] = (sDV, sDV.n)
        free["mixb"] = [(sOD[0], sOD[0].n), (sOD[1], sOD[1].n)]

    def s_fin(s, f0=sOD[0].n, f1=sOD[1].n):
        s.wait_ge(sOD[0].h, f0)
        s.wait_ge(sOD[1].h, f1)
    p.on("sync", s_fin)



def emit_store_out(r, p, out):
    nc = r.nc
    ost = [r.sb_st(f"ost{i}", [128, D], F32) for i in range(2)]
    srcf = r.sb_st("srcf", [128, D], F32)
    ptf = r.st_st.enter_context(nc.psum_tensor("ptf_st", [128, 512], F32))
    s_c = r.sem("s_c")
    s_t2 = r.sem("s_t2")
    s_e2 = r.sem("s_e2")
    s_o2 = [r.sem("s_o2a"), r.sem("s_o2b")]

    ntiles = T // 128
    for i in range(ntiles):
        buf = i % 2
        b = (i * 128) // SEQ
        trow = (i * 128) % SEQ

        def v_gather(v, i=i, buf=buf, need=(16 * (i // 2)) if i >= 2 else 0):
            if i >= 2:
                v.wait_ge(s_o2[buf].h, need)
            src = bass.AP(r.x_fm, i * 128, [[DC * T, 128], [T, DC], [1, 128]])
            v.tensor_copy(srcf[:].rearrange("p (j c) -> p j c", j=DC), src)
            v.drain()
            v.sem_inc(s_c.h, 1)
        p.on("vector", v_gather)
        c_t = s_c.inc()

        for half in range(2):
            def t_tr(t, c_t=c_t, half=half, i=i):
                t.wait_ge(s_c.h, c_t)
                if i > 0 or half > 0:
                    t.wait_ge(s_e2.h, 2 * i + half)
                for j in range(4):
                    jj = half * 4 + j
                    t.matmul(ptf[:, j * 128:(j + 1) * 128],
                             srcf[:, jj * 128:(jj + 1) * 128], r.idf[:],
                             is_transpose=True, start=True, stop=True).then_inc(s_t2.h, 1)
            p.on("tensor", t_tr)
            t2_t = s_t2.inc(4)

            def v_ev(v, t2_t=t2_t, half=half, buf=buf):
                v.wait_ge(s_t2.h, t2_t)
                v.tensor_copy(ost[buf][:, half * 512:(half + 1) * 512], ptf[:, 0:512])
                v.drain()
                v.sem_inc(s_e2.h, 1)
            p.on("vector", v_ev)
            s_e2.inc()
        e2_t = s_e2.n

        def s_out(s, buf=buf, b=b, trow=trow, e2_t=e2_t):
            s.wait_ge(s_e2.h, e2_t)
            s.dma_start(out=out[b, trow:trow + 128, :], in_=ost[buf][:]).then_inc(s_o2[buf].h, 16)
        p.on("sync", s_out)
        s_o2[buf].inc(16)

    fin0, fin1 = s_o2[0].n, s_o2[1].n

    def s_fin(s, fin0=fin0, fin1=fin1):
        s.wait_ge(s_o2[0].h, fin0)
        s.wait_ge(s_o2[1].h, fin1)
    p.on("sync", s_fin)


def kernel(**inputs):
    xs = np.ascontiguousarray(inputs["L0_attn_x"])
    nc = build_nc()
    in_maps = []
    for c in range(8):
        m = {"x": xs[2 * c:2 * c + 2]}
        for k, v in inputs.items():
            if k == "L0_attn_x":
                continue
            a = np.ascontiguousarray(np.asarray(v))
            if k == "L1_ffn_act_scale":
                a = a.reshape(1)
            m[k] = a
        in_maps.append(m)
    res = run_bass_kernel_spmd(nc, in_maps, core_ids=list(range(8)))
    return np.concatenate([r_["out"] for r_ in res.results], axis=0)


# version 24 (final)
# speedup vs baseline: 1.0728x; 1.0595x over previous
import contextlib
import math
import numpy as np

import concourse.bass as bass
import concourse.mybir as mybir
from concourse.bass_utils import run_bass_kernel_spmd

F32 = mybir.dt.float32
BF16 = mybir.dt.bfloat16
I32 = mybir.dt.int32
AF = mybir.ActivationFunctionType
ALU = mybir.AluOpType

D = 1024
NB = 2
SEQ = 2048
T = NB * SEQ
CH = 512
NCH = T // CH
DC = D // 128
FF = 4096
FC = FF // 128
EPS = 1e-5
LOG127 = math.log(127.0)
DEV_PREP = False


class Sem:
    def __init__(self, handle):
        self.h = handle
        self.n = 0

    def inc(self, k=1):
        self.n += k
        return self.n


def _slab(s, sem_h, dst, src, nsp=4):
    stp = 128 // nsp
    last = None
    for i in range(nsp):
        last = s.dma_start(out=dst[i * stp:(i + 1) * stp, :],
                           in_=src[i * stp:(i + 1) * stp, :])
        if i < nsp - 1:
            last.then_inc(sem_h, 16)
    return last


class Prog:
    ENGINES = ("sync", "vector", "scalar", "tensor", "gpsimd")

    def __init__(self):
        self.steps = {e: [] for e in self.ENGINES}

    def on(self, eng, fn):
        self.steps[eng].append(fn)

    def run(self, block):
        prog = self

        @block.sync
        def _(e):
            for fn in prog.steps["sync"]:
                fn(e)

        @block.vector
        def _(e):
            for fn in prog.steps["vector"]:
                fn(e)

        @block.scalar
        def _(e):
            for fn in prog.steps["scalar"]:
                fn(e)

        @block.tensor
        def _(e):
            for fn in prog.steps["tensor"]:
                fn(e)

        @block.gpsimd
        def _(e):
            for fn in prog.steps["gpsimd"]:
                fn(e)


class R:
    pass


def build_nc():
    nc = bass.Bass("TRN2", target_bir_lowering=False, debug=False)
    r = R()
    r.nc = nc

    x_in = nc.dram_tensor("x", [NB, SEQ, D], F32, kind="ExternalInput").ap()
    out = nc.dram_tensor("out", [NB, SEQ, D], F32, kind="ExternalOutput").ap()
    W = {}
    for name, shape in [
        ("L0_attn_wq", [D, D]), ("L0_attn_bq", [D]), ("L0_attn_wu", [D, D]),
        ("L0_attn_bu", [D]), ("L0_ffn_w1", [FF, D]), ("L0_ffn_b1", [FF]),
        ("L0_ffn_w2", [D, FF]), ("L0_ffn_b2", [D]),
        ("L1_attn_ln1_g", [D]), ("L1_attn_ln1_b", [D]),
        ("L1_attn_Wk", [D, D]), ("L1_attn_Wv", [D, D]), ("L1_attn_Wr", [D, D]),
        ("L1_attn_Wo", [D, D]), ("L1_attn_time_decay", [D]),
        ("L1_attn_time_first", [D]), ("L1_attn_mix_k", [D]),
        ("L1_attn_mix_v", [D]), ("L1_attn_mix_r", [D]),
        ("L1_attn_ln2_g", [D]), ("L1_attn_ln2_b", [D]),
        ("L1_attn_Wfk", [D, FF]), ("L1_attn_Wfv", [FF, D]), ("L1_attn_Wfr", [D, D]),
        ("L1_ffn_bias", [D]), ("L1_ffn_act_scale", [1]),
    ]:
        W[name] = nc.dram_tensor(name, shape, F32, kind="ExternalInput").ap()
    W["L1_ffn_ternary_weight"] = nc.dram_tensor(
        "L1_ffn_ternary_weight", [D, D], I32, kind="ExternalInput").ap()
    r.W = W

    _ck = "ExternalOutput" if DEV_PREP else "Internal"
    r.w1T = nc.dram_tensor("w1T_c", [FC, 128, DC * 128], BF16, kind=_ck).ap()
    r.w2T = nc.dram_tensor("w2T_c", [DC, 128, FC * 128], BF16, kind=_ck).ap()
    r.wkB = nc.dram_tensor("wk_c", [DC, 128, DC * 128], BF16, kind=_ck).ap()
    r.wvB = nc.dram_tensor("wv_c", [DC, 128, DC * 128], BF16, kind=_ck).ap()
    r.wrB = nc.dram_tensor("wr_c", [DC, 128, DC * 128], BF16, kind=_ck).ap()
    r.woB = nc.dram_tensor("wo_c", [DC, 128, DC * 128], BF16, kind=_ck).ap()
    r.wfkB = nc.dram_tensor("wfk_c", [FC, 128, DC * 128], BF16, kind=_ck).ap()
    r.wfvB = nc.dram_tensor("wfv_c", [FF, D], BF16, kind=_ck).ap()
    r.wfrB = nc.dram_tensor("wfr_c", [DC, 128, DC * 128], BF16, kind=_ck).ap()
    r.wtT = nc.dram_tensor("wtT_c", [DC, 128, DC * 128], BF16, kind=_ck).ap()
    r.wqT = nc.dram_tensor("wqT_c", [DC, 128, DC * 128], BF16, kind=_ck).ap()
    r.wuT = nc.dram_tensor("wuT_c", [DC, 128, DC * 128], BF16, kind=_ck).ap()

    st = contextlib.ExitStack()
    nc._kernel_stack = st

    def sb(name, shape, dtype):
        return st.enter_context(nc.sbuf_tensor(name, shape, dtype))

    def mksem(name):
        return Sem(st.enter_context(nc.semaphore(name)))

    r.sb, r.sem, r.st = sb, mksem, st

    r.x_fm = sb("x_fm", [128, DC * T], BF16)
    r.idb = sb("idb", [128, 128], BF16)
    r.idf = sb("idf", [128, 128], F32)
    r.swapI = sb("swapI", [128, 128], BF16)
    r.ones_f_row1 = sb("ones_f_row1", [1, 128], F32)
    r.ones_f_col = sb("ones_f_col", [128, 1], F32)
    r.ones_bf_col = sb("ones_bf_col", [128, 1], BF16)
    r.epsc = sb("epsc", [128, 1], F32)
    r.l127 = sb("l127", [128, 1], F32)
    r.bqc = sb("bqc", [128, DC], F32)
    r.buc = sb("buc", [128, DC], F32)
    r.b1c = sb("b1c", [128, FC], F32)
    r.b2c = sb("b2c", [128, DC], F32)
    r.corr1c = sb("corr1c", [128, FC], F32)
    r.corr2c = sb("corr2c", [128, DC], F32)
    r.corrL1c = sb("corrL1c", [128, DC], F32)
    r.mw1 = sb("mw1", [1, 1], F32)
    r.mw2 = sb("mw2", [1, 1], F32)
    r.s1w_col = sb("s1w_col", [128, 1], F32)
    r.s2w_col = sb("s2w_col", [128, 1], F32)
    for nm in ("g1c", "b1lc", "mkc", "mvc", "mrc", "omkc", "omvc", "omrc",
               "tdc", "tfc", "ewc", "etfc", "g2lc", "b2lc", "bL1c"):
        setattr(r, nm, sb(nm, [128, DC], F32))
    r.s_col = sb("s_col", [128, 1], F32)
    r.sinv_col = sb("sinv_col", [128, 1], F32)
    r.deltac = sb("deltac", [128, 1], I32)
    r.offc = sb("offc", [128, 1], F32)
    r.offc64 = sb("offc64", [128, 1], F32)

    stackA = contextlib.ExitStack()
    r.sb_phase = lambda name, shape, dtype: stackA.enter_context(
        nc.sbuf_tensor(name, shape, dtype))
    r.st_phase = stackA
    with nc.Block() as blk:
        p = Prog()
        emit_consts(r, p)
        emit_load_x(r, p, x_in)
        emit_prep(r, p)
        p.run(blk)
    stackA.close()

    stackB = contextlib.ExitStack()
    psb = {"sb": lambda name, shape, dtype: stackB.enter_context(
        nc.sbuf_tensor(name, shape, dtype)), "st": stackB}
    with nc.Block() as blk:
        p = Prog()
        emit_l0(r, p, psb)
        p.run(blk)
    stackB.close()

    r.xn_buf = st.enter_context(nc.sbuf_tensor("xn_buf", [128, DC * T], BF16))
    stackC = contextlib.ExitStack()
    psc = {"sb": lambda name, shape, dtype: stackC.enter_context(
        nc.sbuf_tensor(name, shape, dtype)), "st": stackC}
    with nc.Block() as blk:
        p = Prog()
        emit_l1(r, p, psc, out)
        p.run(blk)
    stackC.close()

    nc.finalize()
    return nc



def emit_consts(r, p):
    nc = r.nc
    iop = r.sb_phase("iop_t", [128, 128], F32)
    iof = r.sb_phase("iof_t", [128, 128], F32)
    iot = r.sb_phase("iot_t", [128, 128], F32)
    s_id = r.sem("s_id")
    r.s_id = s_id

    def g_prep(g):
        g.iota(iop[:], [[0, 128]], channel_multiplier=1, allow_small_or_imprecise_dtypes=True)
        g.iota(iof[:], [[1, 128]], channel_multiplier=0, allow_small_or_imprecise_dtypes=True)
        g.memset(r.ones_f_row1[:], 1.0)
        g.memset(r.ones_f_col[:], 1.0)
        g.memset(r.ones_bf_col[:], 1.0)
        g.memset(r.epsc[:], EPS)
        g.memset(r.l127[:], LOG127)
        g.drain()
        g.sem_inc(s_id.h, 1)
    p.on("gpsimd", g_prep)
    s_id.inc()

    def v_ident(v, tgt=s_id.n):
        v.wait_ge(s_id.h, tgt)
        v.tensor_tensor(r.idb[:], iop[:], iof[:], ALU.is_equal)
        v.drain()
        v.tensor_copy(r.idf[:], r.idb[:])
        v.drain()
        v.tensor_scalar(iot[:], iop[:], 63.5, None, ALU.is_gt)
        v.drain()
        v.scalar_tensor_tensor(iop[:], iot[:], -128.0, iop[:], ALU.mult, ALU.add)
        v.drain()
        v.tensor_scalar(iop[:], iop[:], 64.0, None, ALU.add)
        v.drain()
        v.tensor_tensor(r.swapI[:], iop[:], iof[:], ALU.is_equal)
        v.drain()
        v.memset(r.offc[:], 0.75)
        v.drain()
        v.tensor_scalar(r.deltac[:], r.offc[:], 0.0, None, ALU.add)
        v.drain()
        v.tensor_scalar(r.offc[:], r.deltac[:], -0.5, 128.5, ALU.mult, ALU.add)
        v.tensor_scalar(r.offc64[:], r.deltac[:], -0.5, 64.5, ALU.mult, ALU.add)
        v.drain()
        v.sem_inc(s_id.h, 1)
    p.on("vector", v_ident)
    s_id.inc()
    r.consts_done = s_id.n



def emit_load_x(r, p, x_in):
    nc = r.nc
    xst = [r.sb_phase(f"xst{i}", [128, D], BF16) for i in range(2)]
    pt = r.st_phase.enter_context(nc.psum_tensor("pt_ld", [128, D], BF16))
    s_ld = [r.sem("s_ld0"), r.sem("s_ld1")]
    s_tr = r.sem("s_tr")
    s_ev = r.sem("s_ev")

    ntiles = T // 128
    for i in range(ntiles):
        buf = i % 2
        b = (i * 128) // SEQ
        trow = (i * 128) % SEQ

        def g_load(g, buf=buf, b=b, trow=trow, need=(8 * (i - 1)) if i >= 2 else 0):
            if need:
                g.wait_ge(s_tr.h, need)
            src = x_in[b, trow:trow + 128, :]
            for sp in range(4):
                g.dma_start(out=xst[buf][:, sp * 256:(sp + 1) * 256],
                            in_=src[:, sp * 256:(sp + 1) * 256]).then_inc(s_ld[buf].h, 16)
        p.on("gpsimd", g_load)
        ld_t = s_ld[buf].inc(64)

        def t_tr(t, buf=buf, ld_t=ld_t, i=i):
            t.wait_ge(s_ld[buf].h, ld_t)
            if i == 0:
                t.wait_ge(r.s_id.h, r.consts_done)
            if i > 0:
                t.wait_ge(s_ev.h, i)
            for j in range(DC):
                t.matmul(pt[:, j * 128:(j + 1) * 128], xst[buf][:, j * 128:(j + 1) * 128],
                         r.idb[:], is_transpose=True,
                         start=True, stop=True).then_inc(s_tr.h, 1)
        p.on("tensor", t_tr)
        s_tr.inc(8)
        tr_t = s_tr.n

        def v_ev(v, i=i, tr_t=tr_t):
            v.wait_ge(s_tr.h, tr_t)
            dst = bass.AP(r.x_fm, i * 128, [[DC * T, 128], [T, DC], [1, 128]])
            v.tensor_copy(dst, pt[:].rearrange("p (j c) -> p j c", j=DC)).then_inc(s_ev.h, 1)
        p.on("vector", v_ev)
        s_ev.inc()
    r.s_xload = s_ev
    r.xload_done = s_ev.n



def emit_prep(r, p):
    nc = r.nc
    st = r.st
    W = r.W
    wst = [r.sb_phase(f"wst{i}", [128, FF], F32) for i in range(2)] \
        + [r.sb_phase(f"wstS{i}", [128, D], F32) for i in range(2)]
    wsb = [r.sb_phase(f"wsb{i}", [128, FF], BF16) for i in range(2)] \
        + [r.sb_phase(f"wsbS{i}", [128, D], BF16) for i in range(2)]
    wib = r.sb_phase("wib", [128, FF], I32)
    wqs = r.sb_phase("wqs", [128, FF], F32)
    tpst = [[r.sb_phase(f"tpst_{b}_{j}", [128, 128], BF16) for j in range(8)]
            for b in range(4)]
    rowst = r.sb_phase("rowst", [FC, 128], F32)
    colsc = r.sb_phase("colsc", [128, 2], F32)
    rows = r.sb_phase("prep_rows", [1, 64], F32)
    pcol = r.st_phase.enter_context(nc.psum_tensor("pp_col", [128, 64], F32))
    ptp = [r.st_phase.enter_context(nc.psum_tensor(f"pp_tp{b}", [128, DC * 128], BF16))
           for b in range(4)]
    s_wl = [r.sem(f"s_wl{i}") for i in range(4)]
    s_wc = [r.sem(f"s_wc{i}") for i in range(4)]
    s_dv = r.sem("s_dv")
    s_tp = [r.sem(f"s_tp{i}") for i in range(4)]
    s_wo = [r.sem(f"s_wo{i}") for i in range(4)]
    s_go = [r.sem(f"s_go{i}") for i in range(4)]
    s_pm = r.sem("s_pm")
    s_td = r.sem("s_td")
    r.s_prep_done = s_td

    tile_no = [0]

    def tile_gate(eng, extra=0):
        k = tile_no[0]
        if k >= 2:
            eng.wait_ge(s_td.h, k - 1 - extra)

    NSP = 4

    def dma_split(eng, sem, dst, src, nsp=NSP):
        ncols = dst.shape[-1]
        step = ncols // nsp
        for i in range(nsp):
            eng.dma_start(out=dst[..., i * step:(i + 1) * step],
                          in_=src[..., i * step:(i + 1) * step]).then_inc(sem, 16)

    def emit_tp_grp(buf, srcs, wait_sem, wait_t, wo_prev=None):
        prev_tp = s_tp[buf].n
        srcs = tuple(srcs)

        def t_tp(t, buf=buf, srcs=srcs, wait_sem=wait_sem, wait_t=wait_t,
                 prev_tp=prev_tp):
            t.wait_ge(wait_sem.h, wait_t)
            if prev_tp:
                t.wait_ge(s_tp[buf].h, prev_tp)
            for j, src in enumerate(srcs):
                t.matmul(ptp[buf][:, j * 128:(j + 1) * 128], src, r.idb[:],
                         is_transpose=True, start=True, stop=True).then_inc(s_pm.h, 1)
        p.on("tensor", t_tp)
        pm_t = s_pm.inc(len(srcs))

        def v_tp(sc, buf=buf, n=len(srcs), pm_t=pm_t, wo_prev=wo_prev):
            sc.wait_ge(s_pm.h, pm_t)
            if wo_prev:
                sc.wait_ge(wo_prev[0].h, wo_prev[1])
            for j in range(n):
                sc.activation(out=tpst[buf][j][:],
                              in_=ptp[buf][:, j * 128:(j + 1) * 128], func=AF.Copy)
            sc.drain()
            sc.sem_inc(s_tp[buf].h, 16 * n)
        p.on("scalar", v_tp)
        return s_tp[buf].inc(16 * len(srcs))

    vec_jobs = [
        ("L0_attn_bq", r.bqc, DC), ("L0_attn_bu", r.buc, DC),
        ("L0_ffn_b1", r.b1c, FC), ("L0_ffn_b2", r.b2c, DC),
        ("L1_attn_ln1_g", r.g1c, DC), ("L1_attn_ln1_b", r.b1lc, DC),
        ("L1_attn_mix_k", r.mkc, DC), ("L1_attn_mix_v", r.mvc, DC),
        ("L1_attn_mix_r", r.mrc, DC), ("L1_attn_time_decay", r.tdc, DC),
        ("L1_attn_time_first", r.tfc, DC), ("L1_attn_ln2_g", r.g2lc, DC),
        ("L1_attn_ln2_b", r.b2lc, DC), ("L1_ffn_bias", r.bL1c, DC),
    ]
    for vi, (nm, dst, n) in enumerate(vec_jobs):
        def s_ldv(s, nm=nm, n=n, vi=vi, prev=s_dv.n):
            if vi > 0:
                s.wait_ge(s_dv.h, prev)
            s.dma_start(out=rowst[0:n, :],
                        in_=W[nm].rearrange("(a b) -> a b", b=128)).then_inc(s_wl[0].h, 16)
        p.on("sync", s_ldv)
        wl_t = s_wl[0].inc(16)

        def t_tp(t, wl_t=wl_t, n=n, vi=vi):
            t.wait_ge(s_wl[0].h, wl_t)
            if vi == 0:
                t.wait_ge(r.s_id.h, r.consts_done)
            t.matmul(pcol[:, 0:n], rowst[0:n, :], r.idf[0:n, 0:n],
                     is_transpose=True, start=True, stop=True).then_inc(s_pm.h, 1)
        p.on("tensor", t_tp)
        pm_t = s_pm.inc()

        def v_ev(v, pm_t=pm_t, dst=dst, n=n):
            v.wait_ge(s_pm.h, pm_t)
            v.tensor_copy(dst[:, 0:n], pcol[:, 0:n])
            v.drain()
            v.sem_inc(s_dv.h, 1)
        p.on("vector", v_ev)
        s_dv.inc()
    vdone = s_dv.n

    def sc_derived(sc, vdone=vdone):
        sc.wait_ge(s_dv.h, vdone)
        sc.activation(out=r.etfc[:], in_=r.tfc[:], func=AF.Exp)
        sc.activation(out=r.ewc[:], in_=r.tdc[:], func=AF.Exp)
        sc.drain()
        sc.activation(out=r.ewc[:], in_=r.ewc[:], func=AF.Exp, scale=-1.0)
        sc.drain()
        sc.sem_inc(s_dv.h, 1)
    p.on("scalar", sc_derived)
    s_dv.inc()

    def v_derived(v, vdone=vdone):
        v.wait_ge(s_dv.h, vdone)
        for m, om in ((r.mkc, r.omkc), (r.mvc, r.omvc), (r.mrc, r.omrc)):
            v.tensor_scalar(om[:], m[:], -1.0, 1.0, ALU.mult, ALU.add)
        v.drain()
        v.sem_inc(s_dv.h, 1)
    p.on("vector", v_derived)
    s_dv.inc()

    def s_lds(s, prev=s_dv.n):
        s.wait_ge(s_dv.h, prev)
        s.dma_start(out=rows[0:1, 0:1], in_=W["L1_ffn_act_scale"][None, 0:1]).then_inc(s_wl[1].h, 16)
    p.on("sync", s_lds)
    wl_t = s_wl[1].inc(16)

    def v_scol(v, wl_t=wl_t):
        v.wait_ge(s_wl[1].h, wl_t)
        v.tensor_scalar(rows[0:1, 1:2], rows[0:1, 0:1], 1e-5, None, ALU.max)
        v.drain()
        v.reciprocal(rows[0:1, 2:3], rows[0:1, 1:2])
        v.drain()
        v.sem_inc(s_dv.h, 1)
    p.on("vector", v_scol)
    sd_t = s_dv.inc()

    def t_scol(t, sd_t=sd_t):
        t.wait_ge(s_dv.h, sd_t)
        t.matmul(pcol[:, 8:9], r.ones_f_row1[:], rows[0:1, 1:2], start=True, stop=True)
        t.matmul(pcol[:, 9:10], r.ones_f_row1[:], rows[0:1, 2:3],
                 start=True, stop=True).then_inc(s_pm.h, 2)
    p.on("tensor", t_scol)
    pm_t = s_pm.inc(2)

    def v_scol2(v, pm_t=pm_t):
        v.wait_ge(s_pm.h, pm_t)
        v.tensor_copy(r.s_col[:], pcol[:, 8:9])
        v.tensor_copy(r.sinv_col[:], pcol[:, 9:10])
        v.drain()
        v.sem_inc(s_dv.h, 1)
    p.on("vector", v_scol2)
    s_dv.inc()

    for wi, (nm, dstT) in enumerate([("L0_attn_wq", r.wqT), ("L0_attn_wu", r.wuT)]):
        for oc in range(DC):
            buf = (wi * DC + oc) % 4
            k = tile_no[0]

            def g_cast(g, nm=nm, oc=oc, buf=buf, k=k):
                if k >= 4:
                    g.wait_ge(s_td.h, k - 3)
                dma_split(g, s_wc[buf].h, wsb[buf][:, 0:D],
                          W[nm][oc * 128:(oc + 1) * 128, :])
            p.on("gpsimd", g_cast)
            wc_t = s_wc[buf].inc(16 * NSP)

            tp_t = emit_tp_grp(buf, [wsb[buf][:, dc * 128:(dc + 1) * 128]
                                     for dc in range(DC)], s_wc[buf], wc_t)

            def s_wr(s, tp_t=tp_t, buf=buf, dstT=dstT, oc=oc):
                s.wait_ge(s_tp[buf].h, tp_t)
                for dc in range(DC):
                    s.dma_start(out=dstT[oc, :, dc * 128:(dc + 1) * 128],
                                in_=tpst[buf][dc][:]).then_inc(s_wo[buf].h, 16)
            p.on("sync", s_wr)
            wo_t = s_wo[buf].inc(16 * DC)

            def s_done(s, buf=buf, wo_t=wo_t):
                s.wait_ge(s_wo[buf].h, wo_t)
                s.sem_inc(s_td.h, 1)
            p.on("sync", s_done)
            s_td.inc()
            tile_no[0] += 1

    for wi, (nm, n_t, hbmT, corr, mwrow, swcol, rb) in enumerate([
            ("L0_ffn_w1", FC, r.w1T, r.corr1c, r.mw1, r.s1w_col, 16),
            ("L0_ffn_w2", DC, r.w2T, r.corr2c, r.mw2, r.s2w_col, 24)]):
        wsrc = W[nm]
        ncols = wsrc.shape[1]
        nel = float(n_t * 128 * ncols)
        nbuf = 4 if ncols == D else 2
        abs_toks = []
        p1_prev = s_dv.n
        for ti in range(n_t):
            buf = ti % nbuf

            def s_ld(s, ti=ti, buf=buf, wsrc=wsrc, ncols=ncols, p1_prev=p1_prev,
                     war=(abs_toks[ti - nbuf] if ti >= nbuf else None)):
                s.wait_ge(s_dv.h, p1_prev if war is None else war)
                dma_split(s, s_wl[buf].h, wst[buf][:, 0:ncols],
                          wsrc[ti * 128:(ti + 1) * 128, :])
            p.on("sync", s_ld)
            wl_t = s_wl[buf].inc(16 * NSP)

            def v_abs(v, wl_t=wl_t, ti=ti, buf=buf, ncols=ncols):
                v.wait_ge(s_wl[buf].h, wl_t)
                if ti == 0:
                    v.memset(colsc[:, 0:1], 0.0)
                    v.drain()
                absb = wib[:].bitcast(F32)
                v.tensor_scalar(absb[:, 0:ncols].bitcast(I32),
                                wst[buf][:, 0:ncols].bitcast(I32),
                                0x7FFFFFFF, None, ALU.bitwise_and)
                v.drain()
                v.reduce_sum(colsc[:, 1:2], absb[:, 0:ncols], axis=mybir.AxisListType.X)
                v.drain()
                v.tensor_tensor(colsc[:, 0:1], colsc[:, 0:1], colsc[:, 1:2], ALU.add)
                v.drain()
                v.sem_inc(s_dv.h, 1)
            p.on("vector", v_abs)
            abs_toks.append(s_dv.inc())
        absdone = s_dv.n

        def t_tot(t, absdone=absdone, rb=rb):
            t.wait_ge(s_dv.h, absdone)
            t.matmul(pcol[0:1, rb:rb + 1], r.ones_f_col[:], colsc[:, 0:1],
                     start=True, stop=True).then_inc(s_pm.h, 1)
        p.on("tensor", t_tot)
        pm_t = s_pm.inc()

        def v_scale(v, pm_t=pm_t, nel=nel, mwrow=mwrow, rb=rb):
            v.wait_ge(s_pm.h, pm_t)
            v.tensor_scalar(rows[0:1, rb + 1:rb + 2], pcol[0:1, rb:rb + 1],
                            1.0 / nel, 1e-5, ALU.mult, ALU.max)
            v.drain()
            v.tensor_copy(mwrow[0:1, 0:1], rows[0:1, rb + 1:rb + 2])
            v.reciprocal(rows[0:1, rb + 2:rb + 3], rows[0:1, rb + 1:rb + 2])
            v.drain()
            v.sem_inc(s_dv.h, 1)
        p.on("vector", v_scale)
        sc_t = s_dv.inc()

        def t_bc(t, sc_t=sc_t, rb=rb):
            t.wait_ge(s_dv.h, sc_t)
            t.matmul(pcol[:, rb + 3:rb + 4], r.ones_f_row1[:],
                     rows[0:1, rb + 2:rb + 3], start=True, stop=True).then_inc(s_pm.h, 1)
        p.on("tensor", t_bc)
        pm_t = s_pm.inc()

        def v_swcol(v, pm_t=pm_t, swcol=swcol, rb=rb):
            v.wait_ge(s_pm.h, pm_t)
            v.tensor_copy(swcol[:], pcol[:, rb + 3:rb + 4])
            v.drain()
            v.sem_inc(s_dv.h, 1)
        p.on("vector", v_swcol)
        s_dv.inc()

        pass1_done = s_dv.n
        for ti in range(n_t):
            buf = ti % nbuf
            k = tile_no[0]

            def s_ld(s, ti=ti, buf=buf, wsrc=wsrc, ncols=ncols, k=k, p1=pass1_done,
                     nbuf=nbuf):
                if ti < nbuf:
                    s.wait_ge(s_dv.h, p1)
                gn = k - (3 if nbuf == 4 else 1)
                if k >= 2 and gn >= 1:
                    s.wait_ge(s_td.h, gn)
                dma_split(s, s_wl[buf].h, wst[buf][:, 0:ncols],
                          wsrc[ti * 128:(ti + 1) * 128, :])
            p.on("sync", s_ld)
            wl_t = s_wl[buf].inc(16 * NSP)

            def v_q(v, wl_t=wl_t, buf=buf, ncols=ncols, swcol=swcol, ti=ti, corr=corr):
                v.wait_ge(s_wl[buf].h, wl_t)
                v.tensor_scalar(wqs[:, 0:ncols], wst[buf][:, 0:ncols], swcol[:, 0:1], None,
                                ALU.mult)
                v.drain()
                v.tensor_scalar(wib[:, 0:ncols], wqs[:, 0:ncols], r.offc64[:, 0:1], None,
                                ALU.add)
                v.drain()
                v.tensor_scalar(wib[:, 0:ncols], wib[:, 0:ncols], 63.0, 65.0, ALU.max, ALU.min)
                v.drain()
                v.tensor_scalar(wsb[buf][:, 0:ncols], wib[:, 0:ncols], -64.0, None, ALU.add)
                v.drain()
                v.reduce_sum(colsc[:, 1:2], wsb[buf][:, 0:ncols], axis=mybir.AxisListType.X)
                v.drain()
                v.tensor_scalar(corr[:, ti:ti + 1], colsc[:, 1:2], -128.0, None, ALU.mult)
                v.drain()
                v.sem_inc(s_dv.h, 1)
            p.on("vector", v_q)
            q_t = s_dv.inc()

            ngrp = ncols // 128 // 8
            for gi in range(ngrp):
                tp_t = emit_tp_grp(
                    buf, [wsb[buf][:, (gi * 8 + j) * 128:(gi * 8 + j + 1) * 128]
                          for j in range(8)], s_dv, q_t,
                    wo_prev=((s_go[buf], s_go[buf].n) if gi > 0 else None))

                def s_wr(g, tp_t=tp_t, buf=buf, gi=gi, hbmT=hbmT, ti=ti):
                    g.wait_ge(s_tp[buf].h, tp_t)
                    for j in range(8):
                        jj = gi * 8 + j
                        g.dma_start(out=hbmT[ti, :, jj * 128:(jj + 1) * 128],
                                    in_=tpst[buf][j][:]).then_inc(s_go[buf].h, 16)
                p.on("gpsimd", s_wr)
                s_go[buf].inc(16 * 8)
            wo_t = s_go[buf].n

            def s_done(g, buf=buf, wo_t=wo_t):
                g.wait_ge(s_go[buf].h, wo_t)
                g.sem_inc(s_td.h, 1)
            p.on("gpsimd", s_done)
            s_td.inc()
            tile_no[0] += 1

    cast_jobs = [
        ("L1_attn_Wfk", r.wfkB, 0.5),
    ]
    for nm, hbm, scale in cast_jobs:
        ntl = W[nm].shape[0] // 128
        ncols = W[nm].shape[1]
        for ti in range(ntl):
            buf = ti % 2
            k = tile_no[0]

            def g_cast(g, nm=nm, ti=ti, buf=buf, ncols=ncols, k=k):
                if k >= 2:
                    g.wait_ge(s_td.h, k - 1)
                dma_split(g, s_wc[buf].h, wsb[buf][:, 0:ncols],
                          W[nm][ti * 128:(ti + 1) * 128, :])
            p.on("gpsimd", g_cast)
            wc_t = s_wc[buf].inc(16 * NSP)
            dv_t = None
            if scale != 1.0:
                def v_s(v, wc_t=wc_t, buf=buf, ncols=ncols, scale=scale):
                    v.wait_ge(s_wc[buf].h, wc_t)
                    v.tensor_scalar(wsb[buf][:, 0:ncols], wsb[buf][:, 0:ncols], scale, None,
                                    ALU.mult)
                    v.drain()
                    v.sem_inc(s_dv.h, 1)
                p.on("vector", v_s)
                dv_t = s_dv.inc()

            def s_wr(s, wc_t=wc_t, buf=buf, ncols=ncols, hbm=hbm, ti=ti, dv_t=dv_t):
                if dv_t is None:
                    s.wait_ge(s_wc[buf].h, wc_t)
                else:
                    s.wait_ge(s_dv.h, dv_t)
                if hbm is r.wfvB:
                    dma_split(s, s_wo[buf].h, hbm[ti * 128:(ti + 1) * 128, :],
                              wsb[buf][:, 0:ncols])
                else:
                    nbl = ncols // 128
                    srcv = wsb[buf][:, 0:ncols].rearrange("q (blk f) -> q blk f", f=128)
                    stp = nbl // NSP
                    for i in range(NSP):
                        dstv = hbm[i * stp:(i + 1) * stp, :,
                                   ti * 128:(ti + 1) * 128].rearrange(
                                       "blk q f -> q blk f")
                        s.dma_start(out=dstv, in_=srcv[:, i * stp:(i + 1) * stp, :]
                                    ).then_inc(s_wo[buf].h, 16)
            p.on("sync", s_wr)
            wo_t = s_wo[buf].inc(16 * NSP)

            def s_done(s, buf=buf, wo_t=wo_t):
                s.wait_ge(s_wo[buf].h, wo_t)
                s.sem_inc(s_td.h, 1)
            p.on("sync", s_done)
            s_td.inc()
            tile_no[0] += 1

    for ti in range(DC):
        buf = ti % 2
        k = tile_no[0]

        def g_cast(g, ti=ti, buf=buf, k=k):
            if k >= 2:
                g.wait_ge(s_td.h, k - 1)
            dma_split(g, s_wc[buf].h, wsb[buf][:, 0:D],
                      r.W["L1_ffn_ternary_weight"][ti * 128:(ti + 1) * 128, :])
        p.on("gpsimd", g_cast)
        wc_t = s_wc[buf].inc(16 * NSP)

        def v_m1(v, wc_t=wc_t, buf=buf, ti=ti):
            v.wait_ge(s_wc[buf].h, wc_t)
            v.tensor_scalar(wsb[buf][:, 0:D], wsb[buf][:, 0:D], -1.0, None, ALU.add)
            v.drain()
            v.reduce_sum(colsc[:, 1:2], wsb[buf][:, 0:D], axis=mybir.AxisListType.X)
            v.drain()
            v.tensor_scalar(r.corrL1c[:, ti:ti + 1], colsc[:, 1:2], -128.0, None, ALU.mult)
            v.drain()
            v.sem_inc(s_dv.h, 1)
        p.on("vector", v_m1)
        dv_t = s_dv.inc()

        tp_t = emit_tp_grp(buf, [wsb[buf][:, j * 128:(j + 1) * 128]
                                 for j in range(DC)], s_dv, dv_t)

        def s_wr(s, tp_t=tp_t, buf=buf, ti=ti):
            s.wait_ge(s_tp[buf].h, tp_t)
            for j in range(DC):
                s.dma_start(out=r.wtT[ti, :, j * 128:(j + 1) * 128],
                            in_=tpst[buf][j][:]).then_inc(s_wo[buf].h, 16)
        p.on("sync", s_wr)
        wo_t = s_wo[buf].inc(16 * DC)

        def s_done(s, buf=buf, wo_t=wo_t):
            s.wait_ge(s_wo[buf].h, wo_t)
            s.sem_inc(s_td.h, 1)
        p.on("sync", s_done)
        s_td.inc()
        tile_no[0] += 1

    s_lc = r.sem("s_lc")

    def g_hbm_casts(g):
        for nm, hbm in [("L1_attn_Wk", r.wkB), ("L1_attn_Wv", r.wvB),
                        ("L1_attn_Wr", r.wrB), ("L1_attn_Wo", r.woB),
                        ("L1_attn_Wfr", r.wfrB)]:
            src = W[nm].rearrange("(dc q) (blk f) -> dc q blk f", q=128, f=128)
            dst = hbm.rearrange("blk q (dc f) -> dc q blk f", f=128)
            for i in range(DC):
                g.dma_start(out=dst[i], in_=src[i]).then_inc(s_lc.h, 16)
        for i in range(8):
            g.dma_start(out=r.wfvB[i * 512:(i + 1) * 512, :],
                        in_=W["L1_attn_Wfv"][i * 512:(i + 1) * 512, :]
                        ).then_inc(s_lc.h, 16)
    p.on("gpsimd", g_hbm_casts)
    r.l1_casts = (s_lc, s_lc.inc(16 * 48))

    r.prep_tiles_done = s_td.n
    r.s_dv_done = s_dv.n
    r.s_dv = s_dv



def _step(p, eng, waits, fn, inc=None, inck=1):
    def go(e, waits=tuple(waits), fn=fn, inc=inc, inck=inck):
        for s, n in waits:
            if n > 0:
                e.wait_ge(s.h, n)
        ret = fn(e)
        if inc is not None:
            mode, s = inc
            if mode == "then":
                ret.then_inc(s.h, inck)
            else:
                e.drain()
                e.sem_inc(s.h, inck)
    p.on(eng, go)


def emit_l0(r, p, psb):
    nc = r.nc
    sb = psb["sb"]
    st = psb["st"]
    xf = r.x_fm

    r.e16 = None
    r.ones_bf_row = sb("l0_ones_bf_row", [1, CH], BF16)
    r.e2 = sb("l0_e2", [128, 2], BF16)
    r.elo = sb("l0_elo", [1, 128], F32)
    r.ehi = sb("l0_ehi", [1, 128], F32)
    scr = sb("l0_scr", [128, DC * CH], F32)
    intq2 = [sb(f"l0_intq{i}", [128, CH], I32) for i in range(2)]
    sqs = [sb(f"l0_sqs{i}", [128, CH], F32) for i in range(2)]
    sq2 = [sb(f"l0_sq2{i}", [128, CH], F32) for i in range(2)]
    sqp = [sb(f"l0_sqp{i}", [128, CH], BF16) for i in range(2)]
    g2keep = sb("l0_g2k", [1, CH], F32)
    xnq = sb("l0_xnq", [128, DC * CH], BF16)
    qaf = sb("l0_qaf", [128, DC * CH], BF16)
    _qpk0 = sb("l0_qpk", [128, 2048], BF16)
    qpk = [_qpk0, _qpk0]
    vt = [sb(f"l0_vt{i}", [128, 1024], BF16) for i in range(2)]
    e_sb = [sb(f"l0_e{i}", [128, 2048], BF16) for i in range(2)]
    hbf = sb("l0_h", [128, FC * CH], BF16)
    w1s = [sb(f"l0_w1s{i}", [128, DC * 128], BF16) for i in range(2)]
    w2s = sb("l0_w2s", [128, FC * 128], BF16)
    wqs = [sb(f"l0_wqs{i}", [128, DC * 128], BF16) for i in range(2)]
    sA_sb = sb("l0_sA", [128, CH], F32)
    sB_sb = sb("l0_sB", [128, CH], F32)
    div_sb = sb("l0_div", [128, CH], F32)
    maxacc = sb("l0_maxacc", [128, CH], F32)
    mxcol = sb("l0_mxcol", [128, 4], F32)
    trst = sb("l0_trst", [16, 128], F32)
    m2acc = sb("l0_m2acc", [2, 8], F32)
    m41 = sb("l0_m41", [4, 1], F32)
    mrow2 = sb("l0_mrow2", [1, 4], F32)
    mcols = sb("l0_mcols", [128, 4], F32)
    zc4 = [sb(f"l0_zc4{i}", [128, 32], F32) for i in range(2)]
    z16 = sb("l0_z16", [128, 16], F32)
    zrow = [scr[0:1, 0:2048], scr[0:1, 2048:4096]]
    rows = sb("l0_rows", [1, 9 * CH], F32)
    nl127 = sb("l0_nl127", [128, 1], F32)

    def rw(i):
        return rows[0:1, i * CH:(i + 1) * CH]
    MEAN, M2, VAR, LNV, RSTD, R2I, G2, HMAX, C1 = 0, 1, 2, 3, 4, 5, 6, 7, 8
    U = 1
    MX = 2
    LNH = 5
    S1 = 3
    SAr = 1
    SSQ, LNV2, RSTD2, MXH2, LNM, LNHN, S2r, C2r, SBr = range(9)

    psA = st.enter_context(nc.psum_tensor("l0_psA", [128, 4 * CH], F32))
    psB = st.enter_context(nc.psum_tensor("l0_psB", [128, 4 * CH], F32))
    pb = [psA[:, 0:CH], psA[:, CH:2 * CH], psB[:, 0:CH], psB[:, CH:2 * CH],
          psA[:, 2 * CH:3 * CH], psA[:, 3 * CH:4 * CH],
          psB[:, 2 * CH:3 * CH], psB[:, 3 * CH:4 * CH]]
    pbAV = [pb[2], pb[3], pb[6], pb[7]]
    scoresF = psA[:, 0:4 * CH]
    bfree = {i: None for i in range(8)}

    def bw(i):
        return [bfree[i]] if bfree[i] else []

    sMM = r.sem("l0_mm")
    sDV = r.sem("l0_dv")
    sAC = r.sem("l0_ac")
    sWS = [r.sem("l0_ws0"), r.sem("l0_ws1")]
    sW2 = r.sem("l0_w2")
    sW2h = [r.sem("l0_w2a"), r.sem("l0_w2b")]
    sGA = r.sem("l0_ga")
    sGB = r.sem("l0_gb")
    sRD = r.sem("l0_rd")
    free = {}
    for nm in ("w1s0", "w1s1", "w2s", "wqs0", "wqs1", "qpk", "vt0", "vt1",
               "e0", "e1", "sqs0", "sqs1", "sq20", "sq21", "scr", "xnq", "qaf",
               "zc40", "zc41", "hbf", "maxacc", "trst", "sqp0", "sqp1",
               "w2h0", "w2h1"):
        free[nm] = None

    def fw(nm):
        f = free[nm]
        if f is None:
            return []
        return list(f) if isinstance(f, list) else [f]

    sBR = r.sem("l0_br")
    sBC = r.sem("l0_bc")

    def g_brows(g):
        g.memset(r.ones_bf_row[:], 1.0)
        g.memset(r.e2[:], 0.0)
        g.memset(r.elo[:], 0.0)
        g.memset(r.ehi[:], 0.0)
        g.drain()
        g.memset(r.e2[0:64, 0:1], 1.0)
        g.memset(r.e2[64:128, 1:2], 1.0)
        g.memset(r.elo[0:1, 0:64], 1.0)
        g.memset(r.ehi[0:1, 64:128], 1.0)
        g.drain()
        g.sem_inc(sBC.h, 1)
    p.on("gpsimd", g_brows)
    sBC.inc(1)
    r.bconsts_done = (sBC, sBC.n)

    def xap(c, dc=None):
        if dc is None:
            return bass.AP(xf, c * CH, [[DC * T, 128], [T, DC], [1, CH]])
        return bass.AP(xf, dc * T + c * CH, [[DC * T, 128], [1, CH]])

    def pbcast(bank, ndc=DC):
        return bank.rearrange("p (o t) -> p o t", o=1).broadcast_to((128, ndc, CH))

    ONE_CONST = [False]

    P_MEAN, P_VAR, P_LNV, P_RSTD = 0, 1, 2, 3

    def rw2(i):
        return scr[0:1, i * CH:(i + 1) * CH]

    def emit_stats(cc, g2_war):
        for dc in range(DC):
            buf = dc % 2
            _step(p, "vector", fw(f"sqp{buf}") + (fw("scr") if dc == 0 else []),
                  lambda v, cc=cc, dc=dc, buf=buf: v.tensor_tensor(
                      sqp[buf][:], xap(cc, dc), xap(cc, dc), ALU.mult),
                  ("drain", sDV))
            dv_t = sDV.inc()
            _step(p, "tensor", [(sDV, dv_t)] + (bw(1) if dc == 0 else []),
                  lambda t, buf=buf, dc=dc: t.matmul(
                      pb[1][0:1, :], r.ones_bf_col[:], sqp[buf][:],
                      start=(dc == 0), stop=(dc == DC - 1)),
                  ("then", sMM))
            free[f"sqp{buf}"] = (sMM, sMM.inc())
        s2_done = sMM.n
        for dc in range(DC):
            wl = (bw(0) + [(r.s_xload, r.xload_done)]) if dc == 0 else []
            _step(p, "tensor", wl,
                  lambda t, cc=cc, dc=dc: t.matmul(
                      pb[0][0:1, :], r.ones_bf_col[:], xap(cc, dc),
                      start=(dc == 0), stop=(dc == DC - 1)),
                  ("then", sMM))
            s1_done = sMM.inc()
        _step(p, "vector", [(sMM, s1_done)] + fw("scr"),
              lambda v: v.tensor_scalar(rw2(P_MEAN), pb[0][0:1, :], 1.0 / D,
                                        None, ALU.mult),
              ("drain", sDV))
        t_mean = sDV.inc()
        bfree[0] = (sDV, t_mean)
        _step(p, "scalar", [(sDV, t_mean)],
              lambda s: s.activation(out=rw2(P_LNV), in_=rw2(P_MEAN), func=AF.Square),
              ("drain", sAC))
        m2_t = sAC.inc()
        _step(p, "vector", [(sAC, m2_t), (sMM, s2_done)],
              lambda v: v.scalar_tensor_tensor(rw2(P_VAR), pb[1][0:1, :], 1.0 / D,
                                               rw2(P_LNV), ALU.mult, ALU.subtract),
              ("drain", sDV))
        var_t = sDV.inc()
        bfree[1] = (sDV, var_t)

        def sc_st1(s):
            s.activation(out=rw2(P_LNV), in_=rw2(P_VAR), func=AF.Ln,
                         bias=r.epsc[0:1, 0:1])
            s.drain()
            s.activation(out=rw2(P_RSTD), in_=rw2(P_LNV), func=AF.Exp, scale=-0.5)
            s.activation(out=g2keep[:], in_=rw2(P_LNV), func=AF.Exp, scale=-1.0)
        _step(p, "scalar", [(sDV, var_t)] + ([g2_war] if g2_war else []),
              sc_st1, ("drain", sAC))
        t_mr = sAC.inc()
        _step(p, "vector", [(sAC, t_mr)],
              lambda v: v.tensor_tensor(rw2(P_VAR), rw2(P_VAR), g2keep[:], ALU.mult),
              ("drain", sDV))
        u_t = sDV.inc()

        def sc_st2(s):
            s.activation(out=rw2(P_LNV), in_=rw2(P_VAR), func=AF.Ln,
                         bias=r.epsc[0:1, 0:1])
            s.drain()
            s.activation(out=g2keep[:], in_=rw2(P_LNV), func=AF.Exp, scale=-0.5)
        _step(p, "scalar", [(sDV, u_t)], sc_st2, ("drain", sAC))
        t_g2 = sAC.inc()
        return (t_mean, t_mr, t_g2)

    pst = None
    for c in range(NCH):
        if c == 0:
            _step(p, "vector", [(r.s_id, r.consts_done)],
                  lambda v: v.tensor_scalar(nl127[:], r.l127[:], -1.0, None, ALU.mult),
                  ("drain", sDV))
            sDV.inc()
            pst = emit_stats(0, None)
        t_mean, t_mr, t_g2 = pst
        _step(p, "tensor", bw(0) + bw(1) + [(sAC, t_mr), (sDV, t_mean)],
              lambda t: (t.matmul(pb[0][:, :], r.ones_f_row1[:], rw2(P_MEAN),
                                  start=True, stop=True).then_inc(sMM.h, 1),
                         t.matmul(pb[1][:, :], r.ones_f_row1[:], rw2(P_RSTD),
                                  start=True, stop=True))[1],
              ("then", sMM))
        sMM.inc(2)
        bc_done = sMM.n
        _step(p, "vector", [(sMM, bc_done), (sAC, t_g2)] + fw("scr"),
              lambda v, c=c: v.tensor_tensor(
                  scr[:].rearrange("p (j t) -> p j t", j=DC), xap(c), pbcast(pb[0]),
                  ALU.subtract),
              ("drain", sDV))
        sDV.inc()
        _step(p, "vector", [],
              lambda v: v.tensor_tensor(
                  scr[:].rearrange("p (j t) -> p j t", j=DC),
                  scr[:].rearrange("p (j t) -> p j t", j=DC), pbcast(pb[1]), ALU.mult),
              ("drain", sDV))
        xn32_done = sDV.inc()
        bfree[0] = (sDV, xn32_done)
        bfree[1] = (sDV, xn32_done)
        _step(p, "vector", fw("xnq"),
              lambda v: v.tensor_copy(xnq[:], scr[:]),
              ("drain", sDV))
        xn_done = sDV.inc()

        qev = []
        for oc in range(DC):
            buf = oc % 2
            _step(p, "sync", fw(f"wqs{buf}") + [(sWS[buf], sWS[buf].n)],
                  lambda s, oc=oc, buf=buf: _slab(
                      s, sWS[buf].h, wqs[buf][:], r.wqT[oc]),
                  ("then", sWS[buf]), inck=16)
            ws_t = sWS[buf].inc(64)
            mb = 4 + (oc % 2)
            for dc in range(DC):
                wl = ([(sWS[buf], ws_t), (sDV, xn_done)] + bw(mb)) if dc == 0 else []
                _step(p, "tensor", wl,
                      lambda t, mb=mb, buf=buf, dc=dc: t.matmul(
                          pb[mb][:, :], wqs[buf][:, dc * 128:(dc + 1) * 128],
                          xnq[:, dc * CH:(dc + 1) * CH], start=(dc == 0),
                          stop=(dc == DC - 1)),
                      ("then", sMM))
                mm_t = sMM.inc()
            free[f"wqs{buf}"] = (sMM, mm_t)
            _step(p, "vector", [(sMM, mm_t)] + (fw("qaf") if oc == 0 else []),
                  lambda v, mb=mb, oc=oc: v.tensor_scalar(
                      qaf[:, oc * CH:(oc + 1) * CH], pb[mb][:, :],
                      r.bqc[:, oc:oc + 1], None, ALU.add),
                  ("drain", sDV))
            bfree[mb] = (sDV, sDV.inc())
            qev.append(sDV.n)
        qproj_done = sDV.n

        hf32 = hbf[:].bitcast(F32)
        _step(p, "vector", fw("hbf"),
              lambda v, hf32=hf32: v.tensor_scalar(
                  hf32[:, 0:DC * CH].bitcast(I32), scr[:].bitcast(I32),
                  0x7FFFFFFF, None, ALU.bitwise_and),
              ("drain", sDV))
        sDV.inc()

        def v_tree(v, hf32=hf32):
            af = hf32[:, 0:DC * CH].rearrange("p (j t) -> p j t", j=DC)
            v.tensor_tensor(af[:, 0:4, :], af[:, 0:4, :], af[:, 4:8, :], ALU.max)
            v.drain()
            v.tensor_tensor(af[:, 0:2, :], af[:, 0:2, :], af[:, 2:4, :], ALU.max)
            v.drain()
            v.tensor_tensor(af[:, 0:1, :], af[:, 0:1, :], af[:, 1:2, :], ALU.max)
        _step(p, "vector", [], v_tree, ("drain", sDV))
        tree_done = sDV.inc()
        for j in range(4):
            _step(p, "tensor", [(sDV, tree_done)] + (bw(2) if j == 0 else []),
                  lambda t, j=j, hf32=hf32: t.matmul(
                      pb[2][:, j * 128:(j + 1) * 128],
                      hf32[:, j * 128:(j + 1) * 128], r.idf[:],
                      is_transpose=True, start=True, stop=True),
                  ("then", sMM))
            sMM.inc()
        tr_done = sMM.n
        free["hbf"] = (sMM, tr_done)

        def v_mxred(v):
            for j in range(4):
                v.reduce_max(mxcol[:, j:j + 1], pb[2][:, j * 128:(j + 1) * 128],
                             axis=mybir.AxisListType.X)
            v.drain()
        _step(p, "vector", [(sMM, tr_done)], v_mxred, ("drain", sDV))
        bfree[2] = (sDV, sDV.inc())
        _step(p, "tensor", bw(2),
              lambda t: t.matmul(pb[2][0:4, 0:128], mxcol[:, 0:4], r.idf[:],
                                 is_transpose=True, start=True, stop=True),
              ("then", sMM))
        sMM.inc()
        _step(p, "vector", [(sMM, sMM.n)] + fw("trst"),
              lambda v: v.tensor_copy(trst[0:4, :], pb[2][0:4, 0:128]),
              ("drain", sDV))
        bfree[2] = (sDV, sDV.inc())
        _step(p, "sync", [(sDV, sDV.n)],
              lambda s: s.dma_start(out=rw(MX), in_=trst[0:4, :]),
              ("then", sRD), inck=16)
        mx_done = sRD.inc(16)
        free["trst"] = (sRD, mx_done)

        def v_rows3(v):
            v.tensor_tensor(rw(HMAX), g2keep[:], rw(MX), ALU.mult)
            v.drain()
            v.tensor_scalar(rw(HMAX), rw(HMAX), 1e-5, None, ALU.max)
        _step(p, "vector", [(sRD, mx_done), (sAC, t_g2)], v_rows3, ("drain", sDV))
        sDV.inc()

        def sc_rows3(s):
            s.activation(out=rw(LNH), in_=rw(HMAX), func=AF.Ln)
            s.drain()
            s.activation(out=rw(S1), in_=rw(LNH), func=AF.Exp, scale=-1.0,
                         bias=r.l127[0:1, 0:1])
        _step(p, "scalar", [(sDV, sDV.n)], sc_rows3, ("drain", sAC))
        sAC.inc()

        def v_rows4(v):
            v.tensor_tensor(rw(C1), g2keep[:], rw(S1), ALU.mult)
            v.tensor_scalar(rw(SAr), rw(HMAX), 1.0 / 127.0, None, ALU.mult)
            v.drain()
            v.tensor_tensor(rw(SAr), rw(SAr), bass.AP(r.mw1, 0, [[1, 1], [0, CH]]),
                            ALU.mult)
        _step(p, "vector", [(sAC, sAC.n)], v_rows4, ("drain", sDV))
        rows4_done = sDV.inc()
        _step(p, "tensor", bw(0) + bw(1) + [(sDV, rows4_done)],
              lambda t: (t.matmul(pb[0][:, :], r.ones_f_row1[:], rw(C1),
                                  start=True, stop=True).then_inc(sMM.h, 1),
                         t.matmul(pb[1][:, :], r.ones_f_row1[:], rw(SAr),
                                  start=True, stop=True))[1],
              ("then", sMM))
        sMM.inc(2)
        cb_done = sMM.n
        _step(p, "vector", [(sMM, cb_done)],
              lambda v: v.tensor_copy(sA_sb[:], pb[1][:, :]),
              ("drain", sDV))
        bfree[1] = (sDV, sDV.inc())
        _step(p, "vector", [(sMM, cb_done), (sDV, qproj_done)],
              lambda v: v.tensor_tensor(
                  scr[:].rearrange("p (j t) -> p j t", j=DC),
                  scr[:].rearrange("p (j t) -> p j t", j=DC), pbcast(pb[0]), ALU.mult),
              ("drain", sDV))
        bfree[0] = (sDV, sDV.inc())
        def v_cast(v):
            for dc in range(DC):
                b2_ = dc % 2
                v.tensor_scalar(intq2[b2_][:], scr[:, dc * CH:(dc + 1) * CH],
                                r.offc[:, 0:1], None, ALU.add)
                v.drain()
                v.tensor_copy(xnq[:, dc * CH:(dc + 1) * CH], intq2[b2_][:])
                v.drain()
        _step(p, "vector", [(sMM, tr_done)], v_cast, ("drain", sDV))
        xq1_done = sDV.inc()
        free["scr"] = (sDV, xq1_done)

        _step(p, "scalar", [(sDV, qproj_done)] + fw("e0") + fw("e1")
              + fw("zc40") + fw("zc41"),
              lambda s: (s.activation(out=e_sb[0][:], in_=qaf[:, 0:2048],
                                      func=AF.Square),
                         s.drain(),
                         s.activation(out=e_sb[1][:], in_=qaf[:, 2048:4096],
                                      func=AF.Square))[-1],
              ("drain", sAC))
        sqq_done = sAC.inc()
        for oc in range(DC):
            _step(p, "tensor", [(sAC, sqq_done)] + bw(1)
                  + ([r.bconsts_done] if (c == 0 and oc == 0) else []),
                  lambda t, oc=oc: t.matmul(
                      pb[1][0:2, :], r.e2[:],
                      e_sb[oc // 4][:, (oc % 4) * 512:(oc % 4) * 512 + 512],
                      start=True, stop=True),
                  ("then", sMM))
            mm_t = sMM.inc()

            def v_m(v, oc=oc):
                for pi in range(4):
                    v.reduce_max(m2acc[0:2, 4 + pi:5 + pi],
                                 pb[1][0:2, 128 * pi:128 * pi + 128],
                                 axis=mybir.AxisListType.X)
                v.drain()
                if oc == 0:
                    v.tensor_copy(m2acc[0:2, 0:4], m2acc[0:2, 4:8])
                else:
                    v.tensor_tensor(m2acc[0:2, 0:4], m2acc[0:2, 0:4],
                                    m2acc[0:2, 4:8], ALU.max)
            _step(p, "vector", [(sMM, mm_t)], v_m, ("drain", sDV))
            bfree[1] = (sDV, sDV.inc())
        _step(p, "tensor", bw(1) + [(sDV, sDV.n)],
              lambda t: t.matmul(pb[1][0:4, 0:2], m2acc[0:2, 0:4], r.idf[0:2, 0:2],
                                 is_transpose=True, start=True, stop=True),
              ("then", sMM))
        sMM.inc()

        def v_m2(v):
            v.reduce_max(m41[0:4, 0:1], pb[1][0:4, 0:2], axis=mybir.AxisListType.X)
            v.drain()
            v.tensor_scalar(m41[0:4, 0:1], m41[0:4, 0:1], -1.0 / 32.0, None, ALU.mult)
        _step(p, "vector", [(sMM, sMM.n)], v_m2, ("drain", sDV))
        bfree[1] = (sDV, sDV.inc())
        _step(p, "tensor", bw(1) + [(sDV, sDV.n)],
              lambda t: t.matmul(pb[1][0:1, 0:4], m41[0:4, 0:1], r.idf[0:4, 0:4],
                                 is_transpose=True, start=True, stop=True),
              ("then", sMM))
        sMM.inc()
        _step(p, "vector", [(sMM, sMM.n)],
              lambda v: v.tensor_copy(mrow2[:], pb[1][0:1, 0:4]),
              ("drain", sDV))
        bfree[1] = (sDV, sDV.inc())
        _step(p, "tensor", bw(1) + [(sDV, sDV.n)],
              lambda t: t.matmul(pb[1][:, 0:4], r.ones_f_row1[:], mrow2[0:1, 0:4],
                                 start=True, stop=True),
              ("then", sMM))
        sMM.inc()
        _step(p, "vector", [(sMM, sMM.n)],
              lambda v: v.tensor_copy(mcols[:], pb[1][:, 0:4]),
              ("drain", sDV))
        bfree[1] = (sDV, sDV.inc())
        mprep_done = sDV.n
        free["e0"] = (sMM, sMM.n)
        free["e1"] = (sMM, sMM.n)

        gl_ts = []

        def emit_mm1(fc):
            buf = fc % 2
            _step(p, "sync", fw(f"w1s{buf}") + [(sWS[buf], sWS[buf].n)],
                  lambda s, fc=fc, buf=buf: _slab(
                      s, sWS[buf].h, w1s[buf][:], r.w1T[fc]),
                  ("then", sWS[buf]), inck=16)
            ws_t = sWS[buf].inc(64)
            mb = 4 + buf
            for dc in range(DC):
                wl = ([(sWS[buf], ws_t), (sDV, xq1_done)] + bw(mb)) if dc == 0 else []
                _step(p, "tensor", wl,
                      lambda t, mb=mb, buf=buf, dc=dc: t.matmul(
                          pb[mb][:, :], w1s[buf][:, dc * 128:(dc + 1) * 128],
                          xnq[:, dc * CH:(dc + 1) * CH],
                          start=(dc == 0), stop=(dc == DC - 1)),
                      ("then", sMM))
                mm_t = sMM.inc()
            free[f"w1s{buf}"] = (sMM, mm_t)
            free["xnq"] = (sMM, mm_t)
            _step(p, "vector", [(sMM, mm_t)] + fw(f"sqs{buf}"),
                  lambda v, mb=mb, buf=buf, fc=fc: v.scalar_tensor_tensor(
                      sqs[buf][:], pb[mb][:, :], r.corr1c[:, fc:fc + 1], sA_sb[:],
                      ALU.add, ALU.mult),
                  ("drain", sDV))
            bfree[mb] = (sDV, sDV.inc())
            _step(p, "scalar", [(sDV, sDV.n)] + (fw("hbf") if fc == 0 else []),
                  lambda s, buf=buf, fc=fc: s.activation(
                      out=hbf[:, fc * CH:(fc + 1) * CH], in_=sqs[buf][:], func=AF.Gelu,
                      bias=r.b1c[:, fc:fc + 1]),
                  ("drain", sAC))
            gl_ts.append(sAC.inc())
            free[f"sqs{buf}"] = (sAC, gl_ts[-1])

        asm_done = []
        for pp in range(2):
            ga_ev = {}
            for pi, (a, rr_) in enumerate([(0, 0), (1, 1), (0, 1), (1, 0)]):
                gb = pi % 2
                gps = psA[:, gb * 1024:(gb + 1) * 1024]
                bks = (0, 1) if gb == 0 else (4, 5)
                sel = r.idb if a == rr_ else r.swapI
                wl = [(sDV, qproj_done)]
                if pi < 2:
                    wl += bw(bks[0]) + bw(bks[1])
                else:
                    wl.append(ga_ev[pi - 2])

                def t_ga(t, a=a, rr_=rr_, gps=gps, sel=sel, pp=pp):
                    mov = qaf[rr_ * 64:rr_ * 64 + 64, :].rearrange(
                        "p (j t) -> p j t", j=DC)[:, :, 128 * (2 * pp + a):
                                                  128 * (2 * pp + a) + 128]
                    t.matmul(gps[:, 0:512], sel[rr_ * 64:rr_ * 64 + 64, :],
                             mov[:, 0:4, :], start=True, stop=True).then_inc(sMM.h, 1)
                    return t.matmul(gps[:, 512:1024], sel[rr_ * 64:rr_ * 64 + 64, :],
                                    mov[:, 4:8, :], start=True, stop=True)
                _step(p, "tensor", wl, t_ga, ("then", sMM))
                mm_t = sMM.inc(2)
                wl2 = [(sMM, mm_t)] + (fw("qpk") if pi == 0 else [])

                def v_ga(v, a=a, rr_=rr_, gps=gps, pp=pp):
                    src = gps[a * 64:a * 64 + 64, :].rearrange("p (u g) -> p u g", g=128)
                    dst = qpk[pp][a * 64:a * 64 + 64, :].rearrange(
                        "p (u g) -> p u g", g=128)[:, rr_::2, :]
                    v.tensor_copy(dst, src)
                _step(p, "vector", wl2, v_ga, ("drain", sDV))
                ga_ev[pi] = (sDV, sDV.inc())
                for b in bks:
                    bfree[b] = ga_ev[pi]
            ga_vt = {0: ga_ev[2], 1: ga_ev[3]}

            for a in range(2):
                for Rb in range(16):
                    wl = [ga_vt[a]] + (bw(0) if Rb == 0 else [])
                    _step(p, "tensor", wl,
                          lambda t, pp=pp, a=a, Rb=Rb: t.matmul(
                              pb[0][:].bitcast(BF16)[:, 64 * Rb:64 * Rb + 64],
                              qpk[pp][a * 64:a * 64 + 64, 128 * Rb:128 * Rb + 128],
                              r.idb[a * 64:a * 64 + 64, a * 64:a * 64 + 64],
                              is_transpose=True, start=True, stop=True),
                          ("then", sMM))
                    sMM.inc()
                _step(p, "vector", [(sMM, sMM.n)] + fw(f"vt{a}"),
                      lambda v, a=a: v.tensor_copy(vt[a][:], pb[0][:].bitcast(BF16)),
                      ("drain", sDV))
                bfree[0] = (sDV, sDV.inc())
            scbuf = [psA[:, 0:2 * CH], psA[:, 2 * CH:4 * CH]]
            exp_toks = []
            exp_done = {0: 0, 1: 0}
            av_prev = {0: 0, 1: 0}

            def emit_scores(a, Rb, rnd):
                i = len(exp_toks)
                sbu = i % 2
                sh = scbuf[sbu]
                for kc in (2 * rnd, 2 * rnd + 1):
                    wl = [(sDV, mprep_done)] if (i == 0 and kc == 2 * rnd) else []
                    if kc == 2 * rnd:
                        if i >= 2:
                            wl.append((sAC, exp_toks[i - 2]))
                        else:
                            wl += (bw(0) + bw(1)) if sbu == 0 else (bw(4) + bw(5))
                    _step(p, "tensor", wl,
                          lambda t, pp=pp, a=a, Rb=Rb, kc=kc, sh=sh: t.matmul(
                              sh[:, (kc % 2) * CH:(kc % 2) * CH + CH],
                              qpk[pp][a * 64:a * 64 + 64, 128 * Rb:128 * Rb + 128],
                              qpk[pp][a * 64:a * 64 + 64, kc * 512:(kc + 1) * 512],
                              start=True, stop=True),
                          ("then", sMM))
                    sMM.inc()
                wl2 = [(sMM, sMM.n)] + (
                    (fw(f"e{a}") + fw(f"zc4{a}")) if (Rb == 0 and rnd == 0) else [])
                _step(p, "scalar", wl2,
                      lambda s, a=a, Rb=Rb, rnd=rnd, pp=pp, sh=sh: s.activation(
                          out=e_sb[a][:, rnd * 1024:(rnd + 1) * 1024],
                          in_=sh, func=AF.Exp, scale=1.0 / 32.0,
                          bias=mcols[:, 2 * pp + a:2 * pp + a + 1],
                          accum_out=zc4[a][:, 2 * Rb + rnd:2 * Rb + rnd + 1]),
                      ("drain", sAC))
                exp_toks.append(sAC.inc())

            def emit_av(a, Rb):
                for qc in range(4):
                    avb = pbAV[qc]
                    avi = [2, 3, 6, 7][qc]
                    wl = [(sAC, exp_done[a])] if qc == 0 else []
                    if Rb == 0:
                        wl += bw(avi)
                    _step(p, "tensor", wl,
                          lambda t, a=a, Rb=Rb, qc=qc, avb=avb: t.matmul(
                              avb[a * 64:a * 64 + 64, :],
                              vt[a][:, 64 * Rb:64 * Rb + 64],
                              e_sb[a][:, qc * 512:(qc + 1) * 512],
                              start=(Rb == 0), stop=(Rb == 15),
                              tile_position=(0, 64 * a),
                              skip_group_check=True),
                          ("then", sMM))
                    sMM.inc()
                av_prev[a] = sMM.n

            for Rb in range(16):
                for a in range(2):
                    if Rb > 0:
                        emit_av(a, Rb - 1)
                    emit_scores(a, Rb, 0)
                    emit_scores(a, Rb, 1)
                    exp_done[a] = sAC.n
            for a in range(2):
                emit_av(a, 15)
            bfree[0] = bfree[1] = (sAC, exp_toks[-2])
            bfree[4] = bfree[5] = (sAC, exp_toks[-1])
            av_done = sMM.n
            free["vt0"] = (sMM, av_done)
            free["vt1"] = (sMM, av_done)
            free["qpk"] = (sMM, av_done)

            for a in range(2):
                def v_z(v, a=a):
                    zv = zc4[a][:].rearrange("p (r k) -> p r k", k=2)
                    v.tensor_tensor(z16[:].rearrange("p (r o) -> p r o", o=1),
                                    zv[:, :, 0:1], zv[:, :, 1:2], ALU.add)
                    v.drain()
                    v.reciprocal(z16[:], z16[:])
                _step(p, "vector", [(sAC, sAC.n)], v_z, ("drain", sDV))
                sDV.inc()
                free[f"zc4{a}"] = (sDV, sDV.n)
                _step(p, "tensor", bw(0) + [(sDV, sDV.n)],
                      lambda t: t.matmul(pb[0][0:16, 0:128], z16[:], r.idf[:],
                                         is_transpose=True, start=True, stop=True),
                      ("then", sMM))
                sMM.inc()
                _step(p, "vector", [(sMM, sMM.n)] + fw("trst"),
                      lambda v: v.tensor_copy(trst[0:16, :], pb[0][0:16, 0:128]),
                      ("drain", sDV))
                bfree[0] = (sDV, sDV.inc())
                _step(p, "sync", [(sDV, sDV.n)],
                      lambda s, a=a: s.dma_start(out=zrow[a], in_=trst[0:16, :]),
                      ("then", sRD), inck=16)
                sRD.inc(16)
                free["trst"] = (sRD, sRD.n)
            z_done = sRD.n

            for qc in range(4):
                def t_div(t, qc=qc):
                    t.matmul(pb[0][:, :], r.elo[:], scr[0:1, qc * 512:(qc + 1) * 512],
                             start=True, stop=False).then_inc(sMM.h, 1)
                    return t.matmul(pb[0][:, :], r.ehi[:],
                                    scr[0:1, 2048 + qc * 512:2048 + (qc + 1) * 512],
                                    start=False, stop=True)
                _step(p, "tensor", bw(0) + [(sRD, z_done)], t_div, ("then", sMM))
                sMM.inc(2)
                _step(p, "vector", [(sMM, sMM.n)],
                      lambda v: v.tensor_copy(div_sb[:], pb[0][:, :]),
                      ("drain", sDV))
                bfree[0] = (sDV, sDV.inc())
                _step(p, "vector", [(sMM, av_done)],
                      lambda v, qc=qc: v.tensor_tensor(
                          e_sb[0][:, qc * 512:(qc + 1) * 512], pbAV[qc][:, :],
                          div_sb[:], ALU.mult),
                      ("drain", sDV))
                dv_t = sDV.inc()
                bfree[[2, 3, 6, 7][qc]] = (sDV, dv_t)
            evac_done = sDV.n
            free["e1"] = (sDV, evac_done)
            free["scr"] = (sMM, sMM.n)

            asm_ev = {}
            for pi, (a, rr_) in enumerate([(0, 0), (1, 1), (0, 1), (1, 0)]):
                gb = pi % 2
                gps = psA[:, gb * 1024:(gb + 1) * 1024]
                bks = (0, 1) if gb == 0 else (4, 5)
                sel = r.idb if a == rr_ else r.swapI
                wl = [(sDV, evac_done)]
                if pi < 2:
                    wl += bw(bks[0]) + bw(bks[1])
                else:
                    wl.append(asm_ev[pi - 2])

                def t_as(t, a=a, rr_=rr_, gps=gps, sel=sel):
                    mov = e_sb[0][a * 64:a * 64 + 64, :].rearrange(
                        "p (u g) -> p u g", g=128)[:, rr_::2, :]
                    t.matmul(gps[:, 0:512], sel[a * 64:a * 64 + 64, :],
                             mov[:, 0:4, :], start=True, stop=True).then_inc(sMM.h, 1)
                    return t.matmul(gps[:, 512:1024], sel[a * 64:a * 64 + 64, :],
                                    mov[:, 4:8, :], start=True, stop=True)
                _step(p, "tensor", wl, t_as, ("then", sMM))
                mm_t = sMM.inc(2)
                wl2 = [(sMM, mm_t)] + (fw("qaf") if pi == 0 else [])

                def v_as(v, a=a, rr_=rr_, gps=gps, pp=pp):
                    src = gps[rr_ * 64:rr_ * 64 + 64, :].rearrange(
                        "p (u g) -> p u g", g=128)
                    dst = qaf[rr_ * 64:rr_ * 64 + 64, :].rearrange(
                        "p (j t) -> p j t", j=DC)[:, :, 128 * (2 * pp + a):
                                                  128 * (2 * pp + a) + 128]
                    v.tensor_copy(dst, src)
                _step(p, "vector", wl2, v_as, ("drain", sDV))
                asm_ev[pi] = (sDV, sDV.inc())
                for b in bks:
                    bfree[b] = asm_ev[pi]
            asm_done.append(sDV.n)
            free["e0"] = (sMM, sMM.n)

        if c + 1 < NCH:
            pst = emit_stats(c + 1, (sDV, rows4_done))
        for fc in range(FC):
            emit_mm1(fc)

        for oc in range(DC):
            buf = oc % 2
            _step(p, "sync", fw(f"wqs{buf}") + [(sWS[buf], sWS[buf].n)],
                  lambda s, oc=oc, buf=buf: _slab(
                      s, sWS[buf].h, wqs[buf][:], r.wuT[oc]),
                  ("then", sWS[buf]), inck=16)
            ws_t = sWS[buf].inc(64)
            mb = 4 + (oc % 2)
            for dc in range(DC):
                wl = ([(sWS[buf], ws_t), (sDV, asm_done[-1])] + bw(mb)) if dc == 0 else []
                _step(p, "tensor", wl,
                      lambda t, mb=mb, buf=buf, dc=dc: t.matmul(
                          pb[mb][:, :], wqs[buf][:, dc * 128:(dc + 1) * 128],
                          qaf[:, dc * CH:(dc + 1) * CH], start=(dc == 0),
                          stop=(dc == DC - 1)),
                      ("then", sMM))
                mm_t = sMM.inc()
            free[f"wqs{buf}"] = (sMM, mm_t)
            _step(p, "vector", [(sMM, mm_t)],
                  lambda v, mb=mb, c=c, oc=oc: v.scalar_tensor_tensor(
                      xap(c, oc), pb[mb][:, :], r.buc[:, oc:oc + 1], xap(c, oc),
                      ALU.add, ALU.add),
                  ("drain", sDV))
            bfree[mb] = (sDV, sDV.inc())
        free["qaf"] = (sMM, sMM.n)

        for fc in range(FC):
            buf = fc % 2
            _step(p, "scalar", fw(f"sq2{buf}") + [(sAC, gl_ts[fc])],
                  lambda s, buf=buf, fc=fc: s.activation(
                      out=sq2[buf][:], in_=hbf[:, fc * CH:(fc + 1) * CH], func=AF.Square),
                  ("drain", sAC))
            sqh_t = sAC.inc()
            _step(p, "tensor", [(sAC, sqh_t)] + (bw(0) if fc == 0 else []),
                  lambda t, buf=buf, fc=fc: t.matmul(
                      pb[0][0:1, :], r.ones_f_col[:], sq2[buf][:],
                      start=(fc == 0), stop=(fc == FC - 1)),
                  ("then", sMM))
            mm_t = sMM.inc()

            def v_mx(v, buf=buf, fc=fc):
                if fc == 0:
                    v.tensor_copy(maxacc[:], sq2[buf][:])
                else:
                    v.tensor_tensor(maxacc[:], maxacc[:], sq2[buf][:], ALU.max)
            _step(p, "vector", [(sAC, sqh_t)] + (fw("maxacc") if fc == 0 else []),
                  v_mx, ("drain", sDV))
            dv_t = sDV.inc()
            free[f"sq2{buf}"] = [(sMM, mm_t), (sDV, dv_t)]
        ssq_done = sMM.n
        vmx_done = sDV.n

        _step(p, "vector", [(sMM, ssq_done)],
              lambda v: v.tensor_scalar(rw(SSQ), pb[0][0:1, :], 1.0 / FF, None, ALU.mult),
              ("drain", sDV))
        bfree[0] = (sDV, sDV.inc())

        def sc_hr1(s):
            s.activation(out=rw(LNV2), in_=rw(SSQ), func=AF.Ln, bias=r.epsc[0:1, 0:1])
            s.drain()
            s.activation(out=rw(RSTD2), in_=rw(LNV2), func=AF.Exp, scale=-0.5)
        _step(p, "scalar", [(sDV, sDV.n)], sc_hr1, ("drain", sAC))
        sAC.inc()
        for j in range(4):
            _step(p, "tensor", [(sDV, vmx_done)] + (bw(2) if j == 0 else []),
                  lambda t, j=j: t.matmul(
                      pb[2][:, j * 128:(j + 1) * 128], maxacc[:, j * 128:(j + 1) * 128],
                      r.idf[:], is_transpose=True, start=True, stop=True),
                  ("then", sMM))
            sMM.inc()
        free["maxacc"] = (sMM, sMM.n)
        _step(p, "vector", [(sMM, sMM.n)],
              lambda v: [v.reduce_max(mxcol[:, j:j + 1], pb[2][:, j * 128:(j + 1) * 128],
                                      axis=mybir.AxisListType.X) for j in range(4)][-1],
              ("drain", sDV))
        bfree[2] = (sDV, sDV.inc())
        _step(p, "tensor", bw(2),
              lambda t: t.matmul(pb[2][0:4, 0:128], mxcol[:, 0:4], r.idf[:],
                                 is_transpose=True, start=True, stop=True),
              ("then", sMM))
        sMM.inc()
        _step(p, "vector", [(sMM, sMM.n)] + fw("trst"),
              lambda v: v.tensor_copy(trst[0:4, :], pb[2][0:4, 0:128]),
              ("drain", sDV))
        bfree[2] = (sDV, sDV.inc())
        _step(p, "sync", [(sDV, sDV.n)],
              lambda s: s.dma_start(out=rw(MXH2), in_=trst[0:4, :]),
              ("then", sRD), inck=16)
        mxh_done = sRD.inc(16)
        free["trst"] = (sRD, mxh_done)

        def sc_hr2(s):
            s.activation(out=rw(LNM), in_=rw(MXH2), func=AF.Ln)
        _step(p, "scalar", [(sRD, mxh_done)], sc_hr2, ("drain", sAC))
        sAC.inc()

        def v_hr(v):
            v.tensor_scalar(rw(SSQ), rw(LNV2), 0.5, None, ALU.mult)
            v.drain()
            v.scalar_tensor_tensor(rw(LNHN), rw(LNM), 0.5, rw(SSQ), ALU.mult,
                                   ALU.subtract)
            v.drain()
            v.tensor_scalar(rw(LNHN), rw(LNHN), math.log(1e-5), None, ALU.max)
        _step(p, "vector", [(sAC, sAC.n)], v_hr, ("drain", sDV))
        sDV.inc()

        def sc_hr3(s):
            s.activation(out=rw(S2r), in_=rw(LNHN), func=AF.Exp, scale=-1.0,
                         bias=r.l127[0:1, 0:1])
            s.drain()
            s.activation(out=rw(SBr), in_=rw(LNHN), func=AF.Exp, bias=nl127[0:1, 0:1])
        _step(p, "scalar", [(sDV, sDV.n)], sc_hr3, ("drain", sAC))
        sAC.inc()

        def v_hr2(v):
            v.tensor_tensor(rw(C2r), rw(RSTD2), rw(S2r), ALU.mult)
            v.drain()
            v.tensor_tensor(rw(SBr), rw(SBr), bass.AP(r.mw2, 0, [[1, 1], [0, CH]]),
                            ALU.mult)
        _step(p, "vector", [(sAC, sAC.n)], v_hr2, ("drain", sDV))
        hr_done = sDV.inc()

        def t_hb(t):
            t.matmul(pb[6][:, :], r.ones_f_row1[:], rw(C2r),
                     start=True, stop=True).then_inc(sMM.h, 1)
            return t.matmul(pb[1][:, :], r.ones_f_row1[:], rw(SBr),
                            start=True, stop=True)
        _step(p, "tensor", bw(6) + bw(1) + [(sDV, hr_done)], t_hb, ("then", sMM))
        sMM.inc(2)
        hb_done = sMM.n
        _step(p, "vector", [(sMM, hb_done)],
              lambda v: v.tensor_copy(sB_sb[:], pb[1][:, :]),
              ("drain", sDV))
        bfree[1] = (sDV, sDV.inc())

        hq_toks = []
        for fc in range(FC):
            def v_q(v, fc=fc):
                b2_ = fc % 2
                hv = hbf[:, fc * CH:(fc + 1) * CH]
                v.tensor_tensor(sqs[b2_][:], hv, pb[6][:, :], ALU.mult)
                v.drain()
                v.tensor_scalar(intq2[b2_][:], sqs[b2_][:], r.offc[:, 0:1], None, ALU.add)
                v.drain()
                v.tensor_copy(hv, intq2[b2_][:])
                v.drain()
            _step(p, "vector", [(sMM, hb_done)], v_q, ("drain", sDV))
            hq_toks.append(sDV.inc())
        hq_done = sDV.n
        bfree[6] = (sDV, hq_done)

        for oc in range(DC):
            w2h_t = []
            for hf in range(2):
                _step(p, "sync", fw(f"w2h{hf}") + [(sW2h[hf], sW2h[hf].n)],
                      lambda s, oc=oc, hf=hf: _slab(
                          s, sW2h[hf].h, w2s[:, hf * 2048:(hf + 1) * 2048],
                          r.w2T[oc][:, hf * 2048:(hf + 1) * 2048]),
                      ("then", sW2h[hf]), inck=16)
                w2h_t.append(sW2h[hf].inc(64))
            mb = 4 + (oc % 2)
            for fc2 in range(FC):
                hf = fc2 // 16
                wl = ([(sW2h[hf], w2h_t[hf])] if fc2 % 16 == 0 else []) \
                    + ([(sDV, hq_toks[fc2])] if oc == 0 else []) \
                    + (bw(mb) if fc2 == 0 else [])
                _step(p, "tensor", wl,
                      lambda t, mb=mb, fc2=fc2: t.matmul(
                          pb[mb][:, :], w2s[:, fc2 * 128:(fc2 + 1) * 128],
                          hbf[:, fc2 * CH:(fc2 + 1) * CH],
                          start=(fc2 == 0), stop=(fc2 == FC - 1)),
                      ("then", sMM))
                mm_t = sMM.inc()
                if fc2 == 15:
                    free["w2h0"] = (sMM, mm_t)
            free["w2h1"] = (sMM, mm_t)

            def v_ev2(v, mb=mb, c=c, oc=oc, buf=oc % 2):
                v.scalar_tensor_tensor(sqs[buf][:], pb[mb][:, :], r.corr2c[:, oc:oc + 1],
                                       sB_sb[:], ALU.add, ALU.mult)
                v.drain()
                v.scalar_tensor_tensor(xap(c, oc), sqs[buf][:], r.b2c[:, oc:oc + 1],
                                       xap(c, oc), ALU.add, ALU.add)
            _step(p, "vector", [(sMM, mm_t)] + fw(f"sqs{oc % 2}"), v_ev2, ("drain", sDV))
            dv_t = sDV.inc()
            bfree[mb] = (sDV, dv_t)
            free[f"sqs{oc % 2}"] = (sDV, dv_t)
        free["hbf"] = (sMM, sMM.n)

    r.l0_done = (sDV, sDV.n)



def _chunk_stats(r, p, pb, sMM, sDV, sAC, rw, src_ap_fn, sqpc, sqst, MEAN, M2, VAR,
                 LNV, RSTD, prev_ac, bw, bfree, src_ready=None):
    for dc in range(DC):
        buf = dc % 2
        wl = ([] if sqst[buf] == 0 else [(sMM, sqst[buf])])
        if src_ready is not None and dc == 0:
            wl = wl + [src_ready]
        _step(p, "scalar", wl,
              lambda s, dc=dc, buf=buf: s.activation(
                  out=sqpc[buf][:], in_=src_ap_fn(dc), func=AF.Square),
              ("drain", sAC))
        ac_t = sAC.inc()
        _step(p, "tensor", [(sAC, ac_t)] + (bw(1) if dc == 0 else []),
              lambda t, dc=dc, buf=buf: t.matmul(
                  pb[1][0:1, :], r.ones_bf_col[:], sqpc[buf][:],
                  start=(dc == 0), stop=(dc == DC - 1)),
              ("then", sMM))
        sqst[buf] = sMM.inc()
    s2_done = sMM.n
    for dc in range(DC):
        _step(p, "tensor", bw(0) if dc == 0 else [],
              lambda t, dc=dc: t.matmul(
                  pb[0][0:1, :], r.ones_bf_col[:], src_ap_fn(dc),
                  start=(dc == 0), stop=(dc == DC - 1)),
              ("then", sMM))
        s1_done = sMM.inc()
    _step(p, "vector", [(sMM, s1_done), (sAC, prev_ac)],
          lambda v: v.tensor_scalar(rw(MEAN), pb[0][0:1, :], 1.0 / D, None, ALU.mult),
          ("drain", sDV))
    bfree[0] = (sDV, sDV.inc())
    _step(p, "scalar", [(sDV, sDV.n)],
          lambda s: s.activation(out=rw(M2), in_=rw(MEAN), func=AF.Square),
          ("drain", sAC))
    sAC.inc()
    _step(p, "vector", [(sAC, sAC.n), (sMM, s2_done)],
          lambda v: v.scalar_tensor_tensor(rw(VAR), pb[1][0:1, :], 1.0 / D, rw(M2),
                                           ALU.mult, ALU.subtract),
          ("drain", sDV))
    bfree[1] = (sDV, sDV.inc())

    def sc_r(s):
        s.activation(out=rw(LNV), in_=rw(VAR), func=AF.Ln, bias=r.epsc[0:1, 0:1])
        s.drain()
        s.activation(out=rw(RSTD), in_=rw(LNV), func=AF.Exp, scale=-0.5)
    _step(p, "scalar", [(sDV, sDV.n)], sc_r, ("drain", sAC))
    return sAC.inc()


def emit_l1(r, p, psb, out):
    nc = r.nc
    sb = psb["sb"]
    st = psb["st"]
    xf = r.x_fm
    xn = r.xn_buf

    sqpc = [sb(f"l1_sq{i}", [128, CH], BF16) for i in range(2)]
    evsc = [sb(f"l1_ev{i}", [128, CH], F32) for i in range(2)]
    sqst = {0: 0, 1: 0}
    _i2 = sb("l1_int", [128, CH], I32)
    intq2 = [_i2, _i2]
    wts = [sb(f"l1_wts{i}", [128, DC * 128], BF16) for i in range(2)]
    rows = sb("l1_rows", [1, 4 * CH], F32)
    MEAN, M2, VAR, RSTD = range(4)
    LNV = VAR
    mixb = sb("l1_mixb", [128, DC * CH], BF16)
    xln = sb("l1_xln", [128, DC * CH], BF16)
    ekt = sb("l1_ek", [128, DC * CH], BF16)
    vt_ = sb("l1_v", [128, DC * CH], BF16)
    rst = sb("l1_rs", [128, DC * CH], BF16)
    At = sb("l1_A", [128, DC * CH], BF16)
    Bt = sb("l1_B", [128, DC * CH], BF16)
    carryA = sb("l1_cA", [128, DC], F32)
    carryB = sb("l1_cB", [128, DC], F32)
    xcar = sb("l1_xcar", [128, DC], BF16)
    ycar = sb("l1_ycar", [128, DC], BF16)
    com = sb("l1_com", [128, DC], F32)
    wfks = wts
    wfvs = [rst[:, 0:D], rst[:, D:2 * D]]
    kks = [rst[:, 2 * D:2 * D + 256], rst[:, 2 * D + 256:2 * D + 512]]
    xiq = vt_
    ynt = xln
    ymt = At
    rs2 = Bt
    outst = ekt[:].bitcast(F32)
    _mixf = mixb[:].bitcast(F32)
    tokst = [_mixf[:, 0:D], _mixf[:, D:2 * D]]

    pb = [st.enter_context(nc.psum_tensor(f"l1_pb{i}", [128, CH], F32)) for i in range(4)]
    pso = st.enter_context(nc.psum_tensor("l1_pso", [128, DC * 256], F32))
    bfree = {i: None for i in range(4)}

    def bw(i):
        return [bfree[i]] if bfree[i] else []

    sMM = r.sem("l1_mm")
    sDV = r.sem("l1_dv")
    sAC = r.sem("l1_ac")
    sWS = [r.sem("l1_ws0"), r.sem("l1_ws1")]
    sWV = [r.sem("l1_wv0"), r.sem("l1_wv1")]
    sOD = [r.sem("l1_od0"), r.sem("l1_od1")]

    free = {}
    for nm in ("wts0", "wts1", "xiq", "mixb", "tmp2", "xln", "ek", "v", "rs", "A", "B",
               "yn", "ym", "rs2", "kks0", "kks1", "wfks0", "wfks1", "wfvs0", "wfvs1",
               "outst", "tok0", "tok1", "ev0", "ev1"):
        free[nm] = None

    def fw(nm):
        f = free[nm]
        if f is None:
            return []
        return list(f) if isinstance(f, list) else [f]

    def rw(i):
        return rows[0:1, i * CH:(i + 1) * CH]

    def xap(c, dc=None):
        if dc is None:
            return bass.AP(xf, c * CH, [[DC * T, 128], [T, DC], [1, CH]])
        return bass.AP(xf, dc * T + c * CH, [[DC * T, 128], [1, CH]])

    def nap(c, dc=None):
        if dc is None:
            return bass.AP(xn, c * CH, [[DC * T, 128], [T, DC], [1, CH]])
        return bass.AP(xn, dc * T + c * CH, [[DC * T, 128], [1, CH]])

    def pbc(bank, ndc=DC):
        return bass.AP(bank, 0, [[CH, 128], [0, ndc], [1, CH]])

    def stream_w(buf, hbm, oc, sem_):
        def go(s, hbm=hbm, oc=oc, buf=buf):
            return _slab(s, sWS[buf].h, wts[buf][:], hbm[oc])
        return go

    def emit_p2a(c):
        prev_ac = sAC.n
        rstd_ac = _chunk_stats(r, p, pb, sMM, sDV, sAC, rw, lambda dc, c=c: xap(c, dc),
                               sqpc, sqst, MEAN, M2, VAR, LNV, RSTD, prev_ac, bw, bfree)
        _step(p, "tensor", bw(0) + bw(1) + [(sAC, rstd_ac)],
              lambda t: (t.matmul(pb[0][:, :], r.ones_f_row1[:], rw(MEAN),
                                  start=True, stop=True).then_inc(sMM.h, 1),
                         t.matmul(pb[1][:, :], r.ones_f_row1[:], rw(RSTD),
                                  start=True, stop=True))[1],
              ("then", sMM))
        sMM.inc(2)
        bc_done = sMM.n
        _step(p, "vector", [(sMM, bc_done)],
              lambda v, c=c: v.tensor_tensor(nap(c), xap(c), pbc(pb[0]), ALU.subtract),
              ("drain", sDV))
        sDV.inc()
        _step(p, "vector", [],
              lambda v, c=c: v.tensor_tensor(nap(c), nap(c), pbc(pb[1]), ALU.mult),
              ("drain", sDV))
        xn_done = sDV.inc()
        bfree[0] = (sDV, xn_done)
        bfree[1] = (sDV, xn_done)

        xitok = []
        for dc in range(DC):
            b2_ = dc % 2

            def v_xi1(v, c=c, dc=dc, b2_=b2_):
                v.tensor_scalar(evsc[b2_][:], nap(c, dc),
                                r.sinv_col[:, 0:1], -127.0, ALU.mult, ALU.max)
                v.drain()
                v.tensor_scalar(intq2[b2_][:], evsc[b2_][:], 127.0, r.offc[:, 0:1],
                                ALU.min, ALU.add)
                v.drain()
                v.tensor_copy(xiq[:, dc * CH:(dc + 1) * CH], intq2[b2_][:])
                v.drain()
            wl = ([(sDV, xn_done)] + fw("v") + fw("ev0") + fw("ev1")) if dc == 0 else []
            _step(p, "vector", wl, v_xi1, ("drain", sDV))
            xitok.append(sDV.inc())
        return {"xn_done": xn_done, "xitok": xitok}

    p2a = None
    for c in range(NCH):
        batch_start = (c % (SEQ // CH) == 0)
        if c == 0:
            p2a = emit_p2a(0)
        xn_done, xitok = p2a["xn_done"], p2a["xitok"]

        for oc in range(DC):
            buf = oc % 2
            _step(p, "sync", fw(f"wts{buf}") + [(sWS[buf], sWS[buf].n)]
                  + ([r.l1_casts] if c == 0 and oc == 0 else []),
                  stream_w(buf, r.wtT, oc, None), ("then", sWS[buf]), inck=16)
            ws_t = sWS[buf].inc(64)
            mb = 2 + buf
            for dc in range(DC):
                wl = ([(sWS[buf], ws_t)] if dc == 0 else []) \
                    + ([(sDV, xitok[dc])] if oc == 0 else []) \
                    + (bw(mb) if dc == 0 else [])
                _step(p, "tensor", wl,
                      lambda t, mb=mb, buf=buf, dc=dc: t.matmul(
                          pb[mb][:, :], wts[buf][:, dc * 128:(dc + 1) * 128],
                          xiq[:, dc * CH:(dc + 1) * CH],
                          start=(dc == 0), stop=(dc == DC - 1)),
                      ("then", sMM))
                mm_t = sMM.inc()
            free[f"wts{buf}"] = (sMM, mm_t)

            def v_ev(v, mb=mb, c=c, oc=oc, buf=buf):
                v.tensor_scalar(evsc[buf][:], pb[mb][:, :], r.corrL1c[:, oc:oc + 1],
                                r.s_col[:, 0:1], ALU.add, ALU.mult)
                v.drain()
                v.scalar_tensor_tensor(xap(c, oc), evsc[buf][:], r.bL1c[:, oc:oc + 1],
                                       xap(c, oc), ALU.add, ALU.add)
            _step(p, "vector", [(sMM, mm_t)], v_ev, ("drain", sDV))
            bfree[mb] = (sDV, sDV.inc())
        free["v"] = (sMM, sMM.n)

        prev_ac = sAC.n
        rstd_ac = _chunk_stats(r, p, pb, sMM, sDV, sAC, rw, lambda dc, c=c: nap(c, dc),
                               sqpc, sqst, MEAN, M2, VAR, LNV, RSTD, prev_ac, bw, bfree,
                               src_ready=(sDV, xn_done))
        _step(p, "vector", [(sAC, rstd_ac)],
              lambda v: v.tensor_tensor(rw(M2), rw(MEAN), rw(RSTD), ALU.mult),
              ("drain", sDV))
        mr_done = sDV.inc()
        _step(p, "tensor", bw(0) + bw(1) + [(sDV, mr_done)],
              lambda t: (t.matmul(pb[0][:, :], r.ones_f_row1[:], rw(RSTD),
                                  start=True, stop=True).then_inc(sMM.h, 1),
                         t.matmul(pb[1][:, :], r.ones_f_row1[:], rw(M2),
                                  start=True, stop=True))[1],
              ("then", sMM))
        sMM.inc(2)
        bc_done = sMM.n
        _step(p, "vector", [(sMM, bc_done)] + fw("xln"),
              lambda v, c=c: v.tensor_tensor(
                  xln[:].rearrange("p (j t) -> p j t", j=DC), nap(c), pbc(pb[0]),
                  ALU.mult),
              ("drain", sDV))
        sDV.inc()
        _step(p, "vector", [],
              lambda v: v.tensor_tensor(
                  xln[:].rearrange("p (j t) -> p j t", j=DC),
                  xln[:].rearrange("p (j t) -> p j t", j=DC), pbc(pb[1]), ALU.subtract),
              ("drain", sDV))
        dv_t = sDV.inc()
        bfree[0] = (sDV, dv_t)
        bfree[1] = (sDV, dv_t)

        def v_gb(v):
            xv = xln[:].rearrange("p (j t) -> p j t", j=DC)
            v.tensor_tensor(xv, xv, bass.AP(r.g1c, 0, [[DC, 128], [1, DC], [0, CH]]),
                            ALU.mult)
            v.drain()
            v.tensor_tensor(xv, xv, bass.AP(r.b1lc, 0, [[DC, 128], [1, DC], [0, CH]]),
                            ALU.add)
        _step(p, "vector", [], v_gb, ("drain", sDV))
        xln_done = sDV.inc()

        proj_done = {}
        ektoks, vtoks, rstoks = [], [], []
        mix_war = None
        for mi, (mc, omc, whbm, nm) in enumerate([
                (r.mkc, r.omkc, r.wkB, "k"), (r.mvc, r.omvc, r.wvB, "v"),
                (r.mrc, r.omrc, r.wrB, "r")]):
            if not batch_start:
                _step(p, "vector", [],
                      lambda v, omc=omc: v.tensor_tensor(com[:], xcar[:], omc[:],
                                                         ALU.mult),
                      ("drain", sDV))
                sDV.inc()
            mixtok = []
            for dc in range(DC):
                def v_mix1(v, mc=mc, omc=omc, dc=dc, batch_start=batch_start):
                    v.tensor_scalar(mixb[:, dc * CH + 1:(dc + 1) * CH],
                                    xln[:, dc * CH:(dc + 1) * CH - 1],
                                    omc[:, dc:dc + 1], None, ALU.mult)
                    v.drain()
                    if batch_start:
                        v.memset(mixb[:, dc * CH:dc * CH + 1], 0.0)
                    else:
                        v.tensor_copy(mixb[:, dc * CH:dc * CH + 1], com[:, dc:dc + 1])
                    v.drain()
                    v.scalar_tensor_tensor(mixb[:, dc * CH:(dc + 1) * CH],
                                           xln[:, dc * CH:(dc + 1) * CH],
                                           mc[:, dc:dc + 1],
                                           mixb[:, dc * CH:(dc + 1) * CH],
                                           ALU.mult, ALU.add)
                    v.drain()
                wl = ([(sDV, xln_done)] + fw("mixb")) if (mi == 0 and dc == 0) else []
                if mix_war is not None:
                    wl = wl + [mix_war[dc]]
                _step(p, "vector", wl, v_mix1, ("drain", sDV))
                mixtok.append(sDV.inc())

            mtoks = [0] * DC
            for oc in range(DC):
                buf = oc % 2
                _step(p, "sync", fw(f"wts{buf}") + [(sWS[buf], sWS[buf].n)],
                      stream_w(buf, whbm, oc, None), ("then", sWS[buf]), inck=16)
                ws_t = sWS[buf].inc(64)
                mb = 2 + buf
                for dc in range(DC):
                    wl = ([(sWS[buf], ws_t)] if dc == 0 else []) \
                        + ([(sDV, mixtok[dc])] if oc == 0 else []) \
                        + (bw(mb) if dc == 0 else [])
                    _step(p, "tensor", wl,
                          lambda t, mb=mb, buf=buf, dc=dc: t.matmul(
                              pb[mb][:, :], wts[buf][:, dc * 128:(dc + 1) * 128],
                              mixb[:, dc * CH:(dc + 1) * CH],
                              start=(dc == 0), stop=(dc == DC - 1)),
                          ("then", sMM))
                    mm_t = sMM.inc()
                    if oc == DC - 1:
                        mtoks[dc] = (sMM, mm_t)
                free[f"wts{buf}"] = (sMM, mm_t)
                if nm == "k":
                    _step(p, "scalar", [(sMM, mm_t)] + (fw("ek") if oc == 0 else []),
                          lambda s, mb=mb, oc=oc: s.activation(
                              out=ekt[:, oc * CH:(oc + 1) * CH], in_=pb[mb][:, :],
                              func=AF.Exp),
                          ("drain", sAC))
                    bfree[mb] = (sAC, sAC.inc())
                    ektoks.append((sAC, sAC.n))
                elif nm == "v":
                    _step(p, "vector", [(sMM, mm_t)] + (fw("v") if oc == 0 else []),
                          lambda v, mb=mb, oc=oc: v.tensor_copy(
                              vt_[:, oc * CH:(oc + 1) * CH], pb[mb][:, :]),
                          ("drain", sDV))
                    bfree[mb] = (sDV, sDV.inc())
                    vtoks.append((sDV, sDV.n))
                else:
                    _step(p, "scalar", [(sMM, mm_t)] + (fw("rs") if oc == 0 else []),
                          lambda s, mb=mb, oc=oc: s.activation(
                              out=rst[:, oc * CH:(oc + 1) * CH], in_=pb[mb][:, :],
                              func=AF.Sigmoid),
                          ("drain", sAC))
                    bfree[mb] = (sAC, sAC.inc())
                    rstoks.append((sAC, sAC.n))
            proj_done[nm] = (sAC.n, sDV.n)
            free["mixb"] = (sMM, sMM.n)
            mix_war = mtoks
        ek_done, v_done, rs_done = proj_done["k"][0], proj_done["v"][1], proj_done["r"][0]

        _step(p, "vector", [(sDV, xn_done)],
              lambda v, c=c: v.tensor_copy(
                  xcar[:].rearrange("p (j o) -> p j o", o=1),
                  nap(c)[:, :, CH - 1:CH]),
              ("drain", sDV))
        xcar_done = sDV.inc()

        scantok = []
        for dc in range(DC):
            def v_scan1(v, dc=dc, batch_start=batch_start):
                ekd = ekt[:, dc * CH:(dc + 1) * CH]
                vd = vt_[:, dc * CH:(dc + 1) * CH]
                if batch_start:
                    v.tensor_tensor(ekt[:, dc * CH:dc * CH + 1],
                                    ekt[:, dc * CH:dc * CH + 1],
                                    r.etfc[:, dc:dc + 1], ALU.mult)
                    v.drain()
                v.tensor_tensor(vd, ekd, vd, ALU.mult)
                v.drain()
                ini_a = 0.0 if batch_start else carryA[:, dc:dc + 1]
                ini_b = 0.0 if batch_start else carryB[:, dc:dc + 1]
                ewb = bass.AP(r.ewc, dc, [[DC, 128], [0, CH]])
                v.tensor_tensor_scan(At[:, dc * CH:(dc + 1) * CH], ewb, vd, ini_a,
                                     ALU.mult, ALU.add)
                v.tensor_tensor_scan(Bt[:, dc * CH:(dc + 1) * CH], ewb, ekd, ini_b,
                                     ALU.mult, ALU.add)
                v.drain()
            wl = [ektoks[dc], vtoks[dc]] + ((fw("A") + fw("B")) if dc == 0 else [])
            _step(p, "vector", wl, v_scan1, ("drain", sDV))
            scantok.append(sDV.inc())

        def v_car(v):
            v.tensor_copy(carryA[:].rearrange("p (j o) -> p j o", o=1),
                          At[:].rearrange("p (j t) -> p j t", j=DC)[:, :, CH - 1:CH])
            v.tensor_copy(carryB[:].rearrange("p (j o) -> p j o", o=1),
                          Bt[:].rearrange("p (j t) -> p j t", j=DC)[:, :, CH - 1:CH])
        _step(p, "vector", [], v_car, ("drain", sDV))
        scan_done = sDV.inc()

        wkvtok = []
        for dc in range(DC):
            b2_ = dc % 2

            def v_recip1(v, dc=dc, b2_=b2_):
                v.tensor_scalar(evsc[b2_][:], Bt[:, dc * CH:(dc + 1) * CH], 1e-8, None,
                                ALU.add)
            _step(p, "vector", [(sDV, scantok[dc])] + fw(f"ev{b2_}"), v_recip1,
                  ("drain", sDV))
            sDV.inc()

            def sc_recip(s, b2_=b2_):
                s.activation(out=evsc[b2_][:], in_=evsc[b2_][:], func=AF.Ln)
                s.drain()
                s.activation(out=evsc[b2_][:], in_=evsc[b2_][:], func=AF.Exp, scale=-1.0)
            _step(p, "scalar", [(sDV, sDV.n)], sc_recip, ("drain", sAC))
            ac_t = sAC.inc()

            def v_wkv(v, dc=dc, b2_=b2_):
                v.tensor_tensor(vt_[:, dc * CH:(dc + 1) * CH],
                                At[:, dc * CH:(dc + 1) * CH], evsc[b2_][:], ALU.mult)
                v.drain()
                v.tensor_tensor(rst[:, dc * CH:(dc + 1) * CH],
                                rst[:, dc * CH:(dc + 1) * CH],
                                vt_[:, dc * CH:(dc + 1) * CH], ALU.mult)
            _step(p, "vector", [(sAC, ac_t), rstoks[dc]], v_wkv, ("drain", sDV))
            free[f"ev{b2_}"] = (sDV, sDV.inc())
            wkvtok.append((sDV, sDV.n))
        rwkv_done = sDV.n
        free["A"] = (sDV, rwkv_done)
        free["B"] = (sDV, rwkv_done)
        free["ek"] = (sDV, rwkv_done)
        free["v"] = (sDV, rwkv_done)

        for oc in range(DC):
            buf = oc % 2
            _step(p, "sync", fw(f"wts{buf}") + [(sWS[buf], sWS[buf].n)],
                  stream_w(buf, r.woB, oc, None), ("then", sWS[buf]), inck=16)
            ws_t = sWS[buf].inc(64)
            mb = 2 + buf
            for dc in range(DC):
                wl = ([(sWS[buf], ws_t)] if dc == 0 else []) \
                    + ([wkvtok[dc]] if oc == 0 else []) \
                    + (bw(mb) if dc == 0 else [])
                _step(p, "tensor", wl,
                      lambda t, mb=mb, buf=buf, dc=dc: t.matmul(
                          pb[mb][:, :], wts[buf][:, dc * 128:(dc + 1) * 128],
                          rst[:, dc * CH:(dc + 1) * CH],
                          start=(dc == 0), stop=(dc == DC - 1)),
                      ("then", sMM))
                mm_t = sMM.inc()
            free[f"wts{buf}"] = (sMM, mm_t)
            _step(p, "vector", [(sMM, mm_t), (sDV, xcar_done)],
                  lambda v, mb=mb, c=c, oc=oc: v.tensor_tensor(
                      nap(c, oc), nap(c, oc), pb[mb][:, :], ALU.add),
                  ("drain", sDV))
            bfree[mb] = (sDV, sDV.inc())
        free["rs"] = (sMM, sMM.n)
        xin_done = sDV.n

        if c + 1 < NCH:
            p2a = emit_p2a(c + 1)

        prev_ac = sAC.n
        rstd_ac = _chunk_stats(r, p, pb, sMM, sDV, sAC, rw, lambda dc, c=c: nap(c, dc),
                               sqpc, sqst, MEAN, M2, VAR, LNV, RSTD, prev_ac, bw, bfree,
                               src_ready=(sDV, xin_done))
        _step(p, "vector", [(sAC, rstd_ac)],
              lambda v: v.tensor_tensor(rw(M2), rw(MEAN), rw(RSTD), ALU.mult),
              ("drain", sDV))
        mr_done = sDV.inc()
        _step(p, "tensor", bw(0) + bw(1) + [(sDV, mr_done)],
              lambda t: (t.matmul(pb[0][:, :], r.ones_f_row1[:], rw(RSTD),
                                  start=True, stop=True).then_inc(sMM.h, 1),
                         t.matmul(pb[1][:, :], r.ones_f_row1[:], rw(M2),
                                  start=True, stop=True))[1],
              ("then", sMM))
        sMM.inc(2)
        bc_done = sMM.n
        _step(p, "vector", [(sMM, bc_done)] + fw("yn") + fw("mixb"),
              lambda v, c=c: v.tensor_tensor(
                  ynt[:].rearrange("p (j t) -> p j t", j=DC), nap(c), pbc(pb[0]),
                  ALU.mult),
              ("drain", sDV))
        sDV.inc()
        _step(p, "vector", [],
              lambda v: v.tensor_tensor(
                  ynt[:].rearrange("p (j t) -> p j t", j=DC),
                  ynt[:].rearrange("p (j t) -> p j t", j=DC), pbc(pb[1]), ALU.subtract),
              ("drain", sDV))
        dv_t = sDV.inc()
        bfree[0] = (sDV, dv_t)
        bfree[1] = (sDV, dv_t)

        def v_gb2(v):
            yv = ynt[:].rearrange("p (j t) -> p j t", j=DC)
            v.tensor_tensor(yv, yv, bass.AP(r.g2lc, 0, [[DC, 128], [1, DC], [0, CH]]),
                            ALU.mult)
            v.drain()
            v.tensor_tensor(yv, yv, bass.AP(r.b2lc, 0, [[DC, 128], [1, DC], [0, CH]]),
                            ALU.add)
        _step(p, "vector", [], v_gb2, ("drain", sDV))
        yn_done = sDV.inc()

        def v_ym(v, c=c, batch_start=batch_start):
            yv = ynt[:].rearrange("p (j t) -> p j t", j=DC)
            mv_ = ymt[:].rearrange("p (j t) -> p j t", j=DC)
            v.tensor_tensor(mv_[:, :, 1:CH], yv[:, :, 1:CH], yv[:, :, 0:CH - 1],
                            ALU.add)
            v.drain()
            if batch_start:
                v.tensor_copy(mv_[:, :, 0:1], yv[:, :, 0:1])
            else:
                v.tensor_tensor(mv_[:, :, 0:1], yv[:, :, 0:1],
                                ycar[:].rearrange("p (j o) -> p j o", o=1), ALU.add)
            v.drain()
            v.tensor_copy(ycar[:].rearrange("p (j o) -> p j o", o=1),
                          yv[:, :, CH - 1:CH])
        _step(p, "vector", [(sDV, yn_done)] + fw("ym") + fw("A"), v_ym, ("drain", sDV))
        ym_done = sDV.inc()

        for oc in range(DC):
            buf = oc % 2
            _step(p, "sync", fw(f"wts{buf}") + [(sWS[buf], sWS[buf].n)],
                  stream_w(buf, r.wfrB, oc, None), ("then", sWS[buf]), inck=16)
            ws_t = sWS[buf].inc(64)
            mb = 2 + buf
            for dc in range(DC):
                wl = ([(sWS[buf], ws_t), (sDV, yn_done)] + bw(mb)) if dc == 0 else []
                _step(p, "tensor", wl,
                      lambda t, mb=mb, buf=buf, dc=dc: t.matmul(
                          pb[mb][:, :], wts[buf][:, dc * 128:(dc + 1) * 128],
                          ynt[:, dc * CH:(dc + 1) * CH],
                          start=(dc == 0), stop=(dc == DC - 1)),
                      ("then", sMM))
                mm_t = sMM.inc()
            free[f"wts{buf}"] = (sMM, mm_t)
            _step(p, "scalar", [(sMM, mm_t), (sDV, rwkv_done)] + (fw("rs2") if oc == 0 else []),
                  lambda s, mb=mb, oc=oc: s.activation(
                      out=rs2[:, oc * CH:(oc + 1) * CH], in_=pb[mb][:, :],
                      func=AF.Sigmoid),
                  ("drain", sAC))
            bfree[mb] = (sAC, sAC.inc())
        rr_done = sAC.n
        free["yn"] = (sMM, sMM.n)

        free["ym"] = None
        for sub in range(CH // 256):
            toff = sub * 256
            kk_ready = {}
            for fs in range(FC + 1):
                if fs < FC:
                    buf = fs % 2
                    _step(p, "sync", fw(f"wts{buf}") + [(sWS[buf], sWS[buf].n)],
                          lambda s, fs=fs, buf=buf: _slab(
                              s, sWS[buf].h, wfks[buf][:], r.wfkB[fs]),
                          ("then", sWS[buf]), inck=16)
                    ws_t = sWS[buf].inc(64)
                    mb = 2 + buf
                    for dc in range(DC):
                        wl = ([(sWS[buf], ws_t), (sDV, ym_done)] + bw(mb)) if dc == 0 else []
                        _step(p, "tensor", wl,
                              lambda t, mb=mb, buf=buf, dc=dc, toff=toff: t.matmul(
                                  pb[mb][:, 0:256], wfks[buf][:, dc * 128:(dc + 1) * 128],
                                  ymt[:, dc * CH + toff:dc * CH + toff + 256],
                                  start=(dc == 0), stop=(dc == DC - 1)),
                              ("then", sMM))
                        mm_t = sMM.inc()
                    free[f"wts{buf}"] = (sMM, mm_t)

                    def v_relu(v, mb=mb, buf=buf):
                        v.tensor_relu(evsc[buf][:, 0:256], pb[mb][:, 0:256])
                    _step(p, "vector", [(sMM, mm_t)] + fw(f"ev{buf}"), v_relu,
                          ("drain", sDV))
                    bfree[mb] = (sDV, sDV.inc())
                    _step(p, "scalar", [(sDV, sDV.n)] + fw(f"kks{buf}"),
                          lambda s, buf=buf: s.activation(
                              out=kks[buf], in_=evsc[buf][:, 0:256],
                              func=AF.Square),
                          ("drain", sAC))
                    kk_ready[fs] = sAC.inc()
                    free[f"ev{buf}"] = (sAC, kk_ready[fs])
                if fs >= 1:
                    f2 = fs - 1
                    buf = f2 % 2
                    _step(p, "sync", fw(f"wfvs{buf}") + fw("rs") + [(sWV[buf], sWV[buf].n)],
                          lambda s, f2=f2, buf=buf: s.dma_start(
                              out=wfvs[buf],
                              in_=r.wfvB[f2 * 128:(f2 + 1) * 128, :]),
                          ("then", sWV[buf]), inck=16)
                    od_t = sWV[buf].inc(16)
                    for oc in range(DC):
                        wl = [(sAC, kk_ready[f2]), (sWV[buf], od_t)] if oc == 0 else []
                        if f2 == 0 and oc == 0:
                            wl += fw("outst")
                        _step(p, "tensor", wl,
                              lambda t, f2=f2, buf=buf, oc=oc: t.matmul(
                                  pso[:, oc * 256:(oc + 1) * 256],
                                  wfvs[buf][:, oc * 128:(oc + 1) * 128], kks[buf],
                                  start=(f2 == 0), stop=(f2 == FC - 1),
                                  skip_group_check=True),
                              ("then", sMM))
                        mm_t2 = sMM.inc()
                    free[f"kks{buf}"] = (sMM, mm_t2)
                    free[f"wfvs{buf}"] = (sMM, mm_t2)
            cm_done = sMM.n

            def v_out(v, c=c, toff=toff):
                ov = outst[:].rearrange("p (j t) -> p j t", j=DC)
                xv = xap(c)[:, :, toff:toff + 256]
                nv = nap(c)[:, :, toff:toff + 256]
                rv = rs2[:].rearrange("p (j t) -> p j t", j=DC)[:, :, toff:toff + 256]
                pv = pso[:].rearrange("p (j t) -> p j t", j=DC)
                v.tensor_tensor(ov, pv, rv, ALU.mult)
                v.drain()
                v.tensor_tensor(ov, ov, xv, ALU.add)
                v.drain()
                v.tensor_tensor(ov, ov, nv, ALU.add)
            _step(p, "vector", [(sMM, cm_done), (sAC, rr_done), (sDV, xin_done)]
                  + fw("outst"), v_out, ("drain", sDV))
            out_done = sDV.inc()

            for tb in range(2):
                buf = tb % 2
                tok0 = c * CH + toff + tb * 128
                b_ = tok0 // SEQ
                trow = tok0 % SEQ
                for half in range(2):
                    _step(p, "tensor", [(sDV, out_done)] + bw(0),
                          lambda t, tb=tb, half=half: [t.matmul(
                              pb[0][:, j * 128:(j + 1) * 128],
                              outst[:, (half * 4 + j) * 256 + tb * 128:
                                    (half * 4 + j) * 256 + tb * 128 + 128],
                              r.idf[:], is_transpose=True, start=True,
                              stop=True).then_inc(sMM.h, 1) for j in range(4)][-1])
                    sMM.inc(4)
                    tr_t = sMM.n
                    _step(p, "vector", [(sMM, tr_t)] + (fw(f"tok{buf}") if half == 0 else []),
                          lambda v, buf=buf, half=half: v.tensor_copy(
                              tokst[buf][:, half * 512:(half + 1) * 512], pb[0][:, 0:512]),
                          ("drain", sDV))
                    bfree[0] = (sDV, sDV.inc())
                tok_done = sDV.n
                _step(p, "sync", [(sDV, tok_done), (sOD[buf], sOD[buf].n)],
                      lambda s, buf=buf, b_=b_, trow=trow: s.dma_start(
                          out=out[b_, trow:trow + 128, :], in_=tokst[buf][:]),
                      ("then", sOD[buf]), inck=16)
                free[f"tok{buf}"] = (sOD[buf], sOD[buf].inc(16))
            free["outst"] = (sMM, sMM.n)
            free["ek"] = (sMM, sMM.n)

        free["ym"] = (sMM, sMM.n)
        free["rs"] = (sMM, sMM.n)
        free["rs2"] = (sDV, sDV.n)
        free["mixb"] = [(sOD[0], sOD[0].n), (sOD[1], sOD[1].n)]

    def s_fin(s, f0=sOD[0].n, f1=sOD[1].n):
        s.wait_ge(sOD[0].h, f0)
        s.wait_ge(sOD[1].h, f1)
    p.on("sync", s_fin)



def emit_store_out(r, p, out):
    nc = r.nc
    ost = [r.sb_st(f"ost{i}", [128, D], F32) for i in range(2)]
    srcf = r.sb_st("srcf", [128, D], F32)
    ptf = r.st_st.enter_context(nc.psum_tensor("ptf_st", [128, 512], F32))
    s_c = r.sem("s_c")
    s_t2 = r.sem("s_t2")
    s_e2 = r.sem("s_e2")
    s_o2 = [r.sem("s_o2a"), r.sem("s_o2b")]

    ntiles = T // 128
    for i in range(ntiles):
        buf = i % 2
        b = (i * 128) // SEQ
        trow = (i * 128) % SEQ

        def v_gather(v, i=i, buf=buf, need=(16 * (i // 2)) if i >= 2 else 0):
            if i >= 2:
                v.wait_ge(s_o2[buf].h, need)
            src = bass.AP(r.x_fm, i * 128, [[DC * T, 128], [T, DC], [1, 128]])
            v.tensor_copy(srcf[:].rearrange("p (j c) -> p j c", j=DC), src)
            v.drain()
            v.sem_inc(s_c.h, 1)
        p.on("vector", v_gather)
        c_t = s_c.inc()

        for half in range(2):
            def t_tr(t, c_t=c_t, half=half, i=i):
                t.wait_ge(s_c.h, c_t)
                if i > 0 or half > 0:
                    t.wait_ge(s_e2.h, 2 * i + half)
                for j in range(4):
                    jj = half * 4 + j
                    t.matmul(ptf[:, j * 128:(j + 1) * 128],
                             srcf[:, jj * 128:(jj + 1) * 128], r.idf[:],
                             is_transpose=True, start=True, stop=True).then_inc(s_t2.h, 1)
            p.on("tensor", t_tr)
            t2_t = s_t2.inc(4)

            def v_ev(v, t2_t=t2_t, half=half, buf=buf):
                v.wait_ge(s_t2.h, t2_t)
                v.tensor_copy(ost[buf][:, half * 512:(half + 1) * 512], ptf[:, 0:512])
                v.drain()
                v.sem_inc(s_e2.h, 1)
            p.on("vector", v_ev)
            s_e2.inc()
        e2_t = s_e2.n

        def s_out(s, buf=buf, b=b, trow=trow, e2_t=e2_t):
            s.wait_ge(s_e2.h, e2_t)
            s.dma_start(out=out[b, trow:trow + 128, :], in_=ost[buf][:]).then_inc(s_o2[buf].h, 16)
        p.on("sync", s_out)
        s_o2[buf].inc(16)

    fin0, fin1 = s_o2[0].n, s_o2[1].n

    def s_fin(s, fin0=fin0, fin1=fin1):
        s.wait_ge(s_o2[0].h, fin0)
        s.wait_ge(s_o2[1].h, fin1)
    p.on("sync", s_fin)


def kernel(**inputs):
    xs = np.ascontiguousarray(inputs["L0_attn_x"])
    nc = build_nc()
    in_maps = []
    for c in range(8):
        m = {"x": xs[2 * c:2 * c + 2]}
        for k, v in inputs.items():
            if k == "L0_attn_x":
                continue
            a = np.ascontiguousarray(np.asarray(v))
            if k == "L1_ffn_act_scale":
                a = a.reshape(1)
            m[k] = a
        in_maps.append(m)
    res = run_bass_kernel_spmd(nc, in_maps, core_ids=list(range(8)))
    return np.concatenate([r_["out"] for r_ in res.results], axis=0)
